# version 07
# baseline (speedup 1.0000x reference)
import numpy as np
import ml_dtypes
import concourse.bass as bass
import concourse.mybir as mybir
from concourse.bass_utils import run_bass_kernel_spmd
from concourse.bass_core_rust import IndirectOffsetOnAxis
from contextlib import ExitStack

F32 = mybir.dt.float32
BF16 = mybir.dt.bfloat16
I32 = mybir.dt.int32
U32 = mybir.dt.uint32
U8 = mybir.dt.uint8
AF = mybir.ActivationFunctionType
ALU = mybir.AluOpType
AX = mybir.AxisListType
BFNP = ml_dtypes.bfloat16

D = 1024
G = 8
NE = 8
NH, HD, DS = 32, 32, 64
CS = 128
EPS_LN = 1e-5
EPS_TERN = 1e-5

ENGS = ("sy", "te", "ve", "sc", "gp")


class Rec:
    def __init__(self):
        self.q = {e: [] for e in ENGS}

    def w(self, eng, tok):
        if tok is not None:
            sem, val = tok
            if val > 0:
                self.q[eng].append(("wait_ge", (sem, val), {}, None))

    def i(self, eng, method, *args, inc=None, **kw):
        self.q[eng].append((method, args, kw, inc))

    def replay(self, name, proxy):
        for method, args, kw, inc in self.q[name]:
            if method == "__act_table__":
                r = proxy.add_instruction(mybir.InstLoadActFuncSet(
                    name=proxy.bass.get_next_instruction_name(),
                    act_func_set_id=args[0], ins=[], outs=[]))
            else:
                r = getattr(proxy, method)(*args, **kw)
            if inc is not None:
                r.then_inc(inc[0], inc[1])


class Ctr:
    def __init__(self, sm, step=1):
        self.sem, self.v, self.step = sm, 0, step

    def nxt(self):
        self.v += self.step
        return (self.sem, self.v)

    def tok(self):
        return (self.sem, self.v)


def build_program(T, TtA=256, TtB=512, TtC=512, phases="ABC", a_stop=10**9, b_stop=10**9, r_stop=10**9, rcheck=False):
    nc = bass.Bass("TRN2", target_bir_lowering=False, detect_race_conditions=rcheck)
    es = ExitStack()
    nA, nB, nC = T // TtA, T // TtB, T // TtC
    KC = TtC // CS

    def din(name, shape, dtype=F32):
        return nc.dram_tensor(name, list(shape), dtype, kind="ExternalInput").ap()

    xT = din("xT", [D, T])
    wkT = din("wkT", [D, D]); wvT = din("wvT", [D, D])
    wrT = din("wrT", [D, D]); woT = din("woT", [D, D])
    tmk = din("tmk", [128, G]); tmv = din("tmv", [128, G]); tmr = din("tmr", [128, G])
    tdp = din("tdp", [128, G]); tfp = din("tfp", [128, G])
    wr_rtr = din("wr_rtr", [D, NE]); br_rtr = din("br_rtr", [1, NE])
    w1 = din("w1", [NE, D, D]); b1 = din("b1", [NE, D])
    w2 = din("w2", [NE, D, D]); b2 = din("b2", [NE, D])
    winT = din("winT", [D, 1184])
    dtb = din("dtb", [NH, 1]); alog = din("alog", [NH, 1])
    dexp = din("dexp", [128, G])
    woutT = din("woutT", [D, D])
    wfT = din("wfT", [D, D])
    outT = nc.dram_tensor("outT", [D, T], F32, kind="ExternalOutput").ap()

    xnbf_d = nc.dram_tensor("xnbf_d", [D, T], BF16, kind="Internal").ap()
    xa_d = nc.dram_tensor("xa_d", [D, T], F32, kind="Internal").ap()
    x1_d = nc.dram_tensor("x1_d", [D, T], F32, kind="Internal").ap()
    w1g_d = nc.dram_tensor("w1g_d", [2, D, D], BF16, kind="Internal").ap()
    w2g_d = nc.dram_tensor("w2g_d", [2, D, D], BF16, kind="Internal").ap()
    bg_d = nc.dram_tensor("bg_d", [4, D], F32, kind="Internal").ap()
    idx_d = nc.dram_tensor("idx_d", [1, 2], U32, kind="Internal").ap()

    ident_bf_c = nc.inline_tensor(np.eye(128, dtype=BFNP), "ident_bf_c")
    ident_f_c = nc.inline_tensor(np.eye(128, dtype=np.float32), "ident_f_c")
    ones_row_f_c = nc.inline_tensor(np.ones((1, 128), np.float32), "ones_row_f_c")
    ones_col_bf_c = nc.inline_tensor(np.ones((128, 1), BFNP), "ones_col_bf_c")
    repl_np = np.zeros((NH, G * 128), np.float32)
    for g in range(G):
        for m in range(128):
            repl_np[4 * g + m // 32, g * 128 + m] = 1.0
    repl_bf_c = nc.inline_tensor(repl_np.astype(BFNP), "repl_bf_c")
    tri_c = nc.inline_tensor(
        (np.arange(CS)[:, None] <= np.arange(CS)[None, :]).astype(np.float32), "tri_c")
    iota2d_c = nc.inline_tensor(
        (np.arange(128)[:, None] + 128 * np.arange(G)[None, :]).astype(np.int32),
        "iota2d_c")
    iota8_c = nc.inline_tensor(
        np.tile(np.arange(8, dtype=np.float32), (128, 1)), "iota8_c")
    selh_np = np.zeros((NH, NH * 128), np.float32)
    for hh_ in range(NH):
        selh_np[hh_, hh_ * 128:(hh_ + 1) * 128] = 1.0
    selh_c = nc.inline_tensor(selh_np, "selh_c")

    nbytes = nc.sbuf_bytes_remaining - 2048
    nbytes -= nbytes % 256
    arena = nc.alloc_sbuf_tensor("arena", [128, nbytes], U8)
    AFV = {F32: arena.bitcast(F32), BF16: arena.bitcast(BF16),
           I32: arena.bitcast(I32), U32: arena.bitcast(U32)}
    DSZ = {F32: 4, BF16: 2, I32: 4, U32: 4}

    class Alloc:
        def __init__(self, base):
            self.cur = base

        def mk(self, dt_, *shape):
            n = int(np.prod(shape))
            sz = DSZ[dt_]
            off = (self.cur + 63) // 64 * 64
            self.cur = off + n * sz
            assert self.cur <= nbytes, f"sbuf arena overflow: {self.cur} > {nbytes}"
            v = AFV[dt_][:, off // sz: off // sz + n]
            if len(shape) == 3:
                return v.rearrange("p (a b c) -> p a b c", a=shape[0], b=shape[1])
            if len(shape) == 2:
                return v.rearrange("p (a b) -> p a b", a=shape[0])
            return v

        def f32(self, *s):
            return self.mk(F32, *s)

        def bf(self, *s):
            return self.mk(BF16, *s)

        def i32(self, *s):
            return self.mk(I32, *s)

        def view(self, dt_, off, *shape):
            n = int(np.prod(shape))
            sz = DSZ[dt_]
            v = AFV[dt_][:, off // sz: off // sz + n]
            if len(shape) == 3:
                return v.rearrange("p (a b c) -> p a b c", a=shape[0], b=shape[1])
            if len(shape) == 2:
                return v.rearrange("p (a b) -> p a b", a=shape[0])
            return v

    P = Alloc(0)
    c_tmx = {"k": P.f32(G), "v": P.f32(G), "r": P.f32(G)}
    c_ew = P.f32(G); c_eu = P.f32(G); c_dexp = P.f32(G)
    c_A = P.f32(1); c_dtb = P.f32(1)
    c_idbf = P.bf(128); c_idf = P.f32(128)
    c_onesr = P.f32(128)
    c_onescbf = P.bf(1)
    c_onescf = P.f32(1)
    c_repl = P.bf(G * 128)
    c_tri = P.f32(CS)
    c_iota2 = P.i32(G)
    c_wr = P.bf(G, NE)
    c_br = P.f32(NE)
    c_sco = P.f32(64)
    c_idxu = P.mk(U32, 8)
    c_eb32 = P.i32(2)
    c_gidx = P.i32(2, G)
    c_b1g = P.f32(2, G); c_b2g = P.f32(2, G); c_bb = P.f32(G)
    c_scl = P.f32(2)
    c_prev = P.f32(G, DS)
    c_tmpg = P.f32(G)
    c_fc = P.f32(2)
    c_msel = P.f32(2)
    c_ebf = P.f32(2)
    c_iotf = P.f32(1)
    persist_end = P.cur

    ps = nc.alloc_psum_tensor("ps", [128, 4096], F32)
    psa = ps.ap()
    PB = [psa[:, 512 * b: 512 * (b + 1)] for b in range(8)]
    ps_bf = ps.bitcast(BF16)
    PBbf = [ps_bf[:, 1024 * b: 1024 * (b + 1)] for b in range(8)]
    PBPR2 = [psa[:, 1024 * p: 1024 * (p + 1)].rearrange(
        "p (a c) -> p a c", a=2) for p in range(2)]

    def mksem(name):
        return es.enter_context(nc.semaphore(name))

    q_in = Ctr(mksem("q_in"), 16)
    q_w = Ctr(mksem("q_w"), 16)
    qx = [Ctr(mksem("qx0"), 16), Ctr(mksem("qx1"), 16)]
    q_sxn = Ctr(mksem("q_sxn"), 16)
    q_sxa = Ctr(mksem("q_sxa"), 16)
    q_sx1 = Ctr(mksem("q_sx1"), 16)
    q_so = Ctr(mksem("q_so"), 16)
    q_g = Ctr(mksem("q_g"), 16)
    c_t = Ctr(mksem("c_t"))
    c_v = Ctr(mksem("c_v"))
    c_s = Ctr(mksem("c_s"))
    q_st = q_sxn

    R = Rec()

    def ld(qeng, ctr, pairs, waits=(), **kw):
        for wt in waits:
            R.w(qeng, wt)
        for out, in_ in pairs:
            ctr.nxt()
            R.i(qeng, "dma_start", out=out, in_=in_, inc=(ctr.sem, 16), **kw)
        return ctr.tok()

    CTRS = {"ve": c_v, "sc": c_s, "te": c_t}

    def one(eng, ctr, method, *args, waits=(), **kw):
        if eng in ("ve", "sc") and ctr.v > 0:
            R.w(eng, ctr.tok())
        for wt in waits:
            R.w(eng, wt)
        ctr.nxt()
        R.i(eng, method, *args, inc=(ctr.sem, 1), **kw)
        return ctr.tok()

    def quiet(eng, method, *args, waits=(), **kw):
        if eng in ("ve", "sc"):
            return one(eng, CTRS[eng], method, *args, waits=waits, **kw)
        for wt in waits:
            R.w(eng, wt)
        R.i(eng, method, *args, **kw)

    def seal(eng, ctr):
        if eng in ("ve", "sc"):
            return ctr.tok()
        m, a, k, inc = R.q[eng][-1]
        assert inc is None and m != "wait_ge", (m, inc)
        ctr.nxt()
        R.q[eng][-1] = (m, a, k, (ctr.sem, 1))
        return ctr.tok()

    def fmaj(ap_, t0, tt):
        return ap_.rearrange("(g p) t -> p g t", p=128)[:, :, t0:t0 + tt]

    from concourse.hw_specs import get_activation_tables
    _act_sets = list(get_activation_tables(nc.m.arch).values())

    def load_act_set(funcs):
        for idx, fs in enumerate(_act_sets):
            if all(f in fs for f in funcs):
                if c_s.v > 0:
                    R.w("sc", c_s.tok())
                c_s.nxt()
                R.i("sc", "__act_table__", idx, inc=(c_s.sem, 1))
                return c_s.tok()
        raise ValueError(f"no act table set contains {funcs}")

    bank_r = {}

    def mm_group(b, mm_list, waits=()):
        w = list(waits)
        if b in bank_r:
            w.append(bank_r.pop(b))
        first = True
        for (out, lhsT, rhs, st, sp, tp) in mm_list:
            kw = {"start": st, "stop": sp}
            if tp is not None:
                kw["tile_position"] = tp
            quiet("te", "matmul", out, lhsT, rhs, waits=(w if first else ()), **kw)
            first = False
        return seal("te", c_t)

    def read_bank(b, tok):
        bank_r[b] = tok

    t_pro = ld("sy", q_in, [
        (c_tmx["k"], tmk), (c_tmx["v"], tmv), (c_tmx["r"], tmr),
        (c_dexp, dexp),
        (c_idbf, ident_bf_c.ap()), (c_idf, ident_f_c.ap()),
        (c_onesr[0:1, :], ones_row_f_c.ap()),
        (c_onescbf, ones_col_bf_c.ap()),
        (c_repl[0:NH, :], repl_bf_c.ap()),
        (c_tri, tri_c.ap()),
        (c_iota2, iota2d_c.ap()),
        (c_br[0:1, :], br_rtr),
        (c_sco[:, 36:44], iota8_c.ap()),
        (c_A[0:NH, :], alog),
        (c_dtb[0:NH, :], dtb),
        (c_tmpg, tdp),
    ])
    load_act_set([AF.Exp, AF.Ln, AF.Square, AF.Identity])
    one("sc", c_s, "activation", c_ew, c_tmpg, AF.Exp, waits=[t_pro])
    one("sc", c_s, "activation", c_ew, c_ew, AF.Exp, scale=-1.0)
    t_sA = one("sc", c_s, "activation", c_A[0:NH, :], c_A[0:NH, :], AF.Exp)
    t_tf = ld("sy", q_in, [(c_tmpg, tfp)], waits=[t_sA])
    t_pro_sc = one("sc", c_s, "activation", c_eu, c_tmpg, AF.Exp, waits=[t_tf])
    one("ve", c_v, "tensor_scalar_mul", c_A[0:NH, :], c_A[0:NH, :], -1.0,
        waits=[t_sA])
    one("ve", c_v, "memset", c_onescf, 1.0, waits=[t_pro])
    t_pro_ve = one("ve", c_v, "memset", c_fc[:, 0:1], EPS_LN)

    if phases == "pro":
        R.w("sy", q_in.tok())
        R.w("sc", (c_s.sem, c_s.v))
        R.w("ve", (c_v.sem, c_v.v))
        return _finish(nc, R, es)

    A = Alloc(persist_end)
    a_x = [A.f32(G, TtA), A.f32(G, TtA)]
    a_xnb = A.f32(G, TtA + 1)
    a_al = A.f32(G, TtA + 1)
    a_be = A.f32(G, TtA + 1)
    off_dl = (A.cur + 63) // 64 * 64
    a_dl = A.f32(G, TtA)
    a_tmp = A.f32(G, TtA)
    a_ek = A.f32(G, TtA)
    a_ekv = A.f32(G, TtA)
    a_er = A.f32(G, TtA)
    a_xsq = A.view(BF16, off_dl, G, TtA)
    a_xnbf = A.bf(G, TtA)
    a_xbf = a_xnbf
    a_xm = [{q: A.bf(G, TtA) for q in ("k", "v", "r")} for _ in range(2)]
    a_rwkv = A.bf(G, TtA)
    a_xa = A.f32(G, TtA)
    a_rows = A.f32(4 * TtA)
    a_wk = A.bf(G, D); a_wv = A.bf(G, D); a_wrr = A.bf(G, D); a_wo = A.bf(G, D)
    S = Alloc(nbytes - 4 * D - 192)
    a_stage = S.f32(D)
    assert A.cur <= nbytes - 4 * D - 192, (A.cur, nbytes - 4 * D - 192)

    t_wA = ld("gp", q_w, [
        (c_wr, wr_rtr.rearrange("(g p) e -> p g e", p=128)),
        (a_wk, wkT.rearrange("(g p) o -> p g o", p=128)),
        (a_wv, wvT.rearrange("(g p) o -> p g o", p=128)),
        (a_wrr, wrT.rearrange("(g p) o -> p g o", p=128)),
        (a_wo, woT.rearrange("(g p) o -> p g o", p=128)),
    ])

    one("ve", c_v, "memset", a_xnb[:, :, 0:1], 0.0)
    one("ve", c_v, "memset", a_al[:, :, 0:1], 0.0)
    one("ve", c_v, "memset", a_be[:, :, 0:1], 0.0)
    one("ve", c_v, "memset", c_prev, 0.0)

    tokA_cons = {}
    tokA_st = {}
    tokA_xst = {}
    tokA_stat = {}
    tokA_kvr = {}
    tokA_mixv = {}
    tokA_rwkv = {}
    evq = {}
    s1tok = {}
    t_gather_done = None
    mixes = (("k", a_wk), ("v", a_wv), ("r", a_wrr))

    def emit_A1(i):
        par = i % 2
        t0 = i * TtA
        xin = a_x[par]

        lw = [t_pro]
        if i - 2 in tokA_cons:
            lw.append(tokA_cons[i - 2])
        t_ld = ld("sy", qx[par], [(xin, fmaj(xT, t0, TtA))], waits=lw)

        cw = [t_ld, t_pro_sc]
        if i - 1 in tokA_stat:
            cw.append(tokA_stat[i - 1])
        if i - 1 in tokA_mixv:
            cw.append(tokA_mixv[i - 1])
        if i - 1 in tokA_xst:
            cw.append(tokA_xst[i - 1])
        one("sc", c_s, "activation", a_xbf, xin, AF.Identity, waits=cw)
        t_c2 = one("sc", c_s, "activation", a_xsq, xin, AF.Square)

        t_stat = mm_group(6,
                          [(PB[6][0:1, 0:TtA], c_onescbf, a_xbf[:, g, :],
                            g == 0, g == G - 1, None) for g in range(G)] +
                          [(PB[6][0:1, TtA:2 * TtA], c_onescbf, a_xsq[:, g, :],
                            g == 0, g == G - 1, None) for g in range(G)],
                          waits=[t_c2])
        tokA_stat[i] = t_stat

        t_rows = one("ve", c_v, "tensor_scalar", out=a_rows[0:1, 0:2 * TtA],
                     in0=PB[6][0:1, 0:2 * TtA], scalar1=1.0 / D, scalar2=None,
                     op0=ALU.mult, waits=[t_stat, t_pro_ve])
        read_bank(6, t_rows)
        t_m2 = one("sc", c_s, "activation", a_rows[0:1, 2 * TtA:3 * TtA],
                   a_rows[0:1, 0:TtA], AF.Square, waits=[t_rows])
        t_var = one("ve", c_v, "tensor_tensor", out=a_rows[0:1, 3 * TtA:4 * TtA],
                    in0=a_rows[0:1, TtA:2 * TtA], in1=a_rows[0:1, 2 * TtA:3 * TtA],
                    op=ALU.subtract, waits=[t_m2])
        one("sc", c_s, "activation", a_rows[0:1, 3 * TtA:4 * TtA],
            a_rows[0:1, 3 * TtA:4 * TtA], AF.Ln, bias=c_fc[0:1, 0:1], waits=[t_var])
        t_rstd = one("sc", c_s, "activation", a_rows[0:1, 3 * TtA:4 * TtA],
                     a_rows[0:1, 3 * TtA:4 * TtA], AF.Exp, scale=-0.5)

        t_bc = mm_group(5, [
            (PB[5][:, 0:TtA], c_onesr[0:1, :], a_rows[0:1, 0:TtA], True, True, None),
            (PB[5][:, TtA:2 * TtA], c_onesr[0:1, :], a_rows[0:1, 3 * TtA:4 * TtA],
             True, True, None),
        ], waits=[t_rstd])

        if i > 0:
            quiet("ve", "tensor_copy", a_xnb[:, :, 0:1], a_xnb[:, :, TtA:TtA + 1])
        one("ve", c_v, "tensor_tensor", out=a_xnb[:, :, 1:TtA + 1], in0=xin,
            in1=PB[5][:, None, 0:TtA].broadcast_to([128, G, TtA]),
            op=ALU.subtract, waits=[t_bc])
        t_xn = one("ve", c_v, "tensor_tensor", out=a_xnb[:, :, 1:TtA + 1],
                   in0=a_xnb[:, :, 1:TtA + 1],
                   in1=PB[5][:, None, TtA:2 * TtA].broadcast_to([128, G, TtA]),
                   op=ALU.mult)
        read_bank(5, t_xn)
        xnbw = [t_xn]
        if i - 1 in tokA_xst:
            xnbw.append(tokA_xst[i - 1])
        t_xnbf = one("sc", c_s, "activation", a_xnbf, a_xnb[:, :, 1:TtA + 1],
                     AF.Identity, waits=xnbw)
        tokA_xst[i] = ld("sy", q_sxn, [(fmaj(xnbf_d, t0, TtA), a_xnbf)],
                         waits=[t_xnbf])

        t_dl = one("ve", c_v, "tensor_tensor", out=a_dl, in0=a_xnb[:, :, 1:TtA + 1],
                   in1=a_xnb[:, :, 0:TtA], op=ALU.subtract)
        tok_mix = {}
        for qi, (q, _) in enumerate(mixes):
            quiet("ve", "tensor_tensor", out=a_tmp, in0=a_dl,
                  in1=c_tmx[q][:, :, None].broadcast_to([128, G, TtA]),
                  op=ALU.mult)
            vw = []
            if qi == 0 and i - 1 in tokA_kvr:
                vw.append(tokA_kvr[i - 1])
            tok_mix[q] = one("ve", c_v, "tensor_tensor", out=a_xm[par][q],
                             in0=a_tmp, in1=a_xnb[:, :, 0:TtA], op=ALU.add,
                             waits=vw)
        tokA_mixv[i] = tok_mix["r"]
        s1tok[i] = (xin, tok_mix, t_xnbf)

    def emit_A2(i):
        par = i % 2
        t0 = i * TtA
        xin, tok_mix, _ = s1tok.pop(i)
        xm = a_xm[par]

        first_evac = True
        for qi, (q, wsb) in enumerate(mixes):
            for j in range(G // 2):
                bb = 2 * (j % 2)
                for h in range(2):
                    m = 2 * j + h
                    mw = [t_wA, tok_mix[q]] if (j == 0 and h == 0) else []
                    t_mm = mm_group(bb + h,
                                    [(PB[bb + h][:, 0:TtA],
                                      wsb[:, g, m * 128:(m + 1) * 128],
                                      xm[q][:, g, :], g == 0, g == G - 1, None)
                                     for g in range(G)], waits=mw)
                pview = PBPR2[bb // 2][:, :, 0:TtA]
                ew_ = [t_mm]
                if first_evac and i - 1 in tokA_rwkv:
                    ew_.append(tokA_rwkv[i - 1])
                first_evac = False
                if q == "k":
                    tk = one("sc", c_s, "activation", a_ek[:, 2 * j:2 * j + 2, :],
                             pview, AF.Exp, waits=ew_)
                elif q == "v":
                    tk = one("ve", c_v, "tensor_tensor",
                             out=a_ekv[:, 2 * j:2 * j + 2, :],
                             in0=a_ek[:, 2 * j:2 * j + 2, :], in1=pview,
                             op=ALU.mult, waits=ew_ + [evq[("k", j)]])
                else:
                    tk = one("sc", c_s, "activation", a_er[:, 2 * j:2 * j + 2, :],
                             pview, AF.Exp, scale=-1.0, waits=ew_)
                evq[(q, j)] = tk
                read_bank(bb, tk)
                read_bank(bb + 1, tk)
        tokA_kvr[i] = (c_t.sem, c_t.v)
        t_ek_all = (c_s.sem, c_s.v)

        if i > 0:
            quiet("ve", "tensor_copy", a_al[:, :, 0:1], a_al[:, :, TtA:TtA + 1])
            quiet("ve", "tensor_copy", a_be[:, :, 0:1], a_be[:, :, TtA:TtA + 1])
        for g in range(G):
            quiet("ve", "tensor_tensor_scan", a_al[:, g, 1:TtA + 1],
                  c_ew[:, g:g + 1].broadcast_to([128, TtA]), a_ekv[:, g, :],
                  a_al[:, g, 0:1], ALU.mult, ALU.add,
                  waits=([t_ek_all] if g == 0 else ()))
        for g in range(G):
            quiet("ve", "tensor_tensor_scan", a_be[:, g, 1:TtA + 1],
                  c_ew[:, g:g + 1].broadcast_to([128, TtA]), a_ek[:, g, :],
                  a_be[:, g, 0:1], ALU.mult, ALU.add)
        t_scan = seal("ve", c_v)

        one("ve", c_v, "tensor_tensor", out=a_ekv, in0=a_ekv,
            in1=c_eu[:, :, None].broadcast_to([128, G, TtA]), op=ALU.mult)
        one("ve", c_v, "tensor_tensor", out=a_ekv, in0=a_ekv,
            in1=a_al[:, :, 0:TtA], op=ALU.add)
        one("ve", c_v, "tensor_tensor", out=a_ek, in0=a_ek,
            in1=c_eu[:, :, None].broadcast_to([128, G, TtA]), op=ALU.mult)
        one("ve", c_v, "tensor_tensor", out=a_ek, in0=a_ek,
            in1=a_be[:, :, 0:TtA], op=ALU.add)
        one("ve", c_v, "tensor_scalar", out=a_er, in0=a_er, scalar1=1.0,
            scalar2=None, op0=ALU.add)
        one("ve", c_v, "tensor_tensor", out=a_ek, in0=a_ek, in1=a_er, op=ALU.mult)
        one("ve", c_v, "reciprocal", a_ek, a_ek)
        t_rwkv = one("ve", c_v, "tensor_tensor", out=a_rwkv, in0=a_ekv, in1=a_ek,
                     op=ALU.mult)
        tokA_rwkv[i] = t_rwkv

        for j in range(G // 2):
            bb = 2 * (j % 2)
            for h in range(2):
                m = 2 * j + h
                mw = [t_rwkv] if (j == 0 and h == 0) else []
                t_mm = mm_group(bb + h,
                                [(PB[bb + h][:, 0:TtA],
                                  a_wo[:, g, m * 128:(m + 1) * 128],
                                  a_rwkv[:, g, :], g == 0, g == G - 1, None)
                                 for g in range(G)], waits=mw)
            xw2 = [t_mm]
            if j == 0 and i - 1 in tokA_st:
                xw2.append(tokA_st[i - 1])
            tk = one("ve", c_v, "tensor_tensor", out=a_xa[:, 2 * j:2 * j + 2, :],
                     in0=xin[:, 2 * j:2 * j + 2, :], in1=PBPR2[bb // 2][:, :, 0:TtA],
                     op=ALU.add, waits=xw2)
            read_bank(bb, tk)
            read_bank(bb + 1, tk)
        tokA_cons[i] = (c_v.sem, c_v.v)
        tokA_st[i] = ld("sy", q_sxa, [(fmaj(xa_d, t0, TtA), a_xa)],
                        waits=[tokA_cons[i]])

    def emit_router():
        nonlocal t_gather_done
        _, _, t_xnbf = s1tok[0]
        if True:
            t_rt = mm_group(7, [(PB[7][0:1, 0:NE], a_xnbf[:, g, 0:1], c_wr[:, g, :],
                                 g == 0, g == G - 1, None) for g in range(G)],
                            waits=[t_xnbf, t_wA])
            tokA_stat[0] = t_rt
            t_rt = one("ve", c_v, "tensor_tensor", out=c_sco[0:1, 0:NE],
                       in0=PB[7][0:1, 0:NE], in1=c_br[0:1, :], op=ALU.add,
                       waits=[t_rt])
            read_bank(7, t_rt)
            t_sb = mm_group(6, [(PB[6][:, 0:8], c_onesr[0:1, :],
                                 c_sco[0:1, 0:8], True, True, None)],
                            waits=[t_rt])
            one("ve", c_v, "tensor_copy", c_sco[:, 0:8], PB[6][:, 0:8],
                waits=[t_sb])
            read_bank(6, (c_v.sem, c_v.v))
            def sel_level(a_sl, b_sl, out_sl, w):
                quiet("ve", "tensor_tensor", out=c_sco[:, 44:44 + w], in0=a_sl,
                      in1=b_sl, op=ALU.subtract)
                quiet("ve", "tensor_scalar", out=c_sco[:, 48:48 + w],
                      in0=c_sco[:, 44:44 + w], scalar1=0.0, scalar2=None,
                      op0=ALU.is_ge)
                quiet("ve", "tensor_tensor", out=c_sco[:, 52:52 + w], in0=a_sl,
                      in1=c_sco[:, 48:48 + w], op=ALU.mult)
                quiet("ve", "tensor_scalar", out=c_sco[:, 48:48 + w],
                      in0=c_sco[:, 48:48 + w], scalar1=-1.0, scalar2=1.0,
                      op0=ALU.mult, op1=ALU.add)
                quiet("ve", "tensor_tensor", out=c_sco[:, 56:56 + w], in0=b_sl,
                      in1=c_sco[:, 48:48 + w], op=ALU.mult)
                quiet("ve", "tensor_tensor", out=out_sl,
                      in0=c_sco[:, 52:52 + w], in1=c_sco[:, 56:56 + w],
                      op=ALU.add)

            def exact_max8(src0, dst_col):
                sel_level(c_sco[:, src0:src0 + 4], c_sco[:, src0 + 4:src0 + 8],
                          c_sco[:, 60:64], 4)
                sel_level(c_sco[:, 60:62], c_sco[:, 62:64], c_sco[:, 60:62], 2)
                sel_level(c_sco[:, 60:61], c_sco[:, 61:62],
                          c_sco[:, dst_col:dst_col + 1], 1)

            def tree_sum8(src0, dst_col):
                quiet("ve", "tensor_tensor", out=c_sco[:, 60:64],
                      in0=c_sco[:, src0:src0 + 4],
                      in1=c_sco[:, src0 + 4:src0 + 8], op=ALU.add)
                quiet("ve", "tensor_tensor", out=c_sco[:, 44:46],
                      in0=c_sco[:, 60:62], in1=c_sco[:, 62:64], op=ALU.add)
                quiet("ve", "tensor_tensor",
                      out=c_sco[:, dst_col:dst_col + 1],
                      in0=c_sco[:, 44:45], in1=c_sco[:, 45:46], op=ALU.add)

            exact_max8(0, 32)
            one("ve", c_v, "tensor_scalar", out=c_sco[:, 8:16],
                in0=c_sco[:, 0:8], scalar1=c_sco[:, 32:33], scalar2=None,
                op0=ALU.is_ge)
            one("ve", c_v, "tensor_tensor", out=c_sco[:, 16:24],
                in0=c_sco[:, 8:16], in1=c_sco[:, 36:44], op=ALU.mult)
            tree_sum8(16, 34)
            one("ve", c_v, "scalar_tensor_tensor", out=c_sco[:, 16:24],
                in0=c_sco[:, 8:16], scalar=-1e30, in1=c_sco[:, 0:8],
                op0=ALU.mult, op1=ALU.add)
            exact_max8(16, 33)
            one("ve", c_v, "tensor_scalar", out=c_sco[:, 24:32],
                in0=c_sco[:, 16:24], scalar1=c_sco[:, 33:34], scalar2=None,
                op0=ALU.is_ge)
            one("ve", c_v, "tensor_tensor", out=c_sco[:, 16:24],
                in0=c_sco[:, 24:32], in1=c_sco[:, 36:44], op=ALU.mult)
            tree_sum8(16, 35)
            t_e1 = one("ve", c_v, "tensor_copy", c_ebf[:, 0:1], c_sco[:, 34:35])
            if r_stop <= 1:
                t_gather_done = q_g.tok()
                return
            t_mc = mm_group(7, [
                (PB[7][0:NE, 0:1], c_sco[0:1, 8:16], c_onescf[0:1, 0:1],
                 True, True, None),
                (PB[7][0:NE, 1:2], c_sco[0:1, 24:32], c_onescf[0:1, 0:1],
                 True, True, None)], waits=[t_e1])
            t_msl = one("ve", c_v, "tensor_copy", c_msel[0:NE, 0:2],
                        PB[7][0:NE, 0:2], waits=[t_mc])
            read_bank(7, t_msl)
            one("ve", c_v, "tensor_copy", c_ebf[:, 0:1], c_sco[:, 34:35],
                waits=[t_e1])
            one("ve", c_v, "tensor_copy", c_ebf[:, 1:2], c_sco[:, 35:36])
            t_rb = one("ve", c_v, "tensor_copy", c_eb32[:, 0:2], c_ebf[:, 0:2])
            t_gi = t_rb
            for s in range(2):
                t_gi = one("ve", c_v, "tensor_scalar", out=c_gidx[:, s, :],
                           in0=c_eb32[:, s:s + 1].broadcast_to([128, G]),
                           scalar1=1024, scalar2=None, op0=ALU.mult,
                           waits=([t_rb] if s == 0 else ()))
                t_gi = one("ve", c_v, "tensor_tensor", out=c_gidx[:, s, :],
                           in0=c_gidx[:, s, :], in1=c_iota2, op=ALU.add)
            if r_stop <= 2:
                t_gather_done = q_g.tok()
                return
            t_bsel_last = None
            for bsrc, dstb in ((b1, c_b1g), (b2, c_b2g)):
                blw = [t_rt] if t_bsel_last is None else [t_bsel_last]
                t_bld = ld("sy", q_in, [(a_stage[0:NE, :], bsrc)], waits=blw)
                for s in range(2):
                    t_bm2 = mm_group(7, [
                        (PB[7][:, g:g + 1], a_stage[0:NE, g * 128:(g + 1) * 128],
                         c_msel[0:NE, s:s + 1], True, True, None)
                        for g in range(G)], waits=[t_msl, t_bld])
                    t_bv = one("ve", c_v, "tensor_copy", dstb[:, s, :],
                               PB[7][:, 0:G], waits=[t_bm2])
                    read_bank(7, t_bv)
                    t_bsel_last = t_bm2
            if r_stop <= 3:
                ld("gp", q_g, [(outT.rearrange("a b -> a b")[0:128, 0:8], c_gidx[:, 0, :])],
                   waits=[t_gi])
                ld("gp", q_g, [(outT[0:128, 8:16], c_gidx[:, 1, :])])
                ld("gp", q_g, [(outT[0:128, 16:18], c_ebf[:, 0:2])])
                ld("gp", q_g, [(outT[0:8, 20:22], c_msel[0:NE, 0:2])], waits=[t_msl])
                t_gather_done = q_g.tok()
                return
            for s in range(2):
                for (wsrc, wdst) in ((w1, w1g_d), (w2, w2g_d)):
                    for cch in range(G):
                        sb = a_stage
                        R.w("gp", t_gi)
                        R.w("gp", t_bsel_last)
                        R.w("gp", q_g.tok())
                        q_g.nxt()
                        R.i("gp", "indirect_dma_start", out=sb, out_offset=None,
                            in_=wsrc.rearrange("e d f -> (e d) f"),
                            in_offset=IndirectOffsetOnAxis(c_gidx[:, s, cch:cch + 1], 0),
                            inc=(q_g.sem, 16))
                        R.w("gp", q_g.tok())
                        q_g.nxt()
                        R.i("gp", "dma_start",
                            out=wdst[s, cch * 128:(cch + 1) * 128, :], in_=sb,
                            inc=(q_g.sem, 16))
            t_gather_done = q_g.tok()

    emit_A1(0)
    if phases != "An":
        emit_router()
    for i in range(nA):
        if i + 1 < nA:
            emit_A1(i + 1)
        emit_A2(i)

    tA = [(c_v.sem, c_v.v), (c_s.sem, c_s.v), (c_t.sem, c_t.v), q_sxn.tok(),
          q_sxa.tok(), t_gather_done]
    if phases in ("A", "An"):
        if t_gather_done is None:
            tA[5] = tA[3]
        for wt in tA:
            R.w("gp", wt)
        import os as _os
        if _os.environ.get('DUMPXN'):
            ld("gp", q_g, [(outT, xnbf_d)])
            for jj in range(4):
                ld("gp", q_g, [(outT[1020 + jj:1021 + jj, 0:TtA],
                                a_rows[0:1, jj * TtA:(jj + 1) * TtA])])
        else:
            ld("gp", q_g, [(outT, xa_d)])
        R.w("gp", q_g.tok())
        return _finish(nc, R, es)

    B = Alloc(persist_end)
    b_xn = [B.bf(G, TtB), B.bf(G, TtB)]
    b_xa = [B.f32(G, TtB), B.f32(G, TtB)]
    b_h = B.bf(G, TtB)
    b_acc = B.f32(G, TtB)
    b_x1 = [B.f32(G, TtB), B.f32(G, TtB)]
    b_w1g = B.bf(2, G, D)
    b_w2g = B.bf(2, G, D)
    assert B.cur <= nbytes

    t_wB = ld("gp", q_w, [
        (b_w1g[:, 0], w1g_d[0].rearrange("(g p) o -> p g o", p=128)),
        (b_w1g[:, 1], w1g_d[1].rearrange("(g p) o -> p g o", p=128)),
        (b_w2g[:, 0], w2g_d[0].rearrange("(g p) o -> p g o", p=128)),
        (b_w2g[:, 1], w2g_d[1].rearrange("(g p) o -> p g o", p=128)),
    ], waits=tA)
    one("ve", c_v, "tensor_tensor", out=c_bb, in0=c_b2g[:, 0, :],
        in1=c_b2g[:, 1, :], op=ALU.add, waits=tA[:1])
    t_bb = one("ve", c_v, "tensor_scalar", out=c_bb, in0=c_bb, scalar1=0.5,
               scalar2=None, op0=ALU.mult)
    load_act_set([AF.Gelu, AF.Identity])

    tokB_cons = {}
    tokB_st = {}
    t_o_done = None
    for i in range(nB):
        par = i % 2
        t0 = i * TtB
        lw = [t_wB]
        if i - 2 in tokB_cons:
            lw.append(tokB_cons[i - 2])
        t_ld = ld("sy", qx[par], [
            (b_xn[par], fmaj(xnbf_d, t0, TtB)),
            (b_xa[par], fmaj(xa_d, t0, TtB)),
        ], waits=lw)

        if b_stop <= 1:
            tokB_cons[i] = (c_v.sem, c_v.v) if c_v.v else (c_t.sem, c_t.v)
            continue
        for s in range(2):
            hw = [t_ld, t_bb]
            if s == 1:
                hw.append(t_o_done)
            for m in range(G):
                b = m % 2
                t_mm = mm_group(b, [(PB[b][:, 0:TtB],
                                     b_w1g[:, s, g, m * 128:(m + 1) * 128],
                                     b_xn[par][:, g, :], g == 0, g == G - 1, None)
                                    for g in range(G)], waits=(hw if m == 0 else ()))
                tk = one("sc", c_s, "activation", b_h[:, m, :], PB[b][:, 0:TtB],
                         AF.Gelu, bias=c_b1g[:, s, m:m + 1], waits=[t_mm])
                read_bank(b, tk)
            t_h_all = (c_s.sem, c_s.v)
            if b_stop <= 2:
                continue
            for m in range(G):
                b = 2 + m % 2
                t_mm = mm_group(b, [(PB[b][:, 0:TtB],
                                     b_w2g[:, s, g, m * 128:(m + 1) * 128],
                                     b_h[:, g, :], g == 0, g == G - 1, None)
                                    for g in range(G)],
                                waits=([t_h_all] if m == 0 else ()))
                if s == 0:
                    tk = one("ve", c_v, "tensor_copy", b_acc[:, m, :], PB[b][:, 0:TtB],
                             waits=[t_mm])
                else:
                    tv = one("ve", c_v, "tensor_tensor", out=b_acc[:, m, :],
                             in0=b_acc[:, m, :], in1=PB[b][:, 0:TtB], op=ALU.add,
                             waits=[t_mm])
                    ts_ = one("sc", c_s, "activation", b_acc[:, m, :], b_acc[:, m, :],
                              AF.Identity, scale=0.5, bias=c_bb[:, m:m + 1],
                              waits=[tv])
                    tk = one("ve", c_v, "tensor_tensor", out=b_x1[par][:, m, :],
                             in0=b_acc[:, m, :], in1=b_xa[par][:, m, :], op=ALU.add,
                             waits=[ts_])
                read_bank(b, tk)
            if s == 0:
                t_o_done = (c_v.sem, c_v.v)
        tokB_cons[i] = (c_v.sem, c_v.v)
        if b_stop <= 3:
            continue
        stw = [tokB_cons[i]]
        if i - 2 in tokB_st:
            stw.append(tokB_st[i - 2])
        tokB_st[i] = ld("gp", q_sx1, [(fmaj(x1_d, t0, TtB), b_x1[par])], waits=stw)

    tB = [(c_v.sem, c_v.v), (c_s.sem, c_s.v), (c_t.sem, c_t.v), q_sx1.tok()]
    if phases == "AB":
        for wt in tB:
            R.w("gp", wt)
        ld("gp", q_g, [(outT, x1_d)])
        R.w("gp", q_g.tok())
        return _finish(nc, R, es)

    C = Alloc(persist_end)
    cx1 = [C.f32(G, TtC), C.f32(G, TtC)]
    c_xn1 = C.bf(G, TtC)
    c_win = C.bf(G, 1184)
    c_wout = C.bf(G, D)
    c_wq = C.bf(G, D)
    c_out = C.f32(G, TtC)
    c_xssm = C.f32(G, TtC)
    c_xsq2 = C.bf(G, TtC)
    c_Bc = C.bf(TtC)
    c_Cc = C.bf(TtC)
    c_dt = C.f32(TtC)
    c_acum = C.f32(TtC)
    c_dec = C.f32(TtC)
    c_expA = C.f32(TtC)
    c_dtbf = C.bf(TtC)
    c_decbf = C.bf(TtC)
    c_expAbf = C.bf(TtC)
    c_acumT = C.f32(KC, NH)
    c_xdtg = C.bf(TtC)
    c_xddb = C.bf(2, CS)
    c_xdtT = C.bf(KC, D)
    c_xddT = C.bf(KC, D)
    c_BT = C.bf(KC, DS)
    c_prevT = C.bf(KC, D)
    c_seg = C.f32(4, KC * CS)
    c_G4 = C.bf(4, KC * CS)
    c_cbt = C.bf(KC, CS)
    c_ys = C.f32(KC * CS)
    c_ybf = C.bf(G, TtC)
    c_rows2 = c_seg.rearrange("p a b -> p (a b)")
    assert C.cur <= nbytes, f"phase C arena {C.cur} > {nbytes}"

    t_wC = ld("gp", q_w, [
        (c_win, winT.rearrange("(g p) o -> p g o", p=128)),
        (c_wout, woutT.rearrange("(g p) o -> p g o", p=128)),
    ], waits=tB)

    load_act_set([AF.Exp, AF.Ln, AF.Square, AF.Identity])

    tstg0, tstg1 = c_xssm, c_out
    wf_r = wfT.rearrange("(h g p) o -> p (h g) o", p=128, h=2)
    t_tl = ld("sy", q_in, [(tstg0, wf_r[:, 0:4, :]), (tstg1, wf_r[:, 4:8, :])],
              waits=tB)
    flat0 = tstg0.rearrange("p a b -> p (a b)")
    flat1 = tstg1.rearrange("p a b -> p (a b)")
    one("ve", c_v, "reduce_sum", c_ys[:, 0:1], flat0, AX.X,
        apply_absolute_value=True, waits=[t_tl] + tB[:1])
    one("ve", c_v, "reduce_sum", c_ys[:, 1:2], flat1, AX.X,
        apply_absolute_value=True)
    t_r2 = one("ve", c_v, "tensor_tensor", out=c_ys[:, 0:1], in0=c_ys[:, 0:1],
               in1=c_ys[:, 1:2], op=ALU.add)
    t_sc0 = mm_group(6, [(PB[6][0:1, 0:1], c_ys[:, 0:1], c_onescf, True, True,
                          None)], waits=[t_r2, tB[2]])
    t_sc1 = one("ve", c_v, "tensor_scalar", out=c_ys[0:1, 2:3], in0=PB[6][0:1, 0:1],
                scalar1=1.0 / (D * D), scalar2=EPS_TERN, op0=ALU.mult, op1=ALU.max,
                waits=[t_sc0])
    read_bank(6, t_sc1)
    t_sc2 = mm_group(6, [(PB[6][:, 4:5], c_onesr[0:1, :], c_ys[0:1, 2:3],
                          True, True, None)], waits=[t_sc1])
    t_sc3 = one("ve", c_v, "tensor_copy", c_scl[:, 0:1], PB[6][:, 4:5], waits=[t_sc2])
    read_bank(6, t_sc3)
    t_sc4 = one("ve", c_v, "reciprocal", c_scl[:, 1:2], c_scl[:, 0:1])
    segflat = c_seg.rearrange("p a b -> p (a b)")
    ybfflat = c_ybf.rearrange("p a b -> p (a b)").bitcast(F32)
    xn1flat = c_xn1.rearrange("p a b -> p (a b)").bitcast(F32)
    tmpA = ybfflat
    tmpB = xn1flat
    wqf = c_wq.rearrange("p a b -> p (a b)")
    t_q = t_sc4
    for h, flat in ((0, flat0), (1, flat1)):
        one("ve", c_v, "tensor_scalar", out=tmpA[:, 0:2048], in0=flat[:, 0:2048],
            scalar1=c_scl[:, 1:2], scalar2=0.5, op0=ALU.mult, op1=ALU.is_ge,
            waits=[t_q])
        one("ve", c_v, "tensor_scalar", out=tmpB[:, 0:2048], in0=flat[:, 0:2048],
            scalar1=c_scl[:, 1:2], scalar2=-0.5, op0=ALU.mult, op1=ALU.is_le)
        one("ve", c_v, "tensor_tensor", out=tmpA[:, 0:2048], in0=tmpA[:, 0:2048],
            in1=tmpB[:, 0:2048], op=ALU.subtract)
        one("ve", c_v, "tensor_scalar", out=tmpB[:, 0:2048], in0=flat[:, 2048:4096],
            scalar1=c_scl[:, 1:2], scalar2=0.5, op0=ALU.mult, op1=ALU.is_ge)
        one("ve", c_v, "tensor_scalar", out=segflat[:, 0:2048], in0=flat[:, 2048:4096],
            scalar1=c_scl[:, 1:2], scalar2=-0.5, op0=ALU.mult, op1=ALU.is_le)
        one("ve", c_v, "tensor_tensor", out=tmpB[:, 0:2048], in0=tmpB[:, 0:2048],
            in1=segflat[:, 0:2048], op=ALU.subtract)
        one("ve", c_v, "tensor_scalar", out=wqf[:, h * 4096:h * 4096 + 2048],
            in0=tmpA[:, 0:2048], scalar1=c_scl[:, 0:1], scalar2=None, op0=ALU.mult)
        t_q = one("ve", c_v, "tensor_scalar",
                  out=wqf[:, h * 4096 + 2048:(h + 1) * 4096], in0=tmpB[:, 0:2048],
                  scalar1=c_scl[:, 0:1], scalar2=None, op0=ALU.mult)
    t_tern = t_q

    tokC_cons = {}
    tokC_st = {}
    for i in range(nC):
        par = i % 2
        t0 = i * TtC
        lw = [t_tern]
        if i - 2 in tokC_cons:
            lw.append(tokC_cons[i - 2])
        t_ld = ld("sy", qx[par], [(cx1[par], fmaj(x1_d, t0, TtC))], waits=lw)

        scw = [t_ld]
        if i - 1 in tokC_cons:
            scw.append(tokC_cons[i - 1])
        t_sq = one("sc", c_s, "activation", c_xsq2, cx1[par], AF.Square, waits=scw)
        t_cbf = one("sc", c_s, "activation", c_ybf, cx1[par], AF.Identity)
        t_s1 = mm_group(6, [(PB[6][0:1, 0:TtC], c_onescbf, c_ybf[:, g, :],
                             g == 0, g == G - 1, None) for g in range(G)],
                        waits=[t_cbf, t_wC, t_tern])
        t_s2 = mm_group(7, [(PB[7][0:1, 0:TtC], c_onescbf, c_xsq2[:, g, :],
                             g == 0, g == G - 1, None) for g in range(G)],
                        waits=[t_sq])
        t_rw = one("ve", c_v, "tensor_scalar", out=c_rows2[0:1, 0:TtC],
                   in0=PB[6][0:1, 0:TtC], scalar1=1.0 / D, scalar2=None,
                   op0=ALU.mult, waits=[t_s1, t_tern])
        t_rw2 = one("ve", c_v, "tensor_scalar", out=c_rows2[0:1, TtC:2 * TtC],
                    in0=PB[7][0:1, 0:TtC], scalar1=1.0 / D, scalar2=None,
                    op0=ALU.mult, waits=[t_s2])
        t_m2 = one("sc", c_s, "activation", c_rows2[0:1, 2 * TtC:3 * TtC],
                   c_rows2[0:1, 0:TtC], AF.Square, waits=[t_rw])
        t_vr = one("ve", c_v, "tensor_tensor", out=c_rows2[0:1, 3 * TtC:4 * TtC],
                   in0=c_rows2[0:1, TtC:2 * TtC], in1=c_rows2[0:1, 2 * TtC:3 * TtC],
                   op=ALU.subtract, waits=[t_m2, t_rw2])
        one("sc", c_s, "activation", c_rows2[0:1, 3 * TtC:4 * TtC],
            c_rows2[0:1, 3 * TtC:4 * TtC], AF.Ln, bias=c_fc[0:1, 0:1], waits=[t_vr])
        t_rs = one("sc", c_s, "activation", c_rows2[0:1, 3 * TtC:4 * TtC],
                   c_rows2[0:1, 3 * TtC:4 * TtC], AF.Exp, scale=-0.5)
        t_bm = mm_group(6, [(PB[6][:, 0:TtC], c_onesr[0:1, :], c_rows2[0:1, 0:TtC],
                             True, True, None)], waits=[t_rw, t_rs])
        t_br2 = mm_group(7, [(PB[7][:, 0:TtC], c_onesr[0:1, :],
                              c_rows2[0:1, 3 * TtC:4 * TtC], True, True, None)],
                         waits=[t_rw2, t_rs])
        t_xnc = one("ve", c_v, "tensor_tensor", out=c_xssm, in0=cx1[par],
                    in1=PB[6][:, None, 0:TtC].broadcast_to([128, G, TtC]),
                    op=ALU.subtract, waits=[t_bm])
        read_bank(6, t_xnc)
        t_xn1 = one("ve", c_v, "tensor_tensor", out=c_xn1, in0=c_xssm,
                    in1=PB[7][:, None, 0:TtC].broadcast_to([128, G, TtC]),
                    op=ALU.mult, waits=[t_br2])
        read_bank(7, t_xn1)

        for jm in range(G // 2):
            for h in range(2):
                m = 2 * jm + h
                mw = [t_xn1] if m == 0 else []
                t_mm = mm_group(h, [(PB[h][:, 0:TtC],
                                     c_win[:, g, m * 128:(m + 1) * 128],
                                     c_xn1[:, g, :], g == 0, g == G - 1, None)
                                    for g in range(G)], waits=mw)
            tk = one("ve", c_v, "tensor_copy", c_xssm[:, 2 * jm:2 * jm + 2, :],
                     PBPR2[0][:, :, 0:TtC], waits=[t_mm, t_xn1])
            read_bank(0, tk)
            read_bank(1, tk)
        mm_specs = [("Ba", c_win[:, :, 1024:1088], 64),
                    ("Cb", c_win[:, :, 1088:1152], 64),
                    ("dt", c_win[:, :, 1152:1184], 32)]
        for si, (mk, wsl, mwid) in enumerate(mm_specs):
            b = si % 2
            mw = []
            t_mm = mm_group(b, [(PB[b][0:mwid, 0:TtC], wsl[:, g, :], c_xn1[:, g, :],
                                 g == 0, g == G - 1, None) for g in range(G)],
                            waits=mw)
            if mk == "Ba":
                tk = one("sc", c_s, "activation", c_Bc[0:64, :], PB[b][0:64, 0:TtC],
                         AF.Identity, waits=[t_mm])
            elif mk == "Cb":
                tk = one("sc", c_s, "activation", c_Cc[0:64, :], PB[b][0:64, 0:TtC],
                         AF.Identity, waits=[t_mm])
            else:
                one("sc", c_s, "activation", c_dt[0:NH, :], PB[b][0:NH, 0:TtC],
                    AF.Exp, bias=c_dtb[0:NH, 0:1], waits=[t_mm])
                tk = one("sc", c_s, "activation", c_dt[0:NH, :], c_dt[0:NH, :],
                         AF.Ln, bias=c_onescf[0:NH, 0:1])
            read_bank(b, tk)
        t_dt_done = tk
        t_xs_all = (c_v.sem, c_v.v)

        t_dtbf = one("sc", c_s, "activation", c_dtbf[0:NH, :], c_dt[0:NH, :],
                     AF.Identity, waits=[t_dt_done])
        t_a = one("ve", c_v, "tensor_scalar", out=c_dt[0:NH, :], in0=c_dt[0:NH, :],
                  scalar1=c_A[0:NH, 0:1], scalar2=None, op0=ALU.mult,
                  waits=[t_dtbf])
        for cc in range(KC):
            quiet("ve", "tensor_tensor_scan", c_acum[0:NH, cc * CS:(cc + 1) * CS],
                  c_onescf[0:NH, 0:1].broadcast_to([NH, CS]),
                  c_dt[0:NH, cc * CS:(cc + 1) * CS], 0.0, ALU.mult, ALU.add)
        t_acum = seal("ve", c_v)
        for cc in range(KC):
            quiet("sc", "activation", c_dec[0:NH, cc * CS:(cc + 1) * CS],
                  c_acum[0:NH, cc * CS:(cc + 1) * CS], AF.Exp, scale=-1.0,
                  bias=c_acum[0:NH, (cc + 1) * CS - 1:(cc + 1) * CS],
                  waits=([t_acum] if cc == 0 else ()))
        one("sc", c_s, "activation", c_expA[0:NH, :], c_acum[0:NH, :], AF.Exp)
        one("sc", c_s, "activation", c_expAbf[0:NH, :], c_expA[0:NH, :], AF.Identity)
        t_decs = one("sc", c_s, "activation", c_decbf[0:NH, :], c_dec[0:NH, :],
                     AF.Identity)

        atw = [t_acum]
        if 3 in bank_r:
            atw.append(bank_r.pop(3))
        for wt in atw:
            R.w("te", wt)
        for cc in range(KC):
            quiet("te", "transpose", PB[3][:, cc * NH:(cc + 1) * NH],
                  c_acum[0:NH, cc * CS:(cc + 1) * CS], c_idf[0:NH, 0:NH])
        t_at = seal("te", c_t)
        t_acT = one("ve", c_v, "tensor_copy", c_acumT.rearrange("p a b -> p (a b)"),
                    PB[3][:, 0:KC * NH], waits=[t_at])
        read_bank(3, t_acT)

        tbank = (PBbf[2], PBbf[3])
        tp_r = {0: None, 1: None}
        nt_box = [0]

        def do_transpose_bf(src_ap, dst_ap, p_src, f_src, ident, extra=()):
            half = nt_box[0] % 2
            nt_box[0] += 1
            w = list(extra)
            if tp_r[half] is not None:
                w.append(tp_r[half])
            if (2 + half) in bank_r:
                w.append(bank_r.pop(2 + half))
            for wt in w:
                R.w("te", wt)
            c_t.nxt()
            R.i("te", "transpose", tbank[half][0:f_src, 0:p_src],
                src_ap, ident, inc=(c_t.sem, 1))
            t_tp = c_t.tok()
            tk = one("sc", c_s, "activation", dst_ap,
                     tbank[half][0:f_src, 0:p_src],
                     AF.Identity, waits=[t_tp])
            tp_r[half] = tk
            read_bank(2 + half, tk)
            return tk

        t_prev_g = None
        xdb_tok = {0: None, 1: None}
        for g in range(G):
            bdt = 4 + (g % 2)
            t_rp = mm_group(bdt, [(PB[bdt][:, 0:TtC],
                                   c_repl[0:NH, g * 128:(g + 1) * 128],
                                   c_dtbf[0:NH, :], True, True, None)],
                            waits=([t_dtbf, t_xs_all, t_acT] if g == 0 else ()))
            xw = [t_rp]
            if t_prev_g is not None:
                xw.append(t_prev_g)
            t_xg = one("ve", c_v, "tensor_tensor", out=c_xdtg, in0=c_xssm[:, g, :],
                       in1=PB[bdt][:, 0:TtC], op=ALU.mult, waits=xw)
            read_bank(bdt, t_xg)
            bdc = 6 + (g % 2)
            t_rp2 = mm_group(bdc, [(PB[bdc][:, 0:TtC],
                                    c_repl[0:NH, g * 128:(g + 1) * 128],
                                    c_decbf[0:NH, :], True, True, None)],
                             waits=([t_decs] if g == 0 else ()))
            t_xd = None
            for cc in range(KC):
                t_tr1 = do_transpose_bf(c_xdtg[:, cc * CS:(cc + 1) * CS],
                                        c_xdtT[:, cc, g * 128:(g + 1) * 128],
                                        128, 128, c_idbf, extra=[t_xg])
                xdw = [t_rp2]
                if xdb_tok[cc % 2] is not None:
                    xdw.append(xdb_tok[cc % 2])
                t_xd = one("ve", c_v, "tensor_tensor", out=c_xddb[:, cc % 2, :],
                           in0=c_xdtg[:, cc * CS:(cc + 1) * CS],
                           in1=PB[bdc][:, cc * CS:(cc + 1) * CS], op=ALU.mult,
                           waits=xdw)
                xdb_tok[cc % 2] = do_transpose_bf(
                    c_xddb[:, cc % 2, :], c_xddT[:, cc, g * 128:(g + 1) * 128],
                    128, 128, c_idbf, extra=[t_xd])
            read_bank(bdc, t_xd)
            t_prev_g = xdb_tok[(KC - 1) % 2]
        t_trans_last = None
        for cc in range(KC):
            t_trans_last = do_transpose_bf(
                c_Bc[0:64, cc * CS:(cc + 1) * CS],
                c_BT[:, cc, :], 64, 128, c_idbf[0:64, 0:64], extra=[t_dt_done])
        t_trans_done = t_trans_last

        t_prev_up = None
        t_cb_last = None
        for cc in range(KC):
            t_cb = mm_group(2, [(PB[2][:, 0:CS], c_Bc[0:64, cc * CS:(cc + 1) * CS],
                                 c_Cc[0:64, cc * CS:(cc + 1) * CS], True, True,
                                 None)], waits=([t_trans_done] if cc == 0 else []))
            tk = one("ve", c_v, "tensor_tensor", out=c_cbt[:, cc, :],
                     in0=PB[2][:, 0:CS], in1=c_tri, op=ALU.mult, waits=[t_cb])
            read_bank(2, tk)
            t_cb_last = tk
            pw = [tk] + ([t_prev_up] if t_prev_up else [])
            for wt in pw:
                R.w("te", wt)
            for g in range(G):
                c_t.nxt()
                R.i("te", "transpose", PB[2][0:64, g * 128:(g + 1) * 128]
                    if g < 4 else PB[3][0:64, (g - 4) * 128:(g - 4 + 1) * 128],
                    c_prev[:, g, :], c_idf, inc=(c_t.sem, 1))
            t_pt = c_t.tok()
            tk1 = one("sc", c_s, "activation", c_prevT[0:64, cc, 0:512],
                      PB[2][0:64, 0:512], AF.Identity, waits=[t_pt, t_acT])
            tk2 = one("sc", c_s, "activation", c_prevT[0:64, cc, 512:1024],
                      PB[3][0:64, 0:512], AF.Identity)
            read_bank(2, tk2)
            read_bank(3, tk2)
            t_st_mm = mm_group(6, [(PB[6][:, g * 64:(g + 1) * 64],
                                    c_xddT[:, cc, g * 128:(g + 1) * 128],
                                    c_BT[:, cc, :], True, True, None)
                                   for g in range(G)], waits=[tk2])
            t_cd = mm_group(7, [(PB[7][:, g:g + 1],
                                 c_repl[0:NH, g * 128:(g + 1) * 128],
                                 c_expAbf[0:NH, (cc + 1) * CS - 1:(cc + 1) * CS],
                                 True, True, None) for g in range(G)], waits=[])
            tk = one("ve", c_v, "tensor_tensor", out=c_prev,
                     in0=c_prev,
                     in1=PB[7][:, 0:G][:, :, None].broadcast_to([128, G, DS]),
                     op=ALU.mult, waits=[t_cd, tk2])
            read_bank(7, tk)
            t_prev_up = one("ve", c_v, "tensor_tensor", out=c_prev, in0=c_prev,
                            in1=PB[6][:, 0:G * DS].rearrange("p (a b) -> p a b", a=G),
                            op=ALU.add, waits=[t_st_mm])
            read_bank(6, t_prev_up)

        t_y_done = None
        grp_read = None
        for grp in range(G):
            bl = []
            for hh in range(4):
                h = 4 * grp + hh
                bl.append((PB[4 + hh][:, 0:TtC],
                           c_idf[0:NH, h:h + 1].broadcast_to([NH, 128]),
                           c_acum[0:NH, 0:TtC], True, True, None))
            bw = [t_acum] if grp == 0 else []
            for hh in range(1, 4):
                if 4 + hh in bank_r:
                    bw.append(bank_r.pop(4 + hh))
            t_bch = mm_group(4, bl, waits=bw)
            t_sub = None
            for hh in range(4):
                h = 4 * grp + hh
                t_sub = one("ve", c_v, "tensor_tensor",
                            out=c_seg[:, hh, :].rearrange("p (a b) -> p a b", a=KC),
                            in0=PB[4 + hh][:, 0:TtC].rearrange("p (a b) -> p a b", a=KC),
                            in1=c_acumT[:, :, h:h + 1].broadcast_to([128, KC, CS]),
                            op=ALU.subtract,
                            waits=([t_bch, t_cb_last] if hh == 0 else ()))
            for hh in range(4):
                read_bank(4 + hh, t_sub)
            t_min = one("ve", c_v, "tensor_scalar", out=segflat[:, 0:4 * KC * CS],
                        in0=segflat[:, 0:4 * KC * CS], scalar1=0.0, scalar2=None,
                        op0=ALU.min, waits=[t_sub])
            t_exp = one("sc", c_s, "activation",
                        c_G4.rearrange("p a b -> p (a b)"),
                        segflat[:, 0:4 * KC * CS], AF.Exp, waits=[t_min])
            t_g4 = one("ve", c_v, "tensor_tensor",
                       out=c_G4, in0=c_G4,
                       in1=c_cbt[:, None, :, :].broadcast_to([128, 4, KC, CS]),
                       op=ALU.mult, waits=[t_exp])
            ydl = []
            for cc in range(KC):
                for hh in range(4):
                    h = 4 * grp + hh
                    ydl.append((PB[2][hh * 32:(hh + 1) * 32, cc * CS:(cc + 1) * CS],
                                c_xdtT[:, cc, h * 32:(h + 1) * 32],
                                c_G4[:, hh, cc * CS:(cc + 1) * CS], True, True,
                                (0, 32 * hh)))
            t_yd = mm_group(2, ydl, waits=[t_g4, t_prev_up])
            yol = []
            for cc in range(KC):
                yol.append((PB[3][:, cc * CS:(cc + 1) * CS],
                            c_prevT[0:64, cc, grp * 128:(grp + 1) * 128],
                            c_Cc[0:64, cc * CS:(cc + 1) * CS], True, True, None))
            t_yo = mm_group(3, yol, waits=[])
            t_ea = mm_group(4, [(PB[4][:, cc * CS:(cc + 1) * CS],
                                 c_repl[0:NH, grp * 128:(grp + 1) * 128],
                                 c_expAbf[0:NH, cc * CS:(cc + 1) * CS], True, True,
                                 None) for cc in range(KC)], waits=[t_min])
            tk0 = one("ve", c_v, "tensor_copy", c_ys, PB[3][:, 0:TtC],
                      waits=[t_yo])
            read_bank(3, tk0)
            tk = one("ve", c_v, "tensor_tensor", out=c_ys, in0=c_ys,
                     in1=PB[4][:, 0:TtC], op=ALU.mult, waits=[t_ea])
            read_bank(4, tk)
            tk2 = one("ve", c_v, "tensor_tensor", out=c_ys, in0=c_ys,
                      in1=PB[2][:, 0:TtC], op=ALU.add, waits=[t_yd])
            read_bank(2, tk2)
            t_y_done = one("ve", c_v, "scalar_tensor_tensor", out=c_ybf[:, grp, :],
                           in0=c_xssm[:, grp, :], scalar=c_dexp[:, grp:grp + 1],
                           in1=c_ys, op0=ALU.mult, op1=ALU.add)

        for jm in range(G // 2):
            for h in range(2):
                m = 2 * jm + h
                mw = [t_y_done] if m == 0 else []
                if m == 0 and i - 1 in tokC_st:
                    mw.append(tokC_st[i - 1])
                mml = [(PB[h][:, 0:TtC], c_wout[:, g, m * 128:(m + 1) * 128],
                        c_ybf[:, g, :], g == 0, False, None) for g in range(G)]
                mml += [(PB[h][:, 0:TtC], c_wq[:, g, m * 128:(m + 1) * 128],
                         c_xn1[:, g, :], False, g == G - 1, None) for g in range(G)]
                t_mm = mm_group(h, mml, waits=mw)
            tk = one("ve", c_v, "tensor_tensor",
                     out=c_out[:, 2 * jm:2 * jm + 2, :],
                     in0=cx1[par][:, 2 * jm:2 * jm + 2, :],
                     in1=PBPR2[0][:, :, 0:TtC],
                     op=ALU.add, waits=[t_mm])
            read_bank(0, tk)
            read_bank(1, tk)
        tokC_cons[i] = (c_v.sem, c_v.v)
        stw = [tokC_cons[i]]
        if i - 1 in tokC_st:
            stw.append(tokC_st[i - 1])
        tokC_st[i] = ld("gp", q_so, [(fmaj(outT, t0, TtC), c_out)], waits=stw)

    R.w("gp", q_so.tok())
    R.w("gp", q_g.tok())
    return _finish(nc, R, es)


def _finish(nc, R, es):
    with nc.Block() as block:
        @block.sync
        def _(sy):
            R.replay("sy", sy)

        @block.tensor
        def _(te):
            R.replay("te", te)

        @block.vector
        def _(ve):
            R.replay("ve", ve)

        @block.scalar
        def _(sc):
            R.replay("sc", sc)

        @block.gpsimd
        def _(gp):
            R.replay("gp", gp)

    es.close()
    return nc


_NCORES = 8
_T = 8192


def _pack_pg(v):
    return np.ascontiguousarray(np.asarray(v, np.float32).reshape(G, 128).T)


def kernel(
    L0_attn_x, L0_attn_time_decay, L0_attn_time_first, L0_attn_time_mix_k,
    L0_attn_time_mix_v, L0_attn_time_mix_r, L0_attn_Wk, L0_attn_Wv, L0_attn_Wr,
    L0_attn_Wo, L0_ffn_Wr, L0_ffn_br, L0_ffn_W1, L0_ffn_b1, L0_ffn_W2, L0_ffn_b2,
    L1_attn_W_in, L1_attn_dt_bias, L1_attn_A_log, L1_attn_D, L1_attn_W_out,
    L1_ffn_weight,
):
    x = np.asarray(L0_attn_x, np.float32)
    Bsz, T, _ = x.shape
    shared = {
        "wkT": np.ascontiguousarray(np.asarray(L0_attn_Wk, np.float32).T),
        "wvT": np.ascontiguousarray(np.asarray(L0_attn_Wv, np.float32).T),
        "wrT": np.ascontiguousarray(np.asarray(L0_attn_Wr, np.float32).T),
        "woT": np.ascontiguousarray(np.asarray(L0_attn_Wo, np.float32).T),
        "tmk": _pack_pg(L0_attn_time_mix_k),
        "tmv": _pack_pg(L0_attn_time_mix_v),
        "tmr": _pack_pg(L0_attn_time_mix_r),
        "tdp": _pack_pg(L0_attn_time_decay),
        "tfp": _pack_pg(L0_attn_time_first),
        "wr_rtr": np.ascontiguousarray(np.asarray(L0_ffn_Wr, np.float32)),
        "br_rtr": np.asarray(L0_ffn_br, np.float32).reshape(1, NE),
        "w1": np.ascontiguousarray(np.asarray(L0_ffn_W1, np.float32)),
        "b1": np.ascontiguousarray(np.asarray(L0_ffn_b1, np.float32)),
        "w2": np.ascontiguousarray(np.asarray(L0_ffn_W2, np.float32)),
        "b2": np.ascontiguousarray(np.asarray(L0_ffn_b2, np.float32)),
        "winT": np.ascontiguousarray(np.asarray(L1_attn_W_in, np.float32).T),
        "dtb": np.asarray(L1_attn_dt_bias, np.float32).reshape(NH, 1),
        "alog": np.asarray(L1_attn_A_log, np.float32).reshape(NH, 1),
        "dexp": _pack_pg(np.repeat(np.asarray(L1_attn_D, np.float32), HD)),
        "woutT": np.ascontiguousarray(np.asarray(L1_attn_W_out, np.float32).T),
        "wfT": np.ascontiguousarray(np.asarray(L1_ffn_weight, np.float32).T),
    }
    in_maps = []
    for b in range(Bsz):
        m = dict(shared)
        m["xT"] = np.ascontiguousarray(x[b].T)
        in_maps.append(m)
    nc = build_program(T)
    res = run_bass_kernel_spmd(nc, in_maps, core_ids=list(range(_NCORES)))
    out = np.stack([np.asarray(r["outT"]).T for r in res.results], axis=0)
    return np.ascontiguousarray(out.astype(np.float32))


# version 08
# speedup vs baseline: 1.0323x; 1.0323x over previous
import numpy as np
import ml_dtypes
import concourse.bass as bass
import concourse.mybir as mybir
from concourse.bass_utils import run_bass_kernel_spmd
from concourse.bass_core_rust import IndirectOffsetOnAxis
from contextlib import ExitStack

F32 = mybir.dt.float32
BF16 = mybir.dt.bfloat16
I32 = mybir.dt.int32
U32 = mybir.dt.uint32
U8 = mybir.dt.uint8
AF = mybir.ActivationFunctionType
ALU = mybir.AluOpType
AX = mybir.AxisListType
BFNP = ml_dtypes.bfloat16

D = 1024
G = 8
NE = 8
NH, HD, DS = 32, 32, 64
CS = 128
EPS_LN = 1e-5
EPS_TERN = 1e-5

ENGS = ("sy", "te", "ve", "sc", "gp")


class Rec:
    def __init__(self):
        self.q = {e: [] for e in ENGS}

    def w(self, eng, tok):
        if tok is not None:
            sem, val = tok
            if val > 0:
                self.q[eng].append(("wait_ge", (sem, val), {}, None))

    def i(self, eng, method, *args, inc=None, **kw):
        self.q[eng].append((method, args, kw, inc))

    def replay(self, name, proxy):
        for method, args, kw, inc in self.q[name]:
            if method == "__act_table__":
                r = proxy.add_instruction(mybir.InstLoadActFuncSet(
                    name=proxy.bass.get_next_instruction_name(),
                    act_func_set_id=args[0], ins=[], outs=[]))
            else:
                r = getattr(proxy, method)(*args, **kw)
            if inc is not None:
                r.then_inc(inc[0], inc[1])


class Ctr:
    def __init__(self, sm, step=1):
        self.sem, self.v, self.step = sm, 0, step

    def nxt(self):
        self.v += self.step
        return (self.sem, self.v)

    def tok(self):
        return (self.sem, self.v)


def build_program(T, TtA=256, TtB=512, TtC=512, phases="ABC", a_stop=10**9, b_stop=10**9, r_stop=10**9, rcheck=False):
    nc = bass.Bass("TRN2", target_bir_lowering=False, detect_race_conditions=rcheck)
    es = ExitStack()
    nA, nB, nC = T // TtA, T // TtB, T // TtC
    KC = TtC // CS

    def din(name, shape, dtype=F32):
        return nc.dram_tensor(name, list(shape), dtype, kind="ExternalInput").ap()

    xT = din("xT", [D, T])
    wkT = din("wkT", [D, D]); wvT = din("wvT", [D, D])
    wrT = din("wrT", [D, D]); woT = din("woT", [D, D])
    tmk = din("tmk", [128, G]); tmv = din("tmv", [128, G]); tmr = din("tmr", [128, G])
    tdp = din("tdp", [128, G]); tfp = din("tfp", [128, G])
    wr_rtr = din("wr_rtr", [D, NE]); br_rtr = din("br_rtr", [1, NE])
    w1 = din("w1", [NE, D, D]); b1 = din("b1", [NE, D])
    w2 = din("w2", [NE, D, D]); b2 = din("b2", [NE, D])
    winT = din("winT", [D, 1184])
    dtb = din("dtb", [NH, 1]); alog = din("alog", [NH, 1])
    dexp = din("dexp", [128, G])
    woutT = din("woutT", [D, D])
    wfT = din("wfT", [D, D])
    outT = nc.dram_tensor("outT", [D, T], F32, kind="ExternalOutput").ap()

    xnbf_d = nc.dram_tensor("xnbf_d", [D, T], BF16, kind="Internal").ap()
    xa_d = nc.dram_tensor("xa_d", [D, T], F32, kind="Internal").ap()
    x1_d = nc.dram_tensor("x1_d", [D, T], F32, kind="Internal").ap()
    w1g_d = nc.dram_tensor("w1g_d", [2, D, D], BF16, kind="Internal").ap()
    w2g_d = nc.dram_tensor("w2g_d", [2, D, D], BF16, kind="Internal").ap()
    bg_d = nc.dram_tensor("bg_d", [4, D], F32, kind="Internal").ap()
    idx_d = nc.dram_tensor("idx_d", [1, 2], U32, kind="Internal").ap()

    ident_bf_c = nc.inline_tensor(np.eye(128, dtype=BFNP), "ident_bf_c")
    ident_f_c = nc.inline_tensor(np.eye(128, dtype=np.float32), "ident_f_c")
    ones_row_f_c = nc.inline_tensor(np.ones((1, 128), np.float32), "ones_row_f_c")
    ones_col_bf_c = nc.inline_tensor(np.ones((128, 1), BFNP), "ones_col_bf_c")
    repl_np = np.zeros((NH, G * 128), np.float32)
    for g in range(G):
        for m in range(128):
            repl_np[4 * g + m // 32, g * 128 + m] = 1.0
    repl_bf_c = nc.inline_tensor(repl_np.astype(BFNP), "repl_bf_c")
    tri_c = nc.inline_tensor(
        (np.arange(CS)[:, None] <= np.arange(CS)[None, :]).astype(np.float32), "tri_c")
    iota2d_c = nc.inline_tensor(
        (np.arange(128)[:, None] + 128 * np.arange(G)[None, :]).astype(np.int32),
        "iota2d_c")
    iota8_c = nc.inline_tensor(
        np.tile(np.arange(8, dtype=np.float32), (128, 1)), "iota8_c")
    selh_np = np.zeros((NH, NH * 128), np.float32)
    for hh_ in range(NH):
        selh_np[hh_, hh_ * 128:(hh_ + 1) * 128] = 1.0
    selh_c = nc.inline_tensor(selh_np, "selh_c")

    nbytes = nc.sbuf_bytes_remaining - 2048
    nbytes -= nbytes % 256
    arena = nc.alloc_sbuf_tensor("arena", [128, nbytes], U8)
    AFV = {F32: arena.bitcast(F32), BF16: arena.bitcast(BF16),
           I32: arena.bitcast(I32), U32: arena.bitcast(U32)}
    DSZ = {F32: 4, BF16: 2, I32: 4, U32: 4}

    class Alloc:
        def __init__(self, base):
            self.cur = base

        def mk(self, dt_, *shape):
            n = int(np.prod(shape))
            sz = DSZ[dt_]
            off = (self.cur + 63) // 64 * 64
            self.cur = off + n * sz
            assert self.cur <= nbytes, f"sbuf arena overflow: {self.cur} > {nbytes}"
            v = AFV[dt_][:, off // sz: off // sz + n]
            if len(shape) == 3:
                return v.rearrange("p (a b c) -> p a b c", a=shape[0], b=shape[1])
            if len(shape) == 2:
                return v.rearrange("p (a b) -> p a b", a=shape[0])
            return v

        def f32(self, *s):
            return self.mk(F32, *s)

        def bf(self, *s):
            return self.mk(BF16, *s)

        def i32(self, *s):
            return self.mk(I32, *s)

        def view(self, dt_, off, *shape):
            n = int(np.prod(shape))
            sz = DSZ[dt_]
            v = AFV[dt_][:, off // sz: off // sz + n]
            if len(shape) == 3:
                return v.rearrange("p (a b c) -> p a b c", a=shape[0], b=shape[1])
            if len(shape) == 2:
                return v.rearrange("p (a b) -> p a b", a=shape[0])
            return v

    P = Alloc(0)
    c_tmx = {"k": P.f32(G), "v": P.f32(G), "r": P.f32(G)}
    c_ew = P.f32(G); c_eu = P.f32(G); c_dexp = P.f32(G)
    c_A = P.f32(1); c_dtb = P.f32(1)
    c_idbf = P.bf(128); c_idf = P.f32(128)
    c_onesr = P.f32(128)
    c_onescbf = P.bf(1)
    c_onescf = P.f32(1)
    c_repl = P.bf(G * 128)
    c_tri = P.f32(CS)
    c_iota2 = P.i32(G)
    c_wr = P.bf(G, NE)
    c_br = P.f32(NE)
    c_sco = P.f32(64)
    c_idxu = P.mk(U32, 8)
    c_eb32 = P.i32(2)
    c_gidx = P.i32(2, G)
    c_b1g = P.f32(2, G); c_b2g = P.f32(2, G); c_bb = P.f32(G)
    c_scl = P.f32(2)
    c_prev = P.f32(G, DS)
    c_tmpg = P.f32(G)
    c_fc = P.f32(2)
    c_msel = P.f32(2)
    c_ebf = P.f32(2)
    c_iotf = P.f32(1)
    persist_end = P.cur

    ps = nc.alloc_psum_tensor("ps", [128, 4096], F32)
    psa = ps.ap()
    PB = [psa[:, 512 * b: 512 * (b + 1)] for b in range(8)]
    ps_bf = ps.bitcast(BF16)
    PBbf = [ps_bf[:, 1024 * b: 1024 * (b + 1)] for b in range(8)]
    PBPR2 = [psa[:, 1024 * p: 1024 * (p + 1)].rearrange(
        "p (a c) -> p a c", a=2) for p in range(2)]

    def mksem(name):
        return es.enter_context(nc.semaphore(name))

    q_in = Ctr(mksem("q_in"), 16)
    q_w = Ctr(mksem("q_w"), 16)
    qx = [Ctr(mksem("qx0"), 16), Ctr(mksem("qx1"), 16)]
    q_sxn = Ctr(mksem("q_sxn"), 16)
    q_sxa = Ctr(mksem("q_sxa"), 16)
    q_sx1 = Ctr(mksem("q_sx1"), 16)
    q_so = Ctr(mksem("q_so"), 16)
    q_g = Ctr(mksem("q_g"), 16)
    c_t = Ctr(mksem("c_t"))
    c_v = Ctr(mksem("c_v"))
    c_s = Ctr(mksem("c_s"))
    q_st = q_sxn

    R = Rec()

    def ld(qeng, ctr, pairs, waits=(), **kw):
        for wt in waits:
            R.w(qeng, wt)
        for out, in_ in pairs:
            ctr.nxt()
            R.i(qeng, "dma_start", out=out, in_=in_, inc=(ctr.sem, 16), **kw)
        return ctr.tok()

    CTRS = {"ve": c_v, "sc": c_s, "te": c_t}

    def one(eng, ctr, method, *args, waits=(), **kw):
        if eng in ("ve", "sc") and ctr.v > 0:
            R.w(eng, ctr.tok())
        for wt in waits:
            R.w(eng, wt)
        ctr.nxt()
        R.i(eng, method, *args, inc=(ctr.sem, 1), **kw)
        return ctr.tok()

    def quiet(eng, method, *args, waits=(), **kw):
        if eng in ("ve", "sc"):
            return one(eng, CTRS[eng], method, *args, waits=waits, **kw)
        for wt in waits:
            R.w(eng, wt)
        R.i(eng, method, *args, **kw)

    def seal(eng, ctr):
        if eng in ("ve", "sc"):
            return ctr.tok()
        m, a, k, inc = R.q[eng][-1]
        assert inc is None and m != "wait_ge", (m, inc)
        ctr.nxt()
        R.q[eng][-1] = (m, a, k, (ctr.sem, 1))
        return ctr.tok()

    def fmaj(ap_, t0, tt):
        return ap_.rearrange("(g p) t -> p g t", p=128)[:, :, t0:t0 + tt]

    from concourse.hw_specs import get_activation_tables
    _act_sets = list(get_activation_tables(nc.m.arch).values())

    def load_act_set(funcs):
        for idx, fs in enumerate(_act_sets):
            if all(f in fs for f in funcs):
                if c_s.v > 0:
                    R.w("sc", c_s.tok())
                c_s.nxt()
                R.i("sc", "__act_table__", idx, inc=(c_s.sem, 1))
                return c_s.tok()
        raise ValueError(f"no act table set contains {funcs}")

    bank_r = {}

    def mm_group(b, mm_list, waits=()):
        w = list(waits)
        if b in bank_r:
            w.append(bank_r.pop(b))
        first = True
        for (out, lhsT, rhs, st, sp, tp) in mm_list:
            kw = {"start": st, "stop": sp}
            if tp is not None:
                kw["tile_position"] = tp
            quiet("te", "matmul", out, lhsT, rhs, waits=(w if first else ()), **kw)
            first = False
        return seal("te", c_t)

    def read_bank(b, tok):
        bank_r[b] = tok

    t_pro = ld("sy", q_in, [
        (c_tmx["k"], tmk), (c_tmx["v"], tmv), (c_tmx["r"], tmr),
        (c_dexp, dexp),
        (c_idbf, ident_bf_c.ap()), (c_idf, ident_f_c.ap()),
        (c_onesr[0:1, :], ones_row_f_c.ap()),
        (c_onescbf, ones_col_bf_c.ap()),
        (c_repl[0:NH, :], repl_bf_c.ap()),
        (c_tri, tri_c.ap()),
        (c_iota2, iota2d_c.ap()),
        (c_br[0:1, :], br_rtr),
        (c_sco[:, 36:44], iota8_c.ap()),
        (c_A[0:NH, :], alog),
        (c_dtb[0:NH, :], dtb),
        (c_tmpg, tdp),
    ])
    load_act_set([AF.Exp, AF.Ln, AF.Square, AF.Identity])
    one("sc", c_s, "activation", c_ew, c_tmpg, AF.Exp, waits=[t_pro])
    one("sc", c_s, "activation", c_ew, c_ew, AF.Exp, scale=-1.0)
    t_sA = one("sc", c_s, "activation", c_A[0:NH, :], c_A[0:NH, :], AF.Exp)
    t_tf = ld("sy", q_in, [(c_tmpg, tfp)], waits=[t_sA])
    t_pro_sc = one("sc", c_s, "activation", c_eu, c_tmpg, AF.Exp, waits=[t_tf])
    one("ve", c_v, "tensor_scalar_mul", c_A[0:NH, :], c_A[0:NH, :], -1.0,
        waits=[t_sA])
    one("ve", c_v, "memset", c_onescf, 1.0, waits=[t_pro])
    t_pro_ve = one("ve", c_v, "memset", c_fc[:, 0:1], EPS_LN)

    if phases == "pro":
        R.w("sy", q_in.tok())
        R.w("sc", (c_s.sem, c_s.v))
        R.w("ve", (c_v.sem, c_v.v))
        return _finish(nc, R, es)

    A = Alloc(persist_end)
    a_x = [A.f32(G, TtA), A.f32(G, TtA)]
    a_xnb = A.f32(G, TtA + 1)
    a_al = A.f32(G, TtA + 1)
    a_be = A.f32(G, TtA + 1)
    off_dl = (A.cur + 63) // 64 * 64
    a_dl = A.f32(G, TtA)
    a_tmp = A.f32(G, TtA)
    a_ek = A.f32(G, TtA)
    a_ekv = A.f32(G, TtA)
    a_er = A.f32(G, TtA)
    a_xsq = A.view(BF16, off_dl, G, TtA)
    a_xnbf = A.bf(G, TtA)
    a_xbf = a_xnbf
    a_xm = [{q: A.bf(G, TtA) for q in ("k", "v", "r")} for _ in range(2)]
    a_rwkv = A.bf(G, TtA)
    a_xa = A.f32(G, TtA)
    a_rows = A.f32(4 * TtA)
    a_wk = A.bf(G, D); a_wv = A.bf(G, D); a_wrr = A.bf(G, D); a_wo = A.bf(G, D)
    S = Alloc(nbytes - 4 * D - 192)
    a_stage = S.f32(D)
    assert A.cur <= nbytes - 4 * D - 192, (A.cur, nbytes - 4 * D - 192)

    t_wA = ld("gp", q_w, [
        (c_wr, wr_rtr.rearrange("(g p) e -> p g e", p=128)),
        (a_wk, wkT.rearrange("(g p) o -> p g o", p=128)),
        (a_wv, wvT.rearrange("(g p) o -> p g o", p=128)),
        (a_wrr, wrT.rearrange("(g p) o -> p g o", p=128)),
        (a_wo, woT.rearrange("(g p) o -> p g o", p=128)),
    ])

    one("ve", c_v, "memset", a_xnb[:, :, 0:1], 0.0)
    one("ve", c_v, "memset", a_al[:, :, 0:1], 0.0)
    one("ve", c_v, "memset", a_be[:, :, 0:1], 0.0)
    one("ve", c_v, "memset", c_prev, 0.0)

    tokA_cons = {}
    tokA_st = {}
    tokA_xst = {}
    tokA_stat = {}
    tokA_kvr = {}
    tokA_mixv = {}
    tokA_rwkv = {}
    evq = {}
    s1tok = {}
    t_gather_done = None
    mixes = (("k", a_wk), ("v", a_wv), ("r", a_wrr))

    def emit_A1(i):
        par = i % 2
        t0 = i * TtA
        xin = a_x[par]

        lw = [t_pro]
        if i - 2 in tokA_cons:
            lw.append(tokA_cons[i - 2])
        t_ld = ld("sy", qx[par], [(xin, fmaj(xT, t0, TtA))], waits=lw)

        cw = [t_ld, t_pro_sc]
        if i - 1 in tokA_stat:
            cw.append(tokA_stat[i - 1])
        if i - 1 in tokA_mixv:
            cw.append(tokA_mixv[i - 1])
        if i - 1 in tokA_xst:
            cw.append(tokA_xst[i - 1])
        one("sc", c_s, "activation", a_xbf, xin, AF.Identity, waits=cw)
        t_c2 = one("sc", c_s, "activation", a_xsq, xin, AF.Square)

        t_stat = mm_group(6,
                          [(PB[6][0:1, 0:TtA], c_onescbf, a_xbf[:, g, :],
                            g == 0, g == G - 1, None) for g in range(G)] +
                          [(PB[6][0:1, TtA:2 * TtA], c_onescbf, a_xsq[:, g, :],
                            g == 0, g == G - 1, None) for g in range(G)],
                          waits=[t_c2])
        tokA_stat[i] = t_stat

        t_rows = one("ve", c_v, "tensor_scalar", out=a_rows[0:1, 0:2 * TtA],
                     in0=PB[6][0:1, 0:2 * TtA], scalar1=1.0 / D, scalar2=None,
                     op0=ALU.mult, waits=[t_stat, t_pro_ve])
        read_bank(6, t_rows)
        t_m2 = one("sc", c_s, "activation", a_rows[0:1, 2 * TtA:3 * TtA],
                   a_rows[0:1, 0:TtA], AF.Square, waits=[t_rows])
        t_var = one("ve", c_v, "tensor_tensor", out=a_rows[0:1, 3 * TtA:4 * TtA],
                    in0=a_rows[0:1, TtA:2 * TtA], in1=a_rows[0:1, 2 * TtA:3 * TtA],
                    op=ALU.subtract, waits=[t_m2])
        one("sc", c_s, "activation", a_rows[0:1, 3 * TtA:4 * TtA],
            a_rows[0:1, 3 * TtA:4 * TtA], AF.Ln, bias=c_fc[0:1, 0:1], waits=[t_var])
        t_rstd = one("sc", c_s, "activation", a_rows[0:1, 3 * TtA:4 * TtA],
                     a_rows[0:1, 3 * TtA:4 * TtA], AF.Exp, scale=-0.5)

        t_bc = mm_group(5, [
            (PB[5][:, 0:TtA], c_onesr[0:1, :], a_rows[0:1, 0:TtA], True, True, None),
            (PB[5][:, TtA:2 * TtA], c_onesr[0:1, :], a_rows[0:1, 3 * TtA:4 * TtA],
             True, True, None),
        ], waits=[t_rstd])

        if i > 0:
            quiet("ve", "tensor_copy", a_xnb[:, :, 0:1], a_xnb[:, :, TtA:TtA + 1])
        one("ve", c_v, "tensor_tensor", out=a_xnb[:, :, 1:TtA + 1], in0=xin,
            in1=PB[5][:, None, 0:TtA].broadcast_to([128, G, TtA]),
            op=ALU.subtract, waits=[t_bc])
        t_xn = one("ve", c_v, "tensor_tensor", out=a_xnb[:, :, 1:TtA + 1],
                   in0=a_xnb[:, :, 1:TtA + 1],
                   in1=PB[5][:, None, TtA:2 * TtA].broadcast_to([128, G, TtA]),
                   op=ALU.mult)
        read_bank(5, t_xn)
        xnbw = [t_xn]
        if i - 1 in tokA_xst:
            xnbw.append(tokA_xst[i - 1])
        t_xnbf = one("sc", c_s, "activation", a_xnbf, a_xnb[:, :, 1:TtA + 1],
                     AF.Identity, waits=xnbw)
        tokA_xst[i] = ld("sy", q_sxn, [(fmaj(xnbf_d, t0, TtA), a_xnbf)],
                         waits=[t_xnbf])

        t_dl = one("ve", c_v, "tensor_tensor", out=a_dl, in0=a_xnb[:, :, 1:TtA + 1],
                   in1=a_xnb[:, :, 0:TtA], op=ALU.subtract)
        tok_mix = {}
        for qi, (q, _) in enumerate(mixes):
            quiet("ve", "tensor_tensor", out=a_tmp, in0=a_dl,
                  in1=c_tmx[q][:, :, None].broadcast_to([128, G, TtA]),
                  op=ALU.mult)
            vw = []
            if qi == 0 and i - 1 in tokA_kvr:
                vw.append(tokA_kvr[i - 1])
            tok_mix[q] = one("ve", c_v, "tensor_tensor", out=a_xm[par][q],
                             in0=a_tmp, in1=a_xnb[:, :, 0:TtA], op=ALU.add,
                             waits=vw)
        tokA_mixv[i] = tok_mix["r"]
        s1tok[i] = (xin, tok_mix, t_xnbf)

    def emit_A2(i):
        par = i % 2
        t0 = i * TtA
        xin, tok_mix, _ = s1tok.pop(i)
        xm = a_xm[par]

        first_evac = True
        for qi, (q, wsb) in enumerate(mixes):
            for j in range(G // 2):
                bb = 2 * (j % 2)
                for h in range(2):
                    m = 2 * j + h
                    mw = [t_wA, tok_mix[q]] if (j == 0 and h == 0) else []
                    t_mm = mm_group(bb + h,
                                    [(PB[bb + h][:, 0:TtA],
                                      wsb[:, g, m * 128:(m + 1) * 128],
                                      xm[q][:, g, :], g == 0, g == G - 1, None)
                                     for g in range(G)], waits=mw)
                pview = PBPR2[bb // 2][:, :, 0:TtA]
                ew_ = [t_mm]
                if first_evac and i - 1 in tokA_rwkv:
                    ew_.append(tokA_rwkv[i - 1])
                first_evac = False
                if q == "k":
                    tk = one("sc", c_s, "activation", a_ek[:, 2 * j:2 * j + 2, :],
                             pview, AF.Exp, waits=ew_)
                elif q == "v":
                    tk = one("ve", c_v, "tensor_tensor",
                             out=a_ekv[:, 2 * j:2 * j + 2, :],
                             in0=a_ek[:, 2 * j:2 * j + 2, :], in1=pview,
                             op=ALU.mult, waits=ew_ + [evq[("k", j)]])
                else:
                    tk = one("sc", c_s, "activation", a_er[:, 2 * j:2 * j + 2, :],
                             pview, AF.Exp, scale=-1.0, waits=ew_)
                evq[(q, j)] = tk
                read_bank(bb, tk)
                read_bank(bb + 1, tk)
        tokA_kvr[i] = (c_t.sem, c_t.v)
        t_ek_all = (c_s.sem, c_s.v)

        if i > 0:
            quiet("ve", "tensor_copy", a_al[:, :, 0:1], a_al[:, :, TtA:TtA + 1])
            quiet("ve", "tensor_copy", a_be[:, :, 0:1], a_be[:, :, TtA:TtA + 1])
        for g in range(G):
            quiet("ve", "tensor_tensor_scan", a_al[:, g, 1:TtA + 1],
                  c_ew[:, g:g + 1].broadcast_to([128, TtA]), a_ekv[:, g, :],
                  a_al[:, g, 0:1], ALU.mult, ALU.add,
                  waits=([t_ek_all] if g == 0 else ()))
        for g in range(G):
            quiet("ve", "tensor_tensor_scan", a_be[:, g, 1:TtA + 1],
                  c_ew[:, g:g + 1].broadcast_to([128, TtA]), a_ek[:, g, :],
                  a_be[:, g, 0:1], ALU.mult, ALU.add)
        t_scan = seal("ve", c_v)

        one("ve", c_v, "tensor_tensor", out=a_ekv, in0=a_ekv,
            in1=c_eu[:, :, None].broadcast_to([128, G, TtA]), op=ALU.mult)
        one("ve", c_v, "tensor_tensor", out=a_ekv, in0=a_ekv,
            in1=a_al[:, :, 0:TtA], op=ALU.add)
        one("ve", c_v, "tensor_tensor", out=a_ek, in0=a_ek,
            in1=c_eu[:, :, None].broadcast_to([128, G, TtA]), op=ALU.mult)
        one("ve", c_v, "tensor_tensor", out=a_ek, in0=a_ek,
            in1=a_be[:, :, 0:TtA], op=ALU.add)
        one("ve", c_v, "tensor_scalar", out=a_er, in0=a_er, scalar1=1.0,
            scalar2=None, op0=ALU.add)
        one("ve", c_v, "tensor_tensor", out=a_ek, in0=a_ek, in1=a_er, op=ALU.mult)
        one("ve", c_v, "reciprocal", a_ek, a_ek)
        t_rwkv = one("ve", c_v, "tensor_tensor", out=a_rwkv, in0=a_ekv, in1=a_ek,
                     op=ALU.mult)
        tokA_rwkv[i] = t_rwkv

        for j in range(G // 2):
            bb = 2 * (j % 2)
            for h in range(2):
                m = 2 * j + h
                mw = [t_rwkv] if (j == 0 and h == 0) else []
                t_mm = mm_group(bb + h,
                                [(PB[bb + h][:, 0:TtA],
                                  a_wo[:, g, m * 128:(m + 1) * 128],
                                  a_rwkv[:, g, :], g == 0, g == G - 1, None)
                                 for g in range(G)], waits=mw)
            xw2 = [t_mm]
            if j == 0 and i - 1 in tokA_st:
                xw2.append(tokA_st[i - 1])
            tk = one("ve", c_v, "tensor_tensor", out=a_xa[:, 2 * j:2 * j + 2, :],
                     in0=xin[:, 2 * j:2 * j + 2, :], in1=PBPR2[bb // 2][:, :, 0:TtA],
                     op=ALU.add, waits=xw2)
            read_bank(bb, tk)
            read_bank(bb + 1, tk)
        tokA_cons[i] = (c_v.sem, c_v.v)
        tokA_st[i] = ld("sy", q_sxa, [(fmaj(xa_d, t0, TtA), a_xa)],
                        waits=[tokA_cons[i]])

    def emit_router():
        nonlocal t_gather_done
        _, _, t_xnbf = s1tok[0]
        if True:
            t_rt = mm_group(7, [(PB[7][0:1, 0:NE], a_xnbf[:, g, 0:1], c_wr[:, g, :],
                                 g == 0, g == G - 1, None) for g in range(G)],
                            waits=[t_xnbf, t_wA])
            tokA_stat[0] = t_rt
            t_rt = one("ve", c_v, "tensor_tensor", out=c_sco[0:1, 0:NE],
                       in0=PB[7][0:1, 0:NE], in1=c_br[0:1, :], op=ALU.add,
                       waits=[t_rt])
            read_bank(7, t_rt)
            t_sb = mm_group(6, [(PB[6][:, 0:8], c_onesr[0:1, :],
                                 c_sco[0:1, 0:8], True, True, None)],
                            waits=[t_rt])
            one("ve", c_v, "tensor_copy", c_sco[:, 0:8], PB[6][:, 0:8],
                waits=[t_sb])
            read_bank(6, (c_v.sem, c_v.v))
            def sel_level(a_sl, b_sl, out_sl, w):
                quiet("ve", "tensor_tensor", out=c_sco[:, 44:44 + w], in0=a_sl,
                      in1=b_sl, op=ALU.subtract)
                quiet("ve", "tensor_scalar", out=c_sco[:, 48:48 + w],
                      in0=c_sco[:, 44:44 + w], scalar1=0.0, scalar2=None,
                      op0=ALU.is_ge)
                quiet("ve", "tensor_tensor", out=c_sco[:, 52:52 + w], in0=a_sl,
                      in1=c_sco[:, 48:48 + w], op=ALU.mult)
                quiet("ve", "tensor_scalar", out=c_sco[:, 48:48 + w],
                      in0=c_sco[:, 48:48 + w], scalar1=-1.0, scalar2=1.0,
                      op0=ALU.mult, op1=ALU.add)
                quiet("ve", "tensor_tensor", out=c_sco[:, 56:56 + w], in0=b_sl,
                      in1=c_sco[:, 48:48 + w], op=ALU.mult)
                quiet("ve", "tensor_tensor", out=out_sl,
                      in0=c_sco[:, 52:52 + w], in1=c_sco[:, 56:56 + w],
                      op=ALU.add)

            def exact_max8(src0, dst_col):
                sel_level(c_sco[:, src0:src0 + 4], c_sco[:, src0 + 4:src0 + 8],
                          c_sco[:, 60:64], 4)
                sel_level(c_sco[:, 60:62], c_sco[:, 62:64], c_sco[:, 60:62], 2)
                sel_level(c_sco[:, 60:61], c_sco[:, 61:62],
                          c_sco[:, dst_col:dst_col + 1], 1)

            def tree_sum8(src0, dst_col):
                quiet("ve", "tensor_tensor", out=c_sco[:, 60:64],
                      in0=c_sco[:, src0:src0 + 4],
                      in1=c_sco[:, src0 + 4:src0 + 8], op=ALU.add)
                quiet("ve", "tensor_tensor", out=c_sco[:, 44:46],
                      in0=c_sco[:, 60:62], in1=c_sco[:, 62:64], op=ALU.add)
                quiet("ve", "tensor_tensor",
                      out=c_sco[:, dst_col:dst_col + 1],
                      in0=c_sco[:, 44:45], in1=c_sco[:, 45:46], op=ALU.add)

            exact_max8(0, 32)
            one("ve", c_v, "tensor_scalar", out=c_sco[:, 8:16],
                in0=c_sco[:, 0:8], scalar1=c_sco[:, 32:33], scalar2=None,
                op0=ALU.is_ge)
            one("ve", c_v, "tensor_tensor", out=c_sco[:, 16:24],
                in0=c_sco[:, 8:16], in1=c_sco[:, 36:44], op=ALU.mult)
            tree_sum8(16, 34)
            one("ve", c_v, "scalar_tensor_tensor", out=c_sco[:, 16:24],
                in0=c_sco[:, 8:16], scalar=-1e30, in1=c_sco[:, 0:8],
                op0=ALU.mult, op1=ALU.add)
            exact_max8(16, 33)
            one("ve", c_v, "tensor_scalar", out=c_sco[:, 24:32],
                in0=c_sco[:, 16:24], scalar1=c_sco[:, 33:34], scalar2=None,
                op0=ALU.is_ge)
            one("ve", c_v, "tensor_tensor", out=c_sco[:, 16:24],
                in0=c_sco[:, 24:32], in1=c_sco[:, 36:44], op=ALU.mult)
            tree_sum8(16, 35)
            t_e1 = one("ve", c_v, "tensor_copy", c_ebf[:, 0:1], c_sco[:, 34:35])
            if r_stop <= 1:
                t_gather_done = q_g.tok()
                return
            t_mc = mm_group(7, [
                (PB[7][0:NE, 0:1], c_sco[0:1, 8:16], c_onescf[0:1, 0:1],
                 True, True, None),
                (PB[7][0:NE, 1:2], c_sco[0:1, 24:32], c_onescf[0:1, 0:1],
                 True, True, None)], waits=[t_e1])
            t_msl = one("ve", c_v, "tensor_copy", c_msel[0:NE, 0:2],
                        PB[7][0:NE, 0:2], waits=[t_mc])
            read_bank(7, t_msl)
            one("ve", c_v, "tensor_copy", c_ebf[:, 0:1], c_sco[:, 34:35],
                waits=[t_e1])
            one("ve", c_v, "tensor_copy", c_ebf[:, 1:2], c_sco[:, 35:36])
            t_rb = one("ve", c_v, "tensor_copy", c_eb32[:, 0:2], c_ebf[:, 0:2])
            t_gi = t_rb
            for s in range(2):
                t_gi = one("ve", c_v, "tensor_scalar", out=c_gidx[:, s, :],
                           in0=c_eb32[:, s:s + 1].broadcast_to([128, G]),
                           scalar1=1024, scalar2=None, op0=ALU.mult,
                           waits=([t_rb] if s == 0 else ()))
                t_gi = one("ve", c_v, "tensor_tensor", out=c_gidx[:, s, :],
                           in0=c_gidx[:, s, :], in1=c_iota2, op=ALU.add)
            if r_stop <= 2:
                t_gather_done = q_g.tok()
                return
            t_bsel_last = None
            for bsrc, dstb in ((b1, c_b1g), (b2, c_b2g)):
                blw = [t_rt] if t_bsel_last is None else [t_bsel_last]
                t_bld = ld("sy", q_in, [(a_stage[0:NE, :], bsrc)], waits=blw)
                for s in range(2):
                    t_bm2 = mm_group(7, [
                        (PB[7][:, g:g + 1], a_stage[0:NE, g * 128:(g + 1) * 128],
                         c_msel[0:NE, s:s + 1], True, True, None)
                        for g in range(G)], waits=[t_msl, t_bld])
                    t_bv = one("ve", c_v, "tensor_copy", dstb[:, s, :],
                               PB[7][:, 0:G], waits=[t_bm2])
                    read_bank(7, t_bv)
                    t_bsel_last = t_bm2
            if r_stop <= 3:
                ld("gp", q_g, [(outT.rearrange("a b -> a b")[0:128, 0:8], c_gidx[:, 0, :])],
                   waits=[t_gi])
                ld("gp", q_g, [(outT[0:128, 8:16], c_gidx[:, 1, :])])
                ld("gp", q_g, [(outT[0:128, 16:18], c_ebf[:, 0:2])])
                ld("gp", q_g, [(outT[0:8, 20:22], c_msel[0:NE, 0:2])], waits=[t_msl])
                t_gather_done = q_g.tok()
                return
            for s in range(2):
                for (wsrc, wdst) in ((w1, w1g_d), (w2, w2g_d)):
                    for cch in range(G):
                        sb = a_stage
                        R.w("gp", t_gi)
                        R.w("gp", t_bsel_last)
                        R.w("gp", q_g.tok())
                        q_g.nxt()
                        R.i("gp", "indirect_dma_start", out=sb, out_offset=None,
                            in_=wsrc.rearrange("e d f -> (e d) f"),
                            in_offset=IndirectOffsetOnAxis(c_gidx[:, s, cch:cch + 1], 0),
                            inc=(q_g.sem, 16))
                        R.w("gp", q_g.tok())
                        q_g.nxt()
                        R.i("gp", "dma_start",
                            out=wdst[s, cch * 128:(cch + 1) * 128, :], in_=sb,
                            inc=(q_g.sem, 16))
            t_gather_done = q_g.tok()

    emit_A1(0)
    if phases != "An":
        emit_router()
    for i in range(nA):
        if i + 1 < nA:
            emit_A1(i + 1)
        emit_A2(i)

    tA = [(c_v.sem, c_v.v), (c_s.sem, c_s.v), (c_t.sem, c_t.v), q_sxn.tok(),
          q_sxa.tok(), t_gather_done]
    if phases in ("A", "An"):
        if t_gather_done is None:
            tA[5] = tA[3]
        for wt in tA:
            R.w("gp", wt)
        import os as _os
        if _os.environ.get('DUMPXN'):
            ld("gp", q_g, [(outT, xnbf_d)])
            for jj in range(4):
                ld("gp", q_g, [(outT[1020 + jj:1021 + jj, 0:TtA],
                                a_rows[0:1, jj * TtA:(jj + 1) * TtA])])
        else:
            ld("gp", q_g, [(outT, xa_d)])
        R.w("gp", q_g.tok())
        return _finish(nc, R, es)

    B = Alloc(persist_end)
    b_xn = [B.bf(G, TtB), B.bf(G, TtB)]
    b_xa = [B.f32(G, TtB), B.f32(G, TtB)]
    b_h = B.bf(G, TtB)
    b_acc = B.f32(G, TtB)
    b_x1 = [B.f32(G, TtB), B.f32(G, TtB)]
    b_w1g = B.bf(2, G, D)
    b_w2g = B.bf(2, G, D)
    assert B.cur <= nbytes

    t_wB = ld("gp", q_w, [
        (b_w1g[:, 0], w1g_d[0].rearrange("(g p) o -> p g o", p=128)),
        (b_w1g[:, 1], w1g_d[1].rearrange("(g p) o -> p g o", p=128)),
        (b_w2g[:, 0], w2g_d[0].rearrange("(g p) o -> p g o", p=128)),
        (b_w2g[:, 1], w2g_d[1].rearrange("(g p) o -> p g o", p=128)),
    ], waits=tA)
    one("ve", c_v, "tensor_tensor", out=c_bb, in0=c_b2g[:, 0, :],
        in1=c_b2g[:, 1, :], op=ALU.add, waits=tA[:1])
    t_bb = one("ve", c_v, "tensor_scalar", out=c_bb, in0=c_bb, scalar1=0.5,
               scalar2=None, op0=ALU.mult)
    load_act_set([AF.Gelu, AF.Identity])

    tokB_cons = {}
    tokB_st = {}
    t_o_done = None
    for i in range(nB):
        par = i % 2
        t0 = i * TtB
        lw = [t_wB]
        if i - 2 in tokB_cons:
            lw.append(tokB_cons[i - 2])
        t_ld = ld("sy", qx[par], [
            (b_xn[par], fmaj(xnbf_d, t0, TtB)),
            (b_xa[par], fmaj(xa_d, t0, TtB)),
        ], waits=lw)

        if b_stop <= 1:
            tokB_cons[i] = (c_v.sem, c_v.v) if c_v.v else (c_t.sem, c_t.v)
            continue
        for s in range(2):
            hw = [t_ld, t_bb]
            if s == 1:
                hw.append(t_o_done)
            for m in range(G):
                b = m % 2
                t_mm = mm_group(b, [(PB[b][:, 0:TtB],
                                     b_w1g[:, s, g, m * 128:(m + 1) * 128],
                                     b_xn[par][:, g, :], g == 0, g == G - 1, None)
                                    for g in range(G)], waits=(hw if m == 0 else ()))
                tk = one("sc", c_s, "activation", b_h[:, m, :], PB[b][:, 0:TtB],
                         AF.Gelu, bias=c_b1g[:, s, m:m + 1], waits=[t_mm])
                read_bank(b, tk)
            t_h_all = (c_s.sem, c_s.v)
            if b_stop <= 2:
                continue
            for m in range(G):
                b = 2 + m % 2
                t_mm = mm_group(b, [(PB[b][:, 0:TtB],
                                     b_w2g[:, s, g, m * 128:(m + 1) * 128],
                                     b_h[:, g, :], g == 0, g == G - 1, None)
                                    for g in range(G)],
                                waits=([t_h_all] if m == 0 else ()))
                if s == 0:
                    tk = one("ve", c_v, "tensor_copy", b_acc[:, m, :], PB[b][:, 0:TtB],
                             waits=[t_mm])
                else:
                    tv = one("ve", c_v, "tensor_tensor", out=b_acc[:, m, :],
                             in0=b_acc[:, m, :], in1=PB[b][:, 0:TtB], op=ALU.add,
                             waits=[t_mm])
                    ts_ = one("sc", c_s, "activation", b_acc[:, m, :], b_acc[:, m, :],
                              AF.Identity, scale=0.5, bias=c_bb[:, m:m + 1],
                              waits=[tv])
                    tk = one("ve", c_v, "tensor_tensor", out=b_x1[par][:, m, :],
                             in0=b_acc[:, m, :], in1=b_xa[par][:, m, :], op=ALU.add,
                             waits=[ts_])
                read_bank(b, tk)
            if s == 0:
                t_o_done = (c_v.sem, c_v.v)
        tokB_cons[i] = (c_v.sem, c_v.v)
        if b_stop <= 3:
            continue
        stw = [tokB_cons[i]]
        if i - 2 in tokB_st:
            stw.append(tokB_st[i - 2])
        tokB_st[i] = ld("gp", q_sx1, [(fmaj(x1_d, t0, TtB), b_x1[par])], waits=stw)

    tB = [(c_v.sem, c_v.v), (c_s.sem, c_s.v), (c_t.sem, c_t.v), q_sx1.tok()]
    if phases == "AB":
        for wt in tB:
            R.w("gp", wt)
        ld("gp", q_g, [(outT, x1_d)])
        R.w("gp", q_g.tok())
        return _finish(nc, R, es)

    C = Alloc(persist_end)
    cx1 = [C.f32(G, TtC), C.f32(G, TtC)]
    c_xn1 = C.bf(G, TtC)
    c_win = C.bf(G, 1184)
    c_wout = C.bf(G, D)
    c_wq = C.bf(G, D)
    c_out = C.f32(G, TtC)
    c_xssm = C.f32(G, TtC)
    c_xsq2 = C.bf(G, TtC)
    c_Bc = C.bf(TtC)
    c_Cc = C.bf(TtC)
    c_dt = C.f32(TtC)
    c_acum = C.f32(TtC)
    c_dec = C.f32(TtC)
    c_expA = C.f32(TtC)
    c_dtbf = C.bf(TtC)
    c_decbf = C.bf(TtC)
    c_expAbf = C.bf(TtC)
    c_acumT = C.f32(KC, NH)
    c_xdtg = C.bf(TtC)
    c_xddb = C.bf(2, CS)
    c_xdtT = C.bf(KC, D)
    c_xddT = C.bf(KC, D)
    c_BT = C.bf(KC, DS)
    c_prevT = C.bf(KC, D)
    c_seg = C.f32(4, KC * CS)
    c_G4 = C.bf(4, KC * CS)
    c_cbt = C.bf(KC, CS)
    c_ys = C.f32(KC * CS)
    c_ybf = C.bf(G, TtC)
    c_rows2 = c_seg.rearrange("p a b -> p (a b)")
    assert C.cur <= nbytes, f"phase C arena {C.cur} > {nbytes}"

    t_wC = ld("gp", q_w, [
        (c_win, winT.rearrange("(g p) o -> p g o", p=128)),
        (c_wout, woutT.rearrange("(g p) o -> p g o", p=128)),
    ], waits=tB)

    load_act_set([AF.Exp, AF.Ln, AF.Square, AF.Identity])

    tstg0, tstg1 = c_xssm, c_out
    wf_r = wfT.rearrange("(h g p) o -> p (h g) o", p=128, h=2)
    t_tl = ld("sy", q_in, [(tstg0, wf_r[:, 0:4, :]), (tstg1, wf_r[:, 4:8, :])],
              waits=tB)
    flat0 = tstg0.rearrange("p a b -> p (a b)")
    flat1 = tstg1.rearrange("p a b -> p (a b)")
    one("ve", c_v, "reduce_sum", c_ys[:, 0:1], flat0, AX.X,
        apply_absolute_value=True, waits=[t_tl] + tB[:1])
    one("ve", c_v, "reduce_sum", c_ys[:, 1:2], flat1, AX.X,
        apply_absolute_value=True)
    t_r2 = one("ve", c_v, "tensor_tensor", out=c_ys[:, 0:1], in0=c_ys[:, 0:1],
               in1=c_ys[:, 1:2], op=ALU.add)
    t_sc0 = mm_group(6, [(PB[6][0:1, 0:1], c_ys[:, 0:1], c_onescf, True, True,
                          None)], waits=[t_r2, tB[2]])
    t_sc1 = one("ve", c_v, "tensor_scalar", out=c_ys[0:1, 2:3], in0=PB[6][0:1, 0:1],
                scalar1=1.0 / (D * D), scalar2=EPS_TERN, op0=ALU.mult, op1=ALU.max,
                waits=[t_sc0])
    read_bank(6, t_sc1)
    t_sc2 = mm_group(6, [(PB[6][:, 4:5], c_onesr[0:1, :], c_ys[0:1, 2:3],
                          True, True, None)], waits=[t_sc1])
    t_sc3 = one("ve", c_v, "tensor_copy", c_scl[:, 0:1], PB[6][:, 4:5], waits=[t_sc2])
    read_bank(6, t_sc3)
    t_sc4 = one("ve", c_v, "reciprocal", c_scl[:, 1:2], c_scl[:, 0:1])
    segflat = c_seg.rearrange("p a b -> p (a b)")
    ybfflat = c_ybf.rearrange("p a b -> p (a b)").bitcast(F32)
    xn1flat = c_xn1.rearrange("p a b -> p (a b)").bitcast(F32)
    tmpA = ybfflat
    tmpB = xn1flat
    wqf = c_wq.rearrange("p a b -> p (a b)")
    t_q = t_sc4
    for h, flat in ((0, flat0), (1, flat1)):
        one("ve", c_v, "tensor_scalar", out=tmpA[:, 0:2048], in0=flat[:, 0:2048],
            scalar1=c_scl[:, 1:2], scalar2=0.5, op0=ALU.mult, op1=ALU.is_ge,
            waits=[t_q])
        one("ve", c_v, "tensor_scalar", out=tmpB[:, 0:2048], in0=flat[:, 0:2048],
            scalar1=c_scl[:, 1:2], scalar2=-0.5, op0=ALU.mult, op1=ALU.is_le)
        one("ve", c_v, "tensor_tensor", out=tmpA[:, 0:2048], in0=tmpA[:, 0:2048],
            in1=tmpB[:, 0:2048], op=ALU.subtract)
        one("ve", c_v, "tensor_scalar", out=tmpB[:, 0:2048], in0=flat[:, 2048:4096],
            scalar1=c_scl[:, 1:2], scalar2=0.5, op0=ALU.mult, op1=ALU.is_ge)
        one("ve", c_v, "tensor_scalar", out=segflat[:, 0:2048], in0=flat[:, 2048:4096],
            scalar1=c_scl[:, 1:2], scalar2=-0.5, op0=ALU.mult, op1=ALU.is_le)
        one("ve", c_v, "tensor_tensor", out=tmpB[:, 0:2048], in0=tmpB[:, 0:2048],
            in1=segflat[:, 0:2048], op=ALU.subtract)
        one("ve", c_v, "tensor_scalar", out=wqf[:, h * 4096:h * 4096 + 2048],
            in0=tmpA[:, 0:2048], scalar1=c_scl[:, 0:1], scalar2=None, op0=ALU.mult)
        t_q = one("ve", c_v, "tensor_scalar",
                  out=wqf[:, h * 4096 + 2048:(h + 1) * 4096], in0=tmpB[:, 0:2048],
                  scalar1=c_scl[:, 0:1], scalar2=None, op0=ALU.mult)
    t_tern = t_q

    tokC_cons = {}
    tokC_st = {}
    for i in range(nC):
        par = i % 2
        t0 = i * TtC
        lw = [t_tern]
        if i - 2 in tokC_cons:
            lw.append(tokC_cons[i - 2])
        t_ld = ld("sy", qx[par], [(cx1[par], fmaj(x1_d, t0, TtC))], waits=lw)

        scw = [t_ld]
        if i - 1 in tokC_cons:
            scw.append(tokC_cons[i - 1])
        t_cbf = one("sc", c_s, "activation", c_ybf, cx1[par], AF.Identity,
                    waits=scw)
        t_sq = one("sc", c_s, "activation", c_xsq2, cx1[par], AF.Square)
        t_s1 = mm_group(6, [(PB[6][0:1, 0:TtC], c_onescbf, c_ybf[:, g, :],
                             g == 0, g == G - 1, None) for g in range(G)],
                        waits=[t_cbf, t_wC, t_tern])
        t_s2 = mm_group(7, [(PB[7][0:1, 0:TtC], c_onescbf, c_xsq2[:, g, :],
                             g == 0, g == G - 1, None) for g in range(G)],
                        waits=[t_sq])
        t_rw = one("ve", c_v, "tensor_scalar", out=c_rows2[0:1, 0:TtC],
                   in0=PB[6][0:1, 0:TtC], scalar1=1.0 / D, scalar2=None,
                   op0=ALU.mult, waits=[t_s1, t_tern])
        t_rw2 = one("ve", c_v, "tensor_scalar", out=c_rows2[0:1, TtC:2 * TtC],
                    in0=PB[7][0:1, 0:TtC], scalar1=1.0 / D, scalar2=None,
                    op0=ALU.mult, waits=[t_s2])
        t_m2 = one("sc", c_s, "activation", c_rows2[0:1, 2 * TtC:3 * TtC],
                   c_rows2[0:1, 0:TtC], AF.Square, waits=[t_rw])
        t_vr = one("ve", c_v, "tensor_tensor", out=c_rows2[0:1, 3 * TtC:4 * TtC],
                   in0=c_rows2[0:1, TtC:2 * TtC], in1=c_rows2[0:1, 2 * TtC:3 * TtC],
                   op=ALU.subtract, waits=[t_m2, t_rw2])
        one("sc", c_s, "activation", c_rows2[0:1, 3 * TtC:4 * TtC],
            c_rows2[0:1, 3 * TtC:4 * TtC], AF.Ln, bias=c_fc[0:1, 0:1], waits=[t_vr])
        t_rs = one("sc", c_s, "activation", c_rows2[0:1, 3 * TtC:4 * TtC],
                   c_rows2[0:1, 3 * TtC:4 * TtC], AF.Exp, scale=-0.5)
        t_bm = mm_group(6, [(PB[6][:, 0:TtC], c_onesr[0:1, :], c_rows2[0:1, 0:TtC],
                             True, True, None)], waits=[t_rw, t_rs])
        t_br2 = mm_group(7, [(PB[7][:, 0:TtC], c_onesr[0:1, :],
                              c_rows2[0:1, 3 * TtC:4 * TtC], True, True, None)],
                         waits=[t_rw2, t_rs])
        t_xnc = one("ve", c_v, "tensor_tensor", out=c_xssm, in0=cx1[par],
                    in1=PB[6][:, None, 0:TtC].broadcast_to([128, G, TtC]),
                    op=ALU.subtract, waits=[t_bm])
        read_bank(6, t_xnc)
        t_xn1 = one("ve", c_v, "tensor_tensor", out=c_xn1, in0=c_xssm,
                    in1=PB[7][:, None, 0:TtC].broadcast_to([128, G, TtC]),
                    op=ALU.mult, waits=[t_br2])
        read_bank(7, t_xn1)

        for jm in range(G // 2):
            bb = 2 * (jm % 2)
            for h in range(2):
                m = 2 * jm + h
                mw = [t_xn1] if m == 0 else []
                t_mm = mm_group(bb + h, [(PB[bb + h][:, 0:TtC],
                                          c_win[:, g, m * 128:(m + 1) * 128],
                                          c_xn1[:, g, :], g == 0, g == G - 1,
                                          None) for g in range(G)], waits=mw)
            tk = one("ve", c_v, "tensor_copy", c_xssm[:, 2 * jm:2 * jm + 2, :],
                     PBPR2[bb // 2][:, :, 0:TtC], waits=[t_mm, t_xn1])
            read_bank(bb, tk)
            read_bank(bb + 1, tk)
        mm_specs = [("Ba", c_win[:, :, 1024:1088], 64),
                    ("Cb", c_win[:, :, 1088:1152], 64),
                    ("dt", c_win[:, :, 1152:1184], 32)]
        for si, (mk, wsl, mwid) in enumerate(mm_specs):
            b = si % 2
            mw = []
            t_mm = mm_group(b, [(PB[b][0:mwid, 0:TtC], wsl[:, g, :], c_xn1[:, g, :],
                                 g == 0, g == G - 1, None) for g in range(G)],
                            waits=mw)
            if mk == "Ba":
                tk = one("sc", c_s, "activation", c_Bc[0:64, :], PB[b][0:64, 0:TtC],
                         AF.Identity, waits=[t_mm])
            elif mk == "Cb":
                tk = one("sc", c_s, "activation", c_Cc[0:64, :], PB[b][0:64, 0:TtC],
                         AF.Identity, waits=[t_mm])
            else:
                one("sc", c_s, "activation", c_dt[0:NH, :], PB[b][0:NH, 0:TtC],
                    AF.Exp, bias=c_dtb[0:NH, 0:1], waits=[t_mm])
                tk = one("sc", c_s, "activation", c_dt[0:NH, :], c_dt[0:NH, :],
                         AF.Ln, bias=c_onescf[0:NH, 0:1])
            read_bank(b, tk)
        t_dt_done = tk
        t_xs_all = (c_v.sem, c_v.v)

        t_dtbf = one("sc", c_s, "activation", c_dtbf[0:NH, :], c_dt[0:NH, :],
                     AF.Identity, waits=[t_dt_done])
        t_a = one("ve", c_v, "tensor_scalar", out=c_dt[0:NH, :], in0=c_dt[0:NH, :],
                  scalar1=c_A[0:NH, 0:1], scalar2=None, op0=ALU.mult,
                  waits=[t_dtbf])
        for cc in range(KC):
            quiet("ve", "tensor_tensor_scan", c_acum[0:NH, cc * CS:(cc + 1) * CS],
                  c_onescf[0:NH, 0:1].broadcast_to([NH, CS]),
                  c_dt[0:NH, cc * CS:(cc + 1) * CS], 0.0, ALU.mult, ALU.add)
        t_acum = seal("ve", c_v)
        for cc in range(KC):
            quiet("sc", "activation", c_dec[0:NH, cc * CS:(cc + 1) * CS],
                  c_acum[0:NH, cc * CS:(cc + 1) * CS], AF.Exp, scale=-1.0,
                  bias=c_acum[0:NH, (cc + 1) * CS - 1:(cc + 1) * CS],
                  waits=([t_acum] if cc == 0 else ()))
        one("sc", c_s, "activation", c_expA[0:NH, :], c_acum[0:NH, :], AF.Exp)
        one("sc", c_s, "activation", c_expAbf[0:NH, :], c_expA[0:NH, :], AF.Identity)
        t_decs = one("sc", c_s, "activation", c_decbf[0:NH, :], c_dec[0:NH, :],
                     AF.Identity)

        atw = [t_acum]
        if 3 in bank_r:
            atw.append(bank_r.pop(3))
        for wt in atw:
            R.w("te", wt)
        for cc in range(KC):
            quiet("te", "transpose", PB[3][:, cc * NH:(cc + 1) * NH],
                  c_acum[0:NH, cc * CS:(cc + 1) * CS], c_idf[0:NH, 0:NH])
        t_at = seal("te", c_t)
        t_acT = one("ve", c_v, "tensor_copy", c_acumT.rearrange("p a b -> p (a b)"),
                    PB[3][:, 0:KC * NH], waits=[t_at])
        read_bank(3, t_acT)

        tbank = (PBbf[2], PBbf[3])
        tp_r = {0: None, 1: None}
        nt_box = [0]

        def do_transpose_bf(src_ap, dst_ap, p_src, f_src, ident, extra=()):
            half = nt_box[0] % 2
            nt_box[0] += 1
            w = list(extra)
            if tp_r[half] is not None:
                w.append(tp_r[half])
            if (2 + half) in bank_r:
                w.append(bank_r.pop(2 + half))
            for wt in w:
                R.w("te", wt)
            c_t.nxt()
            R.i("te", "transpose", tbank[half][0:f_src, 0:p_src],
                src_ap, ident, inc=(c_t.sem, 1))
            t_tp = c_t.tok()
            tk = one("sc", c_s, "activation", dst_ap,
                     tbank[half][0:f_src, 0:p_src],
                     AF.Identity, waits=[t_tp])
            tp_r[half] = tk
            read_bank(2 + half, tk)
            return tk

        t_prev_g = None
        xdb_tok = {0: None, 1: None}
        for g in range(G):
            bdt = 4 + (g % 2)
            t_rp = mm_group(bdt, [(PB[bdt][:, 0:TtC],
                                   c_repl[0:NH, g * 128:(g + 1) * 128],
                                   c_dtbf[0:NH, :], True, True, None)],
                            waits=([t_dtbf, t_xs_all, t_acT] if g == 0 else ()))
            xw = [t_rp]
            if t_prev_g is not None:
                xw.append(t_prev_g)
            t_xg = one("ve", c_v, "tensor_tensor", out=c_xdtg, in0=c_xssm[:, g, :],
                       in1=PB[bdt][:, 0:TtC], op=ALU.mult, waits=xw)
            read_bank(bdt, t_xg)
            bdc = 6 + (g % 2)
            t_rp2 = mm_group(bdc, [(PB[bdc][:, 0:TtC],
                                    c_repl[0:NH, g * 128:(g + 1) * 128],
                                    c_decbf[0:NH, :], True, True, None)],
                             waits=([t_decs] if g == 0 else ()))
            t_xd = None
            for cc in range(KC):
                t_tr1 = do_transpose_bf(c_xdtg[:, cc * CS:(cc + 1) * CS],
                                        c_xdtT[:, cc, g * 128:(g + 1) * 128],
                                        128, 128, c_idbf, extra=[t_xg])
                xdw = [t_rp2]
                if xdb_tok[cc % 2] is not None:
                    xdw.append(xdb_tok[cc % 2])
                t_xd = one("ve", c_v, "tensor_tensor", out=c_xddb[:, cc % 2, :],
                           in0=c_xdtg[:, cc * CS:(cc + 1) * CS],
                           in1=PB[bdc][:, cc * CS:(cc + 1) * CS], op=ALU.mult,
                           waits=xdw)
                xdb_tok[cc % 2] = do_transpose_bf(
                    c_xddb[:, cc % 2, :], c_xddT[:, cc, g * 128:(g + 1) * 128],
                    128, 128, c_idbf, extra=[t_xd])
            read_bank(bdc, t_xd)
            t_prev_g = xdb_tok[(KC - 1) % 2]
        t_trans_last = None
        for cc in range(KC):
            t_trans_last = do_transpose_bf(
                c_Bc[0:64, cc * CS:(cc + 1) * CS],
                c_BT[:, cc, :], 64, 128, c_idbf[0:64, 0:64], extra=[t_dt_done])
        t_trans_done = t_trans_last

        t_prev_up = None
        t_cb_last = None
        for cc in range(KC):
            t_cb = mm_group(2, [(PB[2][:, 0:CS], c_Bc[0:64, cc * CS:(cc + 1) * CS],
                                 c_Cc[0:64, cc * CS:(cc + 1) * CS], True, True,
                                 None)], waits=([t_trans_done] if cc == 0 else []))
            tk = one("ve", c_v, "tensor_tensor", out=c_cbt[:, cc, :],
                     in0=PB[2][:, 0:CS], in1=c_tri, op=ALU.mult, waits=[t_cb])
            read_bank(2, tk)
            t_cb_last = tk
            pw = [tk] + ([t_prev_up] if t_prev_up else [])
            for wt in pw:
                R.w("te", wt)
            for g in range(G):
                c_t.nxt()
                R.i("te", "transpose", PB[2][0:64, g * 128:(g + 1) * 128]
                    if g < 4 else PB[3][0:64, (g - 4) * 128:(g - 4 + 1) * 128],
                    c_prev[:, g, :], c_idf, inc=(c_t.sem, 1))
            t_pt = c_t.tok()
            tk1 = one("sc", c_s, "activation", c_prevT[0:64, cc, 0:512],
                      PB[2][0:64, 0:512], AF.Identity, waits=[t_pt, t_acT])
            tk2 = one("sc", c_s, "activation", c_prevT[0:64, cc, 512:1024],
                      PB[3][0:64, 0:512], AF.Identity)
            read_bank(2, tk2)
            read_bank(3, tk2)
            t_st_mm = mm_group(6, [(PB[6][:, g * 64:(g + 1) * 64],
                                    c_xddT[:, cc, g * 128:(g + 1) * 128],
                                    c_BT[:, cc, :], True, True, None)
                                   for g in range(G)], waits=[tk2])
            t_cd = mm_group(7, [(PB[7][:, g:g + 1],
                                 c_repl[0:NH, g * 128:(g + 1) * 128],
                                 c_expAbf[0:NH, (cc + 1) * CS - 1:(cc + 1) * CS],
                                 True, True, None) for g in range(G)], waits=[])
            tk = one("ve", c_v, "tensor_tensor", out=c_prev,
                     in0=c_prev,
                     in1=PB[7][:, 0:G][:, :, None].broadcast_to([128, G, DS]),
                     op=ALU.mult, waits=[t_cd, tk2])
            read_bank(7, tk)
            t_prev_up = one("ve", c_v, "tensor_tensor", out=c_prev, in0=c_prev,
                            in1=PB[6][:, 0:G * DS].rearrange("p (a b) -> p a b", a=G),
                            op=ALU.add, waits=[t_st_mm])
            read_bank(6, t_prev_up)

        t_y_done = None
        grp_read = None
        for grp in range(G):
            bl = []
            for hh in range(4):
                h = 4 * grp + hh
                bl.append((PB[4 + hh][:, 0:TtC],
                           c_idf[0:NH, h:h + 1].broadcast_to([NH, 128]),
                           c_acum[0:NH, 0:TtC], True, True, None))
            bw = [t_acum] if grp == 0 else []
            for hh in range(1, 4):
                if 4 + hh in bank_r:
                    bw.append(bank_r.pop(4 + hh))
            t_bch = mm_group(4, bl, waits=bw)
            t_sub = None
            for hh in range(4):
                h = 4 * grp + hh
                t_sub = one("ve", c_v, "tensor_tensor",
                            out=c_seg[:, hh, :].rearrange("p (a b) -> p a b", a=KC),
                            in0=PB[4 + hh][:, 0:TtC].rearrange("p (a b) -> p a b", a=KC),
                            in1=c_acumT[:, :, h:h + 1].broadcast_to([128, KC, CS]),
                            op=ALU.subtract,
                            waits=([t_bch, t_cb_last] if hh == 0 else ()))
            for hh in range(4):
                read_bank(4 + hh, t_sub)
            t_min = one("ve", c_v, "tensor_scalar", out=segflat[:, 0:4 * KC * CS],
                        in0=segflat[:, 0:4 * KC * CS], scalar1=0.0, scalar2=None,
                        op0=ALU.min, waits=[t_sub])
            t_exp = one("sc", c_s, "activation",
                        c_G4.rearrange("p a b -> p (a b)"),
                        segflat[:, 0:4 * KC * CS], AF.Exp, waits=[t_min])
            t_g4 = one("ve", c_v, "tensor_tensor",
                       out=c_G4, in0=c_G4,
                       in1=c_cbt[:, None, :, :].broadcast_to([128, 4, KC, CS]),
                       op=ALU.mult, waits=[t_exp])
            ydl = []
            for cc in range(KC):
                for hh in range(4):
                    h = 4 * grp + hh
                    ydl.append((PB[2][hh * 32:(hh + 1) * 32, cc * CS:(cc + 1) * CS],
                                c_xdtT[:, cc, h * 32:(h + 1) * 32],
                                c_G4[:, hh, cc * CS:(cc + 1) * CS], True, True,
                                (0, 32 * hh)))
            t_yd = mm_group(2, ydl, waits=[t_g4, t_prev_up])
            yol = []
            for cc in range(KC):
                yol.append((PB[3][:, cc * CS:(cc + 1) * CS],
                            c_prevT[0:64, cc, grp * 128:(grp + 1) * 128],
                            c_Cc[0:64, cc * CS:(cc + 1) * CS], True, True, None))
            t_yo = mm_group(3, yol, waits=[])
            t_ea = mm_group(4, [(PB[4][:, cc * CS:(cc + 1) * CS],
                                 c_repl[0:NH, grp * 128:(grp + 1) * 128],
                                 c_expAbf[0:NH, cc * CS:(cc + 1) * CS], True, True,
                                 None) for cc in range(KC)], waits=[t_min])
            tk0 = one("ve", c_v, "tensor_copy", c_ys, PB[3][:, 0:TtC],
                      waits=[t_yo])
            read_bank(3, tk0)
            tk = one("ve", c_v, "tensor_tensor", out=c_ys, in0=c_ys,
                     in1=PB[4][:, 0:TtC], op=ALU.mult, waits=[t_ea])
            read_bank(4, tk)
            tk2 = one("ve", c_v, "tensor_tensor", out=c_ys, in0=c_ys,
                      in1=PB[2][:, 0:TtC], op=ALU.add, waits=[t_yd])
            read_bank(2, tk2)
            t_y_done = one("ve", c_v, "scalar_tensor_tensor", out=c_ybf[:, grp, :],
                           in0=c_xssm[:, grp, :], scalar=c_dexp[:, grp:grp + 1],
                           in1=c_ys, op0=ALU.mult, op1=ALU.add)

        for jm in range(G // 2):
            bb = 2 * (jm % 2)
            for h in range(2):
                m = 2 * jm + h
                mw = [t_y_done] if m == 0 else []
                if m == 0 and i - 1 in tokC_st:
                    mw.append(tokC_st[i - 1])
                mml = [(PB[bb + h][:, 0:TtC], c_wout[:, g, m * 128:(m + 1) * 128],
                        c_ybf[:, g, :], g == 0, False, None) for g in range(G)]
                mml += [(PB[bb + h][:, 0:TtC], c_wq[:, g, m * 128:(m + 1) * 128],
                         c_xn1[:, g, :], False, g == G - 1, None)
                        for g in range(G)]
                t_mm = mm_group(bb + h, mml, waits=mw)
            tk = one("ve", c_v, "tensor_tensor",
                     out=c_out[:, 2 * jm:2 * jm + 2, :],
                     in0=cx1[par][:, 2 * jm:2 * jm + 2, :],
                     in1=PBPR2[bb // 2][:, :, 0:TtC],
                     op=ALU.add, waits=[t_mm])
            read_bank(bb, tk)
            read_bank(bb + 1, tk)
        tokC_cons[i] = (c_v.sem, c_v.v)
        stw = [tokC_cons[i]]
        if i - 1 in tokC_st:
            stw.append(tokC_st[i - 1])
        tokC_st[i] = ld("gp", q_so, [(fmaj(outT, t0, TtC), c_out)], waits=stw)

    R.w("gp", q_so.tok())
    R.w("gp", q_g.tok())
    return _finish(nc, R, es)


def _finish(nc, R, es):
    with nc.Block() as block:
        @block.sync
        def _(sy):
            R.replay("sy", sy)

        @block.tensor
        def _(te):
            R.replay("te", te)

        @block.vector
        def _(ve):
            R.replay("ve", ve)

        @block.scalar
        def _(sc):
            R.replay("sc", sc)

        @block.gpsimd
        def _(gp):
            R.replay("gp", gp)

    es.close()
    return nc


_NCORES = 8
_T = 8192


def _pack_pg(v):
    return np.ascontiguousarray(np.asarray(v, np.float32).reshape(G, 128).T)


def kernel(
    L0_attn_x, L0_attn_time_decay, L0_attn_time_first, L0_attn_time_mix_k,
    L0_attn_time_mix_v, L0_attn_time_mix_r, L0_attn_Wk, L0_attn_Wv, L0_attn_Wr,
    L0_attn_Wo, L0_ffn_Wr, L0_ffn_br, L0_ffn_W1, L0_ffn_b1, L0_ffn_W2, L0_ffn_b2,
    L1_attn_W_in, L1_attn_dt_bias, L1_attn_A_log, L1_attn_D, L1_attn_W_out,
    L1_ffn_weight,
):
    x = np.asarray(L0_attn_x, np.float32)
    Bsz, T, _ = x.shape
    shared = {
        "wkT": np.ascontiguousarray(np.asarray(L0_attn_Wk, np.float32).T),
        "wvT": np.ascontiguousarray(np.asarray(L0_attn_Wv, np.float32).T),
        "wrT": np.ascontiguousarray(np.asarray(L0_attn_Wr, np.float32).T),
        "woT": np.ascontiguousarray(np.asarray(L0_attn_Wo, np.float32).T),
        "tmk": _pack_pg(L0_attn_time_mix_k),
        "tmv": _pack_pg(L0_attn_time_mix_v),
        "tmr": _pack_pg(L0_attn_time_mix_r),
        "tdp": _pack_pg(L0_attn_time_decay),
        "tfp": _pack_pg(L0_attn_time_first),
        "wr_rtr": np.ascontiguousarray(np.asarray(L0_ffn_Wr, np.float32)),
        "br_rtr": np.asarray(L0_ffn_br, np.float32).reshape(1, NE),
        "w1": np.ascontiguousarray(np.asarray(L0_ffn_W1, np.float32)),
        "b1": np.ascontiguousarray(np.asarray(L0_ffn_b1, np.float32)),
        "w2": np.ascontiguousarray(np.asarray(L0_ffn_W2, np.float32)),
        "b2": np.ascontiguousarray(np.asarray(L0_ffn_b2, np.float32)),
        "winT": np.ascontiguousarray(np.asarray(L1_attn_W_in, np.float32).T),
        "dtb": np.asarray(L1_attn_dt_bias, np.float32).reshape(NH, 1),
        "alog": np.asarray(L1_attn_A_log, np.float32).reshape(NH, 1),
        "dexp": _pack_pg(np.repeat(np.asarray(L1_attn_D, np.float32), HD)),
        "woutT": np.ascontiguousarray(np.asarray(L1_attn_W_out, np.float32).T),
        "wfT": np.ascontiguousarray(np.asarray(L1_ffn_weight, np.float32).T),
    }
    in_maps = []
    for b in range(Bsz):
        m = dict(shared)
        m["xT"] = np.ascontiguousarray(x[b].T)
        in_maps.append(m)
    nc = build_program(T)
    res = run_bass_kernel_spmd(nc, in_maps, core_ids=list(range(_NCORES)))
    out = np.stack([np.asarray(r["outT"]).T for r in res.results], axis=0)
    return np.ascontiguousarray(out.astype(np.float32))


# version 09
# speedup vs baseline: 1.1215x; 1.0865x over previous
import numpy as np
import ml_dtypes
import concourse.bass as bass
import concourse.mybir as mybir
from concourse.bass_utils import run_bass_kernel_spmd
from concourse.bass_core_rust import IndirectOffsetOnAxis
from contextlib import ExitStack

F32 = mybir.dt.float32
BF16 = mybir.dt.bfloat16
I32 = mybir.dt.int32
U32 = mybir.dt.uint32
U8 = mybir.dt.uint8
AF = mybir.ActivationFunctionType
ALU = mybir.AluOpType
AX = mybir.AxisListType
BFNP = ml_dtypes.bfloat16

D = 1024
G = 8
NE = 8
NH, HD, DS = 32, 32, 64
CS = 128
EPS_LN = 1e-5
EPS_TERN = 1e-5

ENGS = ("sy", "te", "ve", "sc", "gp")


class Rec:
    def __init__(self):
        self.q = {e: [] for e in ENGS}

    def w(self, eng, tok):
        if tok is not None:
            sem, val = tok
            if val > 0:
                self.q[eng].append(("wait_ge", (sem, val), {}, None))

    def i(self, eng, method, *args, inc=None, **kw):
        self.q[eng].append((method, args, kw, inc))

    def replay(self, name, proxy):
        for method, args, kw, inc in self.q[name]:
            if method == "__act_table__":
                r = proxy.add_instruction(mybir.InstLoadActFuncSet(
                    name=proxy.bass.get_next_instruction_name(),
                    act_func_set_id=args[0], ins=[], outs=[]))
            else:
                r = getattr(proxy, method)(*args, **kw)
            if inc is not None:
                r.then_inc(inc[0], inc[1])


class Ctr:
    def __init__(self, sm, step=1):
        self.sem, self.v, self.step = sm, 0, step

    def nxt(self):
        self.v += self.step
        return (self.sem, self.v)

    def tok(self):
        return (self.sem, self.v)


def build_program(T, TtA=256, TtB=512, TtC=512, phases="ABC", a_stop=10**9, b_stop=10**9, r_stop=10**9, rcheck=False):
    nc = bass.Bass("TRN2", target_bir_lowering=False, detect_race_conditions=rcheck)
    es = ExitStack()
    nA, nB, nC = T // TtA, T // TtB, T // TtC
    KC = TtC // CS

    def din(name, shape, dtype=F32):
        return nc.dram_tensor(name, list(shape), dtype, kind="ExternalInput").ap()

    xT = din("xT", [D, T])
    wkT = din("wkT", [D, D]); wvT = din("wvT", [D, D])
    wrT = din("wrT", [D, D]); woT = din("woT", [D, D])
    tmk = din("tmk", [128, G]); tmv = din("tmv", [128, G]); tmr = din("tmr", [128, G])
    tdp = din("tdp", [128, G]); tfp = din("tfp", [128, G])
    wr_rtr = din("wr_rtr", [D, NE]); br_rtr = din("br_rtr", [1, NE])
    w1 = din("w1", [NE, D, D]); b1 = din("b1", [NE, D])
    w2 = din("w2", [NE, D, D]); b2 = din("b2", [NE, D])
    winT = din("winT", [D, 1184])
    dtb = din("dtb", [NH, 1]); alog = din("alog", [NH, 1])
    dexp = din("dexp", [128, G])
    woutT = din("woutT", [D, D])
    wfT = din("wfT", [D, D])
    outT = nc.dram_tensor("outT", [D, T], F32, kind="ExternalOutput").ap()

    xnbf_d = nc.dram_tensor("xnbf_d", [D, T], BF16, kind="Internal").ap()
    xa_d = nc.dram_tensor("xa_d", [D, T], F32, kind="Internal").ap()
    x1_d = nc.dram_tensor("x1_d", [D, T], F32, kind="Internal").ap()
    w1g_d = nc.dram_tensor("w1g_d", [2, D, D], BF16, kind="Internal").ap()
    w2g_d = nc.dram_tensor("w2g_d", [2, D, D], BF16, kind="Internal").ap()
    bg_d = nc.dram_tensor("bg_d", [4, D], F32, kind="Internal").ap()
    idx_d = nc.dram_tensor("idx_d", [1, 2], U32, kind="Internal").ap()
    acum_d = nc.dram_tensor("acum_d", [NH, 512], F32, kind="Internal").ap()

    ident_bf_c = nc.inline_tensor(np.eye(128, dtype=BFNP), "ident_bf_c")
    ident_f_c = nc.inline_tensor(np.eye(128, dtype=np.float32), "ident_f_c")
    ones_row_f_c = nc.inline_tensor(np.ones((1, 128), np.float32), "ones_row_f_c")
    ones_col_bf_c = nc.inline_tensor(np.ones((128, 1), BFNP), "ones_col_bf_c")
    repl_np = np.zeros((NH, G * 128), np.float32)
    for g in range(G):
        for m in range(128):
            repl_np[4 * g + m // 32, g * 128 + m] = 1.0
    repl_bf_c = nc.inline_tensor(repl_np.astype(BFNP), "repl_bf_c")
    tri_c = nc.inline_tensor(
        (np.arange(CS)[:, None] <= np.arange(CS)[None, :]).astype(np.float32), "tri_c")
    iota2d_c = nc.inline_tensor(
        (np.arange(128)[:, None] + 128 * np.arange(G)[None, :]).astype(np.int32),
        "iota2d_c")
    iota8_c = nc.inline_tensor(
        np.tile(np.arange(8, dtype=np.float32), (128, 1)), "iota8_c")
    selh_np = np.zeros((NH, NH * 128), np.float32)
    for hh_ in range(NH):
        selh_np[hh_, hh_ * 128:(hh_ + 1) * 128] = 1.0
    selh_c = nc.inline_tensor(selh_np, "selh_c")

    nbytes = nc.sbuf_bytes_remaining - 2048
    nbytes -= nbytes % 256
    arena = nc.alloc_sbuf_tensor("arena", [128, nbytes], U8)
    AFV = {F32: arena.bitcast(F32), BF16: arena.bitcast(BF16),
           I32: arena.bitcast(I32), U32: arena.bitcast(U32)}
    DSZ = {F32: 4, BF16: 2, I32: 4, U32: 4}

    class Alloc:
        def __init__(self, base):
            self.cur = base

        def mk(self, dt_, *shape):
            n = int(np.prod(shape))
            sz = DSZ[dt_]
            off = (self.cur + 63) // 64 * 64
            self.cur = off + n * sz
            assert self.cur <= nbytes, f"sbuf arena overflow: {self.cur} > {nbytes}"
            v = AFV[dt_][:, off // sz: off // sz + n]
            if len(shape) == 3:
                return v.rearrange("p (a b c) -> p a b c", a=shape[0], b=shape[1])
            if len(shape) == 2:
                return v.rearrange("p (a b) -> p a b", a=shape[0])
            return v

        def f32(self, *s):
            return self.mk(F32, *s)

        def bf(self, *s):
            return self.mk(BF16, *s)

        def i32(self, *s):
            return self.mk(I32, *s)

        def view(self, dt_, off, *shape):
            n = int(np.prod(shape))
            sz = DSZ[dt_]
            v = AFV[dt_][:, off // sz: off // sz + n]
            if len(shape) == 3:
                return v.rearrange("p (a b c) -> p a b c", a=shape[0], b=shape[1])
            if len(shape) == 2:
                return v.rearrange("p (a b) -> p a b", a=shape[0])
            return v

    P = Alloc(0)
    c_tmx = {"k": P.f32(G), "v": P.f32(G), "r": P.f32(G)}
    c_ew = P.f32(G); c_eu = P.f32(G); c_dexp = P.f32(G)
    c_A = P.f32(1); c_dtb = P.f32(1)
    c_idbf = P.bf(128); c_idf = P.f32(128)
    c_onesr = P.f32(128)
    c_onescbf = P.bf(1)
    c_onescf = P.f32(1)
    c_repl = P.bf(G * 128)
    c_tri = P.f32(CS)
    c_iota2 = P.i32(G)
    c_wr = P.bf(G, NE)
    c_br = P.f32(NE)
    c_sco = P.f32(64)
    c_idxu = P.mk(U32, 8)
    c_eb32 = P.i32(2)
    c_gidx = P.i32(2, G)
    c_b1g = P.f32(2, G); c_b2g = P.f32(2, G); c_bb = P.f32(G)
    c_scl = P.f32(2)
    c_prev = P.f32(G, DS)
    c_tmpg = P.f32(G)
    c_fc = P.f32(2)
    c_msel = P.f32(2)
    c_ebf = P.f32(2)
    c_iotf = P.f32(1)
    persist_end = P.cur

    ps = nc.alloc_psum_tensor("ps", [128, 4096], F32)
    psa = ps.ap()
    PB = [psa[:, 512 * b: 512 * (b + 1)] for b in range(8)]
    ps_bf = ps.bitcast(BF16)
    PBbf = [ps_bf[:, 1024 * b: 1024 * (b + 1)] for b in range(8)]
    PBPR2 = [psa[:, 1024 * p: 1024 * (p + 1)].rearrange(
        "p (a c) -> p a c", a=2) for p in range(2)]

    def mksem(name):
        return es.enter_context(nc.semaphore(name))

    q_in = Ctr(mksem("q_in"), 16)
    q_w = Ctr(mksem("q_w"), 16)
    qx = [Ctr(mksem("qx0"), 16), Ctr(mksem("qx1"), 16)]
    q_sxn = Ctr(mksem("q_sxn"), 16)
    q_sxa = Ctr(mksem("q_sxa"), 16)
    q_sx1 = Ctr(mksem("q_sx1"), 16)
    q_so = Ctr(mksem("q_so"), 16)
    q_g = Ctr(mksem("q_g"), 16)
    q_bc = Ctr(mksem("q_bc"), 16)
    c_t = Ctr(mksem("c_t"))
    c_v = Ctr(mksem("c_v"))
    c_s = Ctr(mksem("c_s"))
    q_st = q_sxn

    R = Rec()

    def ld(qeng, ctr, pairs, waits=(), **kw):
        for wt in waits:
            R.w(qeng, wt)
        for out, in_ in pairs:
            ctr.nxt()
            R.i(qeng, "dma_start", out=out, in_=in_, inc=(ctr.sem, 16), **kw)
        return ctr.tok()

    CTRS = {"ve": c_v, "sc": c_s, "te": c_t}

    def one(eng, ctr, method, *args, waits=(), **kw):
        if eng in ("ve", "sc") and ctr.v > 0:
            R.w(eng, ctr.tok())
        for wt in waits:
            R.w(eng, wt)
        ctr.nxt()
        R.i(eng, method, *args, inc=(ctr.sem, 1), **kw)
        return ctr.tok()

    def quiet(eng, method, *args, waits=(), **kw):
        if eng in ("ve", "sc"):
            return one(eng, CTRS[eng], method, *args, waits=waits, **kw)
        for wt in waits:
            R.w(eng, wt)
        R.i(eng, method, *args, **kw)

    def seal(eng, ctr):
        if eng in ("ve", "sc"):
            return ctr.tok()
        m, a, k, inc = R.q[eng][-1]
        assert inc is None and m != "wait_ge", (m, inc)
        ctr.nxt()
        R.q[eng][-1] = (m, a, k, (ctr.sem, 1))
        return ctr.tok()

    def fmaj(ap_, t0, tt):
        return ap_.rearrange("(g p) t -> p g t", p=128)[:, :, t0:t0 + tt]

    from concourse.hw_specs import get_activation_tables
    _act_sets = list(get_activation_tables(nc.m.arch).values())

    def load_act_set(funcs):
        for idx, fs in enumerate(_act_sets):
            if all(f in fs for f in funcs):
                if c_s.v > 0:
                    R.w("sc", c_s.tok())
                c_s.nxt()
                R.i("sc", "__act_table__", idx, inc=(c_s.sem, 1))
                return c_s.tok()
        raise ValueError(f"no act table set contains {funcs}")

    bank_r = {}

    def mm_group(b, mm_list, waits=()):
        w = list(waits)
        if b in bank_r:
            w.append(bank_r.pop(b))
        first = True
        for (out, lhsT, rhs, st, sp, tp) in mm_list:
            kw = {"start": st, "stop": sp}
            if tp is not None:
                kw["tile_position"] = tp
            quiet("te", "matmul", out, lhsT, rhs, waits=(w if first else ()), **kw)
            first = False
        return seal("te", c_t)

    def read_bank(b, tok):
        bank_r[b] = tok

    t_pro = ld("sy", q_in, [
        (c_tmx["k"], tmk), (c_tmx["v"], tmv), (c_tmx["r"], tmr),
        (c_dexp, dexp),
        (c_idbf, ident_bf_c.ap()), (c_idf, ident_f_c.ap()),
        (c_onesr[0:1, :], ones_row_f_c.ap()),
        (c_onescbf, ones_col_bf_c.ap()),
        (c_repl[0:NH, :], repl_bf_c.ap()),
        (c_tri, tri_c.ap()),
        (c_iota2, iota2d_c.ap()),
        (c_br[0:1, :], br_rtr),
        (c_sco[:, 36:44], iota8_c.ap()),
        (c_A[0:NH, :], alog),
        (c_dtb[0:NH, :], dtb),
        (c_tmpg, tdp),
    ])
    load_act_set([AF.Exp, AF.Ln, AF.Square, AF.Identity])
    one("sc", c_s, "activation", c_ew, c_tmpg, AF.Exp, waits=[t_pro])
    one("sc", c_s, "activation", c_ew, c_ew, AF.Exp, scale=-1.0)
    t_sA = one("sc", c_s, "activation", c_A[0:NH, :], c_A[0:NH, :], AF.Exp)
    t_tf = ld("sy", q_in, [(c_tmpg, tfp)], waits=[t_sA])
    t_pro_sc = one("sc", c_s, "activation", c_eu, c_tmpg, AF.Exp, waits=[t_tf])
    one("ve", c_v, "tensor_scalar_mul", c_A[0:NH, :], c_A[0:NH, :], -1.0,
        waits=[t_sA])
    one("ve", c_v, "memset", c_onescf, 1.0, waits=[t_pro])
    t_pro_ve = one("ve", c_v, "memset", c_fc[:, 0:1], EPS_LN)

    if phases == "pro":
        R.w("sy", q_in.tok())
        R.w("sc", (c_s.sem, c_s.v))
        R.w("ve", (c_v.sem, c_v.v))
        return _finish(nc, R, es)

    A = Alloc(persist_end)
    a_x = [A.f32(G, TtA), A.f32(G, TtA)]
    a_xnb = A.f32(G, TtA + 1)
    a_al = A.f32(G, TtA + 1)
    a_be = A.f32(G, TtA + 1)
    off_dl = (A.cur + 63) // 64 * 64
    a_dl = A.f32(G, TtA)
    a_tmp = A.f32(G, TtA)
    a_ek = A.f32(G, TtA)
    a_ekv = A.f32(G, TtA)
    a_er = A.f32(G, TtA)
    a_xsq = A.view(BF16, off_dl, G, TtA)
    a_xnbf = A.bf(G, TtA)
    a_xbf = a_xnbf
    a_xm = [{q: A.bf(G, TtA) for q in ("k", "v", "r")} for _ in range(2)]
    a_rwkv = A.bf(G, TtA)
    a_xa = A.f32(G, TtA)
    a_rows = A.f32(4 * TtA)
    a_wk = A.bf(G, D); a_wv = A.bf(G, D); a_wrr = A.bf(G, D); a_wo = A.bf(G, D)
    S = Alloc(nbytes - 4 * D - 192)
    a_stage = S.f32(D)
    assert A.cur <= nbytes - 4 * D - 192, (A.cur, nbytes - 4 * D - 192)

    t_wA = ld("gp", q_w, [
        (c_wr, wr_rtr.rearrange("(g p) e -> p g e", p=128)),
        (a_wk, wkT.rearrange("(g p) o -> p g o", p=128)),
        (a_wv, wvT.rearrange("(g p) o -> p g o", p=128)),
        (a_wrr, wrT.rearrange("(g p) o -> p g o", p=128)),
        (a_wo, woT.rearrange("(g p) o -> p g o", p=128)),
    ])

    one("ve", c_v, "memset", a_xnb[:, :, 0:1], 0.0)
    one("ve", c_v, "memset", a_al[:, :, 0:1], 0.0)
    one("ve", c_v, "memset", a_be[:, :, 0:1], 0.0)
    one("ve", c_v, "memset", c_prev, 0.0)

    tokA_cons = {}
    tokA_st = {}
    tokA_xst = {}
    tokA_stat = {}
    tokA_kvr = {}
    tokA_mixv = {}
    tokA_rwkv = {}
    evq = {}
    s1tok = {}
    t_gather_done = None
    mixes = (("k", a_wk), ("v", a_wv), ("r", a_wrr))

    def emit_A1(i):
        par = i % 2
        t0 = i * TtA
        xin = a_x[par]

        lw = [t_pro]
        if i - 2 in tokA_cons:
            lw.append(tokA_cons[i - 2])
        t_ld = ld("sy", qx[par], [(xin, fmaj(xT, t0, TtA))], waits=lw)

        cw = [t_ld, t_pro_sc]
        if i - 1 in tokA_stat:
            cw.append(tokA_stat[i - 1])
        if i - 1 in tokA_mixv:
            cw.append(tokA_mixv[i - 1])
        if i - 1 in tokA_xst:
            cw.append(tokA_xst[i - 1])
        one("sc", c_s, "activation", a_xbf, xin, AF.Identity, waits=cw)
        t_c2 = one("sc", c_s, "activation", a_xsq, xin, AF.Square)

        t_stat = mm_group(6,
                          [(PB[6][0:1, 0:TtA], c_onescbf, a_xbf[:, g, :],
                            g == 0, g == G - 1, None) for g in range(G)] +
                          [(PB[6][0:1, TtA:2 * TtA], c_onescbf, a_xsq[:, g, :],
                            g == 0, g == G - 1, None) for g in range(G)],
                          waits=[t_c2])
        tokA_stat[i] = t_stat

        t_rows = one("ve", c_v, "tensor_scalar", out=a_rows[0:1, 0:2 * TtA],
                     in0=PB[6][0:1, 0:2 * TtA], scalar1=1.0 / D, scalar2=None,
                     op0=ALU.mult, waits=[t_stat, t_pro_ve])
        read_bank(6, t_rows)
        t_m2 = one("sc", c_s, "activation", a_rows[0:1, 2 * TtA:3 * TtA],
                   a_rows[0:1, 0:TtA], AF.Square, waits=[t_rows])
        t_var = one("ve", c_v, "tensor_tensor", out=a_rows[0:1, 3 * TtA:4 * TtA],
                    in0=a_rows[0:1, TtA:2 * TtA], in1=a_rows[0:1, 2 * TtA:3 * TtA],
                    op=ALU.subtract, waits=[t_m2])
        one("sc", c_s, "activation", a_rows[0:1, 3 * TtA:4 * TtA],
            a_rows[0:1, 3 * TtA:4 * TtA], AF.Ln, bias=c_fc[0:1, 0:1], waits=[t_var])
        t_rstd = one("sc", c_s, "activation", a_rows[0:1, 3 * TtA:4 * TtA],
                     a_rows[0:1, 3 * TtA:4 * TtA], AF.Exp, scale=-0.5)

        t_bc = mm_group(5, [
            (PB[5][:, 0:TtA], c_onesr[0:1, :], a_rows[0:1, 0:TtA], True, True, None),
            (PB[5][:, TtA:2 * TtA], c_onesr[0:1, :], a_rows[0:1, 3 * TtA:4 * TtA],
             True, True, None),
        ], waits=[t_rstd])

        if i > 0:
            quiet("ve", "tensor_copy", a_xnb[:, :, 0:1], a_xnb[:, :, TtA:TtA + 1])
        one("ve", c_v, "tensor_tensor", out=a_xnb[:, :, 1:TtA + 1], in0=xin,
            in1=PB[5][:, None, 0:TtA].broadcast_to([128, G, TtA]),
            op=ALU.subtract, waits=[t_bc])
        t_xn = one("ve", c_v, "tensor_tensor", out=a_xnb[:, :, 1:TtA + 1],
                   in0=a_xnb[:, :, 1:TtA + 1],
                   in1=PB[5][:, None, TtA:2 * TtA].broadcast_to([128, G, TtA]),
                   op=ALU.mult)
        read_bank(5, t_xn)
        xnbw = [t_xn]
        if i - 1 in tokA_xst:
            xnbw.append(tokA_xst[i - 1])
        t_xnbf = one("sc", c_s, "activation", a_xnbf, a_xnb[:, :, 1:TtA + 1],
                     AF.Identity, waits=xnbw)
        tokA_xst[i] = ld("sy", q_sxn, [(fmaj(xnbf_d, t0, TtA), a_xnbf)],
                         waits=[t_xnbf])

        t_dl = one("ve", c_v, "tensor_tensor", out=a_dl, in0=a_xnb[:, :, 1:TtA + 1],
                   in1=a_xnb[:, :, 0:TtA], op=ALU.subtract)
        tok_mix = {}
        for qi, (q, _) in enumerate(mixes):
            quiet("ve", "tensor_tensor", out=a_tmp, in0=a_dl,
                  in1=c_tmx[q][:, :, None].broadcast_to([128, G, TtA]),
                  op=ALU.mult)
            vw = []
            if qi == 0 and i - 1 in tokA_kvr:
                vw.append(tokA_kvr[i - 1])
            tok_mix[q] = one("ve", c_v, "tensor_tensor", out=a_xm[par][q],
                             in0=a_tmp, in1=a_xnb[:, :, 0:TtA], op=ALU.add,
                             waits=vw)
        tokA_mixv[i] = tok_mix["r"]
        s1tok[i] = (xin, tok_mix, t_xnbf)

    def emit_A2(i):
        par = i % 2
        t0 = i * TtA
        xin, tok_mix, _ = s1tok.pop(i)
        xm = a_xm[par]

        first_evac = True
        for qi, (q, wsb) in enumerate(mixes):
            for j in range(G // 2):
                bb = 2 * (j % 2)
                for h in range(2):
                    m = 2 * j + h
                    mw = [t_wA, tok_mix[q]] if (j == 0 and h == 0) else []
                    t_mm = mm_group(bb + h,
                                    [(PB[bb + h][:, 0:TtA],
                                      wsb[:, g, m * 128:(m + 1) * 128],
                                      xm[q][:, g, :], g == 0, g == G - 1, None)
                                     for g in range(G)], waits=mw)
                pview = PBPR2[bb // 2][:, :, 0:TtA]
                ew_ = [t_mm]
                if first_evac and i - 1 in tokA_rwkv:
                    ew_.append(tokA_rwkv[i - 1])
                first_evac = False
                if q == "k":
                    tk = one("sc", c_s, "activation", a_ek[:, 2 * j:2 * j + 2, :],
                             pview, AF.Exp, waits=ew_)
                elif q == "v":
                    tk = one("ve", c_v, "tensor_tensor",
                             out=a_ekv[:, 2 * j:2 * j + 2, :],
                             in0=a_ek[:, 2 * j:2 * j + 2, :], in1=pview,
                             op=ALU.mult, waits=ew_ + [evq[("k", j)]])
                else:
                    tk = one("sc", c_s, "activation", a_er[:, 2 * j:2 * j + 2, :],
                             pview, AF.Exp, scale=-1.0, waits=ew_)
                evq[(q, j)] = tk
                read_bank(bb, tk)
                read_bank(bb + 1, tk)
        tokA_kvr[i] = (c_t.sem, c_t.v)
        t_ek_all = (c_s.sem, c_s.v)

        if i > 0:
            quiet("ve", "tensor_copy", a_al[:, :, 0:1], a_al[:, :, TtA:TtA + 1])
            quiet("ve", "tensor_copy", a_be[:, :, 0:1], a_be[:, :, TtA:TtA + 1])
        for g in range(G):
            quiet("ve", "tensor_tensor_scan", a_al[:, g, 1:TtA + 1],
                  c_ew[:, g:g + 1].broadcast_to([128, TtA]), a_ekv[:, g, :],
                  a_al[:, g, 0:1], ALU.mult, ALU.add,
                  waits=([t_ek_all] if g == 0 else ()))
        for g in range(G):
            quiet("ve", "tensor_tensor_scan", a_be[:, g, 1:TtA + 1],
                  c_ew[:, g:g + 1].broadcast_to([128, TtA]), a_ek[:, g, :],
                  a_be[:, g, 0:1], ALU.mult, ALU.add)
        t_scan = seal("ve", c_v)

        one("ve", c_v, "tensor_tensor", out=a_ekv, in0=a_ekv,
            in1=c_eu[:, :, None].broadcast_to([128, G, TtA]), op=ALU.mult)
        one("ve", c_v, "tensor_tensor", out=a_ekv, in0=a_ekv,
            in1=a_al[:, :, 0:TtA], op=ALU.add)
        one("ve", c_v, "tensor_tensor", out=a_ek, in0=a_ek,
            in1=c_eu[:, :, None].broadcast_to([128, G, TtA]), op=ALU.mult)
        one("ve", c_v, "tensor_tensor", out=a_ek, in0=a_ek,
            in1=a_be[:, :, 0:TtA], op=ALU.add)
        one("ve", c_v, "tensor_scalar", out=a_er, in0=a_er, scalar1=1.0,
            scalar2=None, op0=ALU.add)
        one("ve", c_v, "tensor_tensor", out=a_ek, in0=a_ek, in1=a_er, op=ALU.mult)
        one("ve", c_v, "reciprocal", a_ek, a_ek)
        t_rwkv = one("ve", c_v, "tensor_tensor", out=a_rwkv, in0=a_ekv, in1=a_ek,
                     op=ALU.mult)
        tokA_rwkv[i] = t_rwkv

        for j in range(G // 2):
            bb = 2 * (j % 2)
            for h in range(2):
                m = 2 * j + h
                mw = [t_rwkv] if (j == 0 and h == 0) else []
                t_mm = mm_group(bb + h,
                                [(PB[bb + h][:, 0:TtA],
                                  a_wo[:, g, m * 128:(m + 1) * 128],
                                  a_rwkv[:, g, :], g == 0, g == G - 1, None)
                                 for g in range(G)], waits=mw)
            xw2 = [t_mm]
            if j == 0 and i - 1 in tokA_st:
                xw2.append(tokA_st[i - 1])
            tk = one("ve", c_v, "tensor_tensor", out=a_xa[:, 2 * j:2 * j + 2, :],
                     in0=xin[:, 2 * j:2 * j + 2, :], in1=PBPR2[bb // 2][:, :, 0:TtA],
                     op=ALU.add, waits=xw2)
            read_bank(bb, tk)
            read_bank(bb + 1, tk)
        tokA_cons[i] = (c_v.sem, c_v.v)
        tokA_st[i] = ld("sy", q_sxa, [(fmaj(xa_d, t0, TtA), a_xa)],
                        waits=[tokA_cons[i]])

    def emit_router():
        nonlocal t_gather_done
        _, _, t_xnbf = s1tok[0]
        if True:
            t_rt = mm_group(7, [(PB[7][0:1, 0:NE], a_xnbf[:, g, 0:1], c_wr[:, g, :],
                                 g == 0, g == G - 1, None) for g in range(G)],
                            waits=[t_xnbf, t_wA])
            tokA_stat[0] = t_rt
            t_rt = one("ve", c_v, "tensor_tensor", out=c_sco[0:1, 0:NE],
                       in0=PB[7][0:1, 0:NE], in1=c_br[0:1, :], op=ALU.add,
                       waits=[t_rt])
            read_bank(7, t_rt)
            t_sb = mm_group(6, [(PB[6][:, 0:8], c_onesr[0:1, :],
                                 c_sco[0:1, 0:8], True, True, None)],
                            waits=[t_rt])
            one("ve", c_v, "tensor_copy", c_sco[:, 0:8], PB[6][:, 0:8],
                waits=[t_sb])
            read_bank(6, (c_v.sem, c_v.v))
            def sel_level(a_sl, b_sl, out_sl, w):
                quiet("ve", "tensor_tensor", out=c_sco[:, 44:44 + w], in0=a_sl,
                      in1=b_sl, op=ALU.subtract)
                quiet("ve", "tensor_scalar", out=c_sco[:, 48:48 + w],
                      in0=c_sco[:, 44:44 + w], scalar1=0.0, scalar2=None,
                      op0=ALU.is_ge)
                quiet("ve", "tensor_tensor", out=c_sco[:, 52:52 + w], in0=a_sl,
                      in1=c_sco[:, 48:48 + w], op=ALU.mult)
                quiet("ve", "tensor_scalar", out=c_sco[:, 48:48 + w],
                      in0=c_sco[:, 48:48 + w], scalar1=-1.0, scalar2=1.0,
                      op0=ALU.mult, op1=ALU.add)
                quiet("ve", "tensor_tensor", out=c_sco[:, 56:56 + w], in0=b_sl,
                      in1=c_sco[:, 48:48 + w], op=ALU.mult)
                quiet("ve", "tensor_tensor", out=out_sl,
                      in0=c_sco[:, 52:52 + w], in1=c_sco[:, 56:56 + w],
                      op=ALU.add)

            def exact_max8(src0, dst_col):
                sel_level(c_sco[:, src0:src0 + 4], c_sco[:, src0 + 4:src0 + 8],
                          c_sco[:, 60:64], 4)
                sel_level(c_sco[:, 60:62], c_sco[:, 62:64], c_sco[:, 60:62], 2)
                sel_level(c_sco[:, 60:61], c_sco[:, 61:62],
                          c_sco[:, dst_col:dst_col + 1], 1)

            def tree_sum8(src0, dst_col):
                quiet("ve", "tensor_tensor", out=c_sco[:, 60:64],
                      in0=c_sco[:, src0:src0 + 4],
                      in1=c_sco[:, src0 + 4:src0 + 8], op=ALU.add)
                quiet("ve", "tensor_tensor", out=c_sco[:, 44:46],
                      in0=c_sco[:, 60:62], in1=c_sco[:, 62:64], op=ALU.add)
                quiet("ve", "tensor_tensor",
                      out=c_sco[:, dst_col:dst_col + 1],
                      in0=c_sco[:, 44:45], in1=c_sco[:, 45:46], op=ALU.add)

            exact_max8(0, 32)
            one("ve", c_v, "tensor_scalar", out=c_sco[:, 8:16],
                in0=c_sco[:, 0:8], scalar1=c_sco[:, 32:33], scalar2=None,
                op0=ALU.is_ge)
            one("ve", c_v, "tensor_tensor", out=c_sco[:, 16:24],
                in0=c_sco[:, 8:16], in1=c_sco[:, 36:44], op=ALU.mult)
            tree_sum8(16, 34)
            one("ve", c_v, "scalar_tensor_tensor", out=c_sco[:, 16:24],
                in0=c_sco[:, 8:16], scalar=-1e30, in1=c_sco[:, 0:8],
                op0=ALU.mult, op1=ALU.add)
            exact_max8(16, 33)
            one("ve", c_v, "tensor_scalar", out=c_sco[:, 24:32],
                in0=c_sco[:, 16:24], scalar1=c_sco[:, 33:34], scalar2=None,
                op0=ALU.is_ge)
            one("ve", c_v, "tensor_tensor", out=c_sco[:, 16:24],
                in0=c_sco[:, 24:32], in1=c_sco[:, 36:44], op=ALU.mult)
            tree_sum8(16, 35)
            t_e1 = one("ve", c_v, "tensor_copy", c_ebf[:, 0:1], c_sco[:, 34:35])
            if r_stop <= 1:
                t_gather_done = q_g.tok()
                return
            t_mc = mm_group(7, [
                (PB[7][0:NE, 0:1], c_sco[0:1, 8:16], c_onescf[0:1, 0:1],
                 True, True, None),
                (PB[7][0:NE, 1:2], c_sco[0:1, 24:32], c_onescf[0:1, 0:1],
                 True, True, None)], waits=[t_e1])
            t_msl = one("ve", c_v, "tensor_copy", c_msel[0:NE, 0:2],
                        PB[7][0:NE, 0:2], waits=[t_mc])
            read_bank(7, t_msl)
            one("ve", c_v, "tensor_copy", c_ebf[:, 0:1], c_sco[:, 34:35],
                waits=[t_e1])
            one("ve", c_v, "tensor_copy", c_ebf[:, 1:2], c_sco[:, 35:36])
            t_rb = one("ve", c_v, "tensor_copy", c_eb32[:, 0:2], c_ebf[:, 0:2])
            t_gi = t_rb
            for s in range(2):
                t_gi = one("ve", c_v, "tensor_scalar", out=c_gidx[:, s, :],
                           in0=c_eb32[:, s:s + 1].broadcast_to([128, G]),
                           scalar1=1024, scalar2=None, op0=ALU.mult,
                           waits=([t_rb] if s == 0 else ()))
                t_gi = one("ve", c_v, "tensor_tensor", out=c_gidx[:, s, :],
                           in0=c_gidx[:, s, :], in1=c_iota2, op=ALU.add)
            if r_stop <= 2:
                t_gather_done = q_g.tok()
                return
            t_bsel_last = None
            for bsrc, dstb in ((b1, c_b1g), (b2, c_b2g)):
                blw = [t_rt] if t_bsel_last is None else [t_bsel_last]
                t_bld = ld("sy", q_in, [(a_stage[0:NE, :], bsrc)], waits=blw)
                for s in range(2):
                    t_bm2 = mm_group(7, [
                        (PB[7][:, g:g + 1], a_stage[0:NE, g * 128:(g + 1) * 128],
                         c_msel[0:NE, s:s + 1], True, True, None)
                        for g in range(G)], waits=[t_msl, t_bld])
                    t_bv = one("ve", c_v, "tensor_copy", dstb[:, s, :],
                               PB[7][:, 0:G], waits=[t_bm2])
                    read_bank(7, t_bv)
                    t_bsel_last = t_bm2
            if r_stop <= 3:
                ld("gp", q_g, [(outT.rearrange("a b -> a b")[0:128, 0:8], c_gidx[:, 0, :])],
                   waits=[t_gi])
                ld("gp", q_g, [(outT[0:128, 8:16], c_gidx[:, 1, :])])
                ld("gp", q_g, [(outT[0:128, 16:18], c_ebf[:, 0:2])])
                ld("gp", q_g, [(outT[0:8, 20:22], c_msel[0:NE, 0:2])], waits=[t_msl])
                t_gather_done = q_g.tok()
                return
            for s in range(2):
                for (wsrc, wdst) in ((w1, w1g_d), (w2, w2g_d)):
                    for cch in range(G):
                        sb = a_stage
                        R.w("gp", t_gi)
                        R.w("gp", t_bsel_last)
                        R.w("gp", q_g.tok())
                        q_g.nxt()
                        R.i("gp", "indirect_dma_start", out=sb, out_offset=None,
                            in_=wsrc.rearrange("e d f -> (e d) f"),
                            in_offset=IndirectOffsetOnAxis(c_gidx[:, s, cch:cch + 1], 0),
                            inc=(q_g.sem, 16))
                        R.w("gp", q_g.tok())
                        q_g.nxt()
                        R.i("gp", "dma_start",
                            out=wdst[s, cch * 128:(cch + 1) * 128, :], in_=sb,
                            inc=(q_g.sem, 16))
            t_gather_done = q_g.tok()

    emit_A1(0)
    if phases != "An":
        emit_router()
    for i in range(nA):
        if i + 1 < nA:
            emit_A1(i + 1)
        emit_A2(i)

    tA = [(c_v.sem, c_v.v), (c_s.sem, c_s.v), (c_t.sem, c_t.v), q_sxn.tok(),
          q_sxa.tok(), t_gather_done]
    if phases in ("A", "An"):
        if t_gather_done is None:
            tA[5] = tA[3]
        for wt in tA:
            R.w("gp", wt)
        import os as _os
        if _os.environ.get('DUMPXN'):
            ld("gp", q_g, [(outT, xnbf_d)])
            for jj in range(4):
                ld("gp", q_g, [(outT[1020 + jj:1021 + jj, 0:TtA],
                                a_rows[0:1, jj * TtA:(jj + 1) * TtA])])
        else:
            ld("gp", q_g, [(outT, xa_d)])
        R.w("gp", q_g.tok())
        return _finish(nc, R, es)

    B = Alloc(persist_end)
    b_xn = [B.bf(G, TtB), B.bf(G, TtB)]
    b_xa = [B.f32(G, TtB), B.f32(G, TtB)]
    b_h = B.bf(G, TtB)
    b_acc = B.f32(G, TtB)
    b_x1 = [B.f32(G, TtB), B.f32(G, TtB)]
    b_w1g = B.bf(2, G, D)
    b_w2g = B.bf(2, G, D)
    assert B.cur <= nbytes

    t_wB = ld("gp", q_w, [
        (b_w1g[:, 0], w1g_d[0].rearrange("(g p) o -> p g o", p=128)),
        (b_w1g[:, 1], w1g_d[1].rearrange("(g p) o -> p g o", p=128)),
        (b_w2g[:, 0], w2g_d[0].rearrange("(g p) o -> p g o", p=128)),
        (b_w2g[:, 1], w2g_d[1].rearrange("(g p) o -> p g o", p=128)),
    ], waits=tA)
    one("ve", c_v, "tensor_tensor", out=c_bb, in0=c_b2g[:, 0, :],
        in1=c_b2g[:, 1, :], op=ALU.add, waits=tA[:1])
    t_bb = one("ve", c_v, "tensor_scalar", out=c_bb, in0=c_bb, scalar1=0.5,
               scalar2=None, op0=ALU.mult)
    load_act_set([AF.Gelu, AF.Identity])

    tokB_cons = {}
    tokB_st = {}
    t_o_done = None
    for i in range(nB):
        par = i % 2
        t0 = i * TtB
        lw = [t_wB]
        if i - 2 in tokB_cons:
            lw.append(tokB_cons[i - 2])
        t_ld = ld("sy", qx[par], [
            (b_xn[par], fmaj(xnbf_d, t0, TtB)),
            (b_xa[par], fmaj(xa_d, t0, TtB)),
        ], waits=lw)

        if b_stop <= 1:
            tokB_cons[i] = (c_v.sem, c_v.v) if c_v.v else (c_t.sem, c_t.v)
            continue
        for s in range(2):
            hw = [t_ld, t_bb]
            if s == 1:
                hw.append(t_o_done)
            for m in range(G):
                b = m % 2
                t_mm = mm_group(b, [(PB[b][:, 0:TtB],
                                     b_w1g[:, s, g, m * 128:(m + 1) * 128],
                                     b_xn[par][:, g, :], g == 0, g == G - 1, None)
                                    for g in range(G)], waits=(hw if m == 0 else ()))
                tk = one("sc", c_s, "activation", b_h[:, m, :], PB[b][:, 0:TtB],
                         AF.Gelu, bias=c_b1g[:, s, m:m + 1], waits=[t_mm])
                read_bank(b, tk)
            t_h_all = (c_s.sem, c_s.v)
            if b_stop <= 2:
                continue
            for m in range(G):
                b = 2 + m % 2
                t_mm = mm_group(b, [(PB[b][:, 0:TtB],
                                     b_w2g[:, s, g, m * 128:(m + 1) * 128],
                                     b_h[:, g, :], g == 0, g == G - 1, None)
                                    for g in range(G)],
                                waits=([t_h_all] if m == 0 else ()))
                if s == 0:
                    tk = one("ve", c_v, "tensor_copy", b_acc[:, m, :], PB[b][:, 0:TtB],
                             waits=[t_mm])
                else:
                    tv = one("ve", c_v, "tensor_tensor", out=b_acc[:, m, :],
                             in0=b_acc[:, m, :], in1=PB[b][:, 0:TtB], op=ALU.add,
                             waits=[t_mm])
                    ts_ = one("sc", c_s, "activation", b_acc[:, m, :], b_acc[:, m, :],
                              AF.Identity, scale=0.5, bias=c_bb[:, m:m + 1],
                              waits=[tv])
                    tk = one("ve", c_v, "tensor_tensor", out=b_x1[par][:, m, :],
                             in0=b_acc[:, m, :], in1=b_xa[par][:, m, :], op=ALU.add,
                             waits=[ts_])
                read_bank(b, tk)
            if s == 0:
                t_o_done = (c_v.sem, c_v.v)
        tokB_cons[i] = (c_v.sem, c_v.v)
        if b_stop <= 3:
            continue
        stw = [tokB_cons[i]]
        if i - 2 in tokB_st:
            stw.append(tokB_st[i - 2])
        tokB_st[i] = ld("gp", q_sx1, [(fmaj(x1_d, t0, TtB), b_x1[par])], waits=stw)

    tB = [(c_v.sem, c_v.v), (c_s.sem, c_s.v), (c_t.sem, c_t.v), q_sx1.tok()]
    if phases == "AB":
        for wt in tB:
            R.w("gp", wt)
        ld("gp", q_g, [(outT, x1_d)])
        R.w("gp", q_g.tok())
        return _finish(nc, R, es)

    C = Alloc(persist_end)
    cx1 = [C.f32(G, TtC), C.f32(G, TtC)]
    c_xn1 = C.bf(G, TtC)
    c_win = C.bf(G, 1184)
    c_wout = C.bf(G, D)
    c_wq = C.bf(G, D)
    c_out = C.f32(G, TtC)
    c_xssm = C.f32(G, TtC)
    c_xsq2 = C.bf(G, TtC)
    c_Bc = C.bf(TtC)
    c_Cc = C.bf(TtC)
    c_dt = C.f32(TtC)
    c_acum = C.f32(TtC)
    c_dec = C.f32(TtC)
    c_expA = C.f32(TtC)
    c_dtbf = C.bf(TtC)
    c_decbf = C.bf(TtC)
    c_expAbf = C.bf(TtC)
    c_acumT = C.f32(KC, NH)
    c_xdtg = C.bf(TtC)
    c_xddb = C.bf(2, CS)
    c_xdtT = C.bf(KC, D)
    c_xddT = C.bf(KC, D)
    c_BT = C.bf(KC, DS)
    c_prevT = C.bf(KC, D)
    c_seg = C.f32(4, KC * CS)
    c_G4 = C.bf(4, KC * CS)
    c_cbt = C.bf(KC, CS)
    c_ys = C.f32(KC * CS)
    c_ybf = C.bf(G, TtC)
    c_rows2 = c_seg.rearrange("p a b -> p (a b)")
    assert C.cur <= nbytes, f"phase C arena {C.cur} > {nbytes}"

    t_wC = ld("gp", q_w, [
        (c_win, winT.rearrange("(g p) o -> p g o", p=128)),
        (c_wout, woutT.rearrange("(g p) o -> p g o", p=128)),
    ], waits=tB)

    load_act_set([AF.Exp, AF.Ln, AF.Square, AF.Identity])

    tstg0, tstg1 = c_xssm, c_out
    wf_r = wfT.rearrange("(h g p) o -> p (h g) o", p=128, h=2)
    t_tl = ld("sy", q_in, [(tstg0, wf_r[:, 0:4, :]), (tstg1, wf_r[:, 4:8, :])],
              waits=tB)
    flat0 = tstg0.rearrange("p a b -> p (a b)")
    flat1 = tstg1.rearrange("p a b -> p (a b)")
    one("ve", c_v, "reduce_sum", c_ys[:, 0:1], flat0, AX.X,
        apply_absolute_value=True, waits=[t_tl] + tB[:1])
    one("ve", c_v, "reduce_sum", c_ys[:, 1:2], flat1, AX.X,
        apply_absolute_value=True)
    t_r2 = one("ve", c_v, "tensor_tensor", out=c_ys[:, 0:1], in0=c_ys[:, 0:1],
               in1=c_ys[:, 1:2], op=ALU.add)
    t_sc0 = mm_group(6, [(PB[6][0:1, 0:1], c_ys[:, 0:1], c_onescf, True, True,
                          None)], waits=[t_r2, tB[2]])
    t_sc1 = one("ve", c_v, "tensor_scalar", out=c_ys[0:1, 2:3], in0=PB[6][0:1, 0:1],
                scalar1=1.0 / (D * D), scalar2=EPS_TERN, op0=ALU.mult, op1=ALU.max,
                waits=[t_sc0])
    read_bank(6, t_sc1)
    t_sc2 = mm_group(6, [(PB[6][:, 4:5], c_onesr[0:1, :], c_ys[0:1, 2:3],
                          True, True, None)], waits=[t_sc1])
    t_sc3 = one("ve", c_v, "tensor_copy", c_scl[:, 0:1], PB[6][:, 4:5], waits=[t_sc2])
    read_bank(6, t_sc3)
    t_sc4 = one("ve", c_v, "reciprocal", c_scl[:, 1:2], c_scl[:, 0:1])
    segflat = c_seg.rearrange("p a b -> p (a b)")
    ybfflat = c_ybf.rearrange("p a b -> p (a b)").bitcast(F32)
    xn1flat = c_xn1.rearrange("p a b -> p (a b)").bitcast(F32)
    tmpA = ybfflat
    tmpB = xn1flat
    wqf = c_wq.rearrange("p a b -> p (a b)")
    t_q = t_sc4
    for h, flat in ((0, flat0), (1, flat1)):
        one("ve", c_v, "tensor_scalar", out=tmpA[:, 0:2048], in0=flat[:, 0:2048],
            scalar1=c_scl[:, 1:2], scalar2=0.5, op0=ALU.mult, op1=ALU.is_ge,
            waits=[t_q])
        one("ve", c_v, "tensor_scalar", out=tmpB[:, 0:2048], in0=flat[:, 0:2048],
            scalar1=c_scl[:, 1:2], scalar2=-0.5, op0=ALU.mult, op1=ALU.is_le)
        one("ve", c_v, "tensor_tensor", out=tmpA[:, 0:2048], in0=tmpA[:, 0:2048],
            in1=tmpB[:, 0:2048], op=ALU.subtract)
        one("ve", c_v, "tensor_scalar", out=tmpB[:, 0:2048], in0=flat[:, 2048:4096],
            scalar1=c_scl[:, 1:2], scalar2=0.5, op0=ALU.mult, op1=ALU.is_ge)
        one("ve", c_v, "tensor_scalar", out=segflat[:, 0:2048], in0=flat[:, 2048:4096],
            scalar1=c_scl[:, 1:2], scalar2=-0.5, op0=ALU.mult, op1=ALU.is_le)
        one("ve", c_v, "tensor_tensor", out=tmpB[:, 0:2048], in0=tmpB[:, 0:2048],
            in1=segflat[:, 0:2048], op=ALU.subtract)
        one("ve", c_v, "tensor_scalar", out=wqf[:, h * 4096:h * 4096 + 2048],
            in0=tmpA[:, 0:2048], scalar1=c_scl[:, 0:1], scalar2=None, op0=ALU.mult)
        t_q = one("ve", c_v, "tensor_scalar",
                  out=wqf[:, h * 4096 + 2048:(h + 1) * 4096], in0=tmpB[:, 0:2048],
                  scalar1=c_scl[:, 0:1], scalar2=None, op0=ALU.mult)
    t_tern = t_q

    tokC_cons = {}
    tokC_st = {}
    for i in range(nC):
        par = i % 2
        t0 = i * TtC
        lw = [t_tern]
        if i - 2 in tokC_cons:
            lw.append(tokC_cons[i - 2])
        t_ld = ld("sy", qx[par], [(cx1[par], fmaj(x1_d, t0, TtC))], waits=lw)

        scw = [t_ld]
        if i - 1 in tokC_cons:
            scw.append(tokC_cons[i - 1])
        t_cbf = one("sc", c_s, "activation", c_ybf, cx1[par], AF.Identity,
                    waits=scw)
        t_sq = one("sc", c_s, "activation", c_xsq2, cx1[par], AF.Square)
        t_s1 = mm_group(6, [(PB[6][0:1, 0:TtC], c_onescbf, c_ybf[:, g, :],
                             g == 0, g == G - 1, None) for g in range(G)],
                        waits=[t_cbf, t_wC, t_tern])
        t_s2 = mm_group(7, [(PB[7][0:1, 0:TtC], c_onescbf, c_xsq2[:, g, :],
                             g == 0, g == G - 1, None) for g in range(G)],
                        waits=[t_sq])
        t_rw = one("ve", c_v, "tensor_scalar", out=c_rows2[0:1, 0:TtC],
                   in0=PB[6][0:1, 0:TtC], scalar1=1.0 / D, scalar2=None,
                   op0=ALU.mult, waits=[t_s1, t_tern])
        t_rw2 = one("ve", c_v, "tensor_scalar", out=c_rows2[0:1, TtC:2 * TtC],
                    in0=PB[7][0:1, 0:TtC], scalar1=1.0 / D, scalar2=None,
                    op0=ALU.mult, waits=[t_s2])
        t_m2 = one("sc", c_s, "activation", c_rows2[0:1, 2 * TtC:3 * TtC],
                   c_rows2[0:1, 0:TtC], AF.Square, waits=[t_rw])
        t_vr = one("ve", c_v, "tensor_tensor", out=c_rows2[0:1, 3 * TtC:4 * TtC],
                   in0=c_rows2[0:1, TtC:2 * TtC], in1=c_rows2[0:1, 2 * TtC:3 * TtC],
                   op=ALU.subtract, waits=[t_m2, t_rw2])
        one("sc", c_s, "activation", c_rows2[0:1, 3 * TtC:4 * TtC],
            c_rows2[0:1, 3 * TtC:4 * TtC], AF.Ln, bias=c_fc[0:1, 0:1], waits=[t_vr])
        t_rs = one("sc", c_s, "activation", c_rows2[0:1, 3 * TtC:4 * TtC],
                   c_rows2[0:1, 3 * TtC:4 * TtC], AF.Exp, scale=-0.5)
        t_bm = mm_group(6, [(PB[6][:, 0:TtC], c_onesr[0:1, :], c_rows2[0:1, 0:TtC],
                             True, True, None)], waits=[t_rw, t_rs])
        t_br2 = mm_group(7, [(PB[7][:, 0:TtC], c_onesr[0:1, :],
                              c_rows2[0:1, 3 * TtC:4 * TtC], True, True, None)],
                         waits=[t_rw2, t_rs])
        t_xnc = one("ve", c_v, "tensor_tensor", out=c_xssm, in0=cx1[par],
                    in1=PB[6][:, None, 0:TtC].broadcast_to([128, G, TtC]),
                    op=ALU.subtract, waits=[t_bm])
        read_bank(6, t_xnc)
        t_xn1 = one("ve", c_v, "tensor_tensor", out=c_xn1, in0=c_xssm,
                    in1=PB[7][:, None, 0:TtC].broadcast_to([128, G, TtC]),
                    op=ALU.mult, waits=[t_br2])
        read_bank(7, t_xn1)

        for jm in range(G // 2):
            bb = 2 * (jm % 2)
            for h in range(2):
                m = 2 * jm + h
                mw = [t_xn1] if m == 0 else []
                t_mm = mm_group(bb + h, [(PB[bb + h][:, 0:TtC],
                                          c_win[:, g, m * 128:(m + 1) * 128],
                                          c_xn1[:, g, :], g == 0, g == G - 1,
                                          None) for g in range(G)], waits=mw)
            tk = one("ve", c_v, "tensor_copy", c_xssm[:, 2 * jm:2 * jm + 2, :],
                     PBPR2[bb // 2][:, :, 0:TtC], waits=[t_mm, t_xn1])
            read_bank(bb, tk)
            read_bank(bb + 1, tk)
        mm_specs = [("Ba", c_win[:, :, 1024:1088], 64),
                    ("Cb", c_win[:, :, 1088:1152], 64),
                    ("dt", c_win[:, :, 1152:1184], 32)]
        for si, (mk, wsl, mwid) in enumerate(mm_specs):
            b = si % 2
            mw = []
            t_mm = mm_group(b, [(PB[b][0:mwid, 0:TtC], wsl[:, g, :], c_xn1[:, g, :],
                                 g == 0, g == G - 1, None) for g in range(G)],
                            waits=mw)
            if mk == "Ba":
                tk = one("sc", c_s, "activation", c_Bc[0:64, :], PB[b][0:64, 0:TtC],
                         AF.Identity, waits=[t_mm])
            elif mk == "Cb":
                tk = one("sc", c_s, "activation", c_Cc[0:64, :], PB[b][0:64, 0:TtC],
                         AF.Identity, waits=[t_mm])
            else:
                one("sc", c_s, "activation", c_dt[0:NH, :], PB[b][0:NH, 0:TtC],
                    AF.Exp, bias=c_dtb[0:NH, 0:1], waits=[t_mm])
                tk = one("sc", c_s, "activation", c_dt[0:NH, :], c_dt[0:NH, :],
                         AF.Ln, bias=c_onescf[0:NH, 0:1])
            read_bank(b, tk)
        t_dt_done = tk
        t_xs_all = (c_v.sem, c_v.v)

        t_dtbf = one("sc", c_s, "activation", c_dtbf[0:NH, :], c_dt[0:NH, :],
                     AF.Identity, waits=[t_dt_done])
        t_a = one("ve", c_v, "tensor_scalar", out=c_dt[0:NH, :], in0=c_dt[0:NH, :],
                  scalar1=c_A[0:NH, 0:1], scalar2=None, op0=ALU.mult,
                  waits=[t_dtbf])
        for cc in range(KC):
            quiet("ve", "tensor_tensor_scan", c_acum[0:NH, cc * CS:(cc + 1) * CS],
                  c_onescf[0:NH, 0:1].broadcast_to([NH, CS]),
                  c_dt[0:NH, cc * CS:(cc + 1) * CS], 0.0, ALU.mult, ALU.add)
        t_acum = seal("ve", c_v)
        for cc in range(KC):
            quiet("sc", "activation", c_dec[0:NH, cc * CS:(cc + 1) * CS],
                  c_acum[0:NH, cc * CS:(cc + 1) * CS], AF.Exp, scale=-1.0,
                  bias=c_acum[0:NH, (cc + 1) * CS - 1:(cc + 1) * CS],
                  waits=([t_acum] if cc == 0 else ()))
        one("sc", c_s, "activation", c_expA[0:NH, :], c_acum[0:NH, :], AF.Exp)
        one("sc", c_s, "activation", c_expAbf[0:NH, :], c_expA[0:NH, :], AF.Identity)
        t_decs = one("sc", c_s, "activation", c_decbf[0:NH, :], c_dec[0:NH, :],
                     AF.Identity)

        ld("sy", q_bc, [(acum_d[:, 0:TtC], c_acum[0:NH, 0:TtC])],
           waits=[t_acum, q_bc.tok()])

        atw = [t_acum]
        if 3 in bank_r:
            atw.append(bank_r.pop(3))
        for wt in atw:
            R.w("te", wt)
        for cc in range(KC):
            quiet("te", "transpose", PB[3][:, cc * NH:(cc + 1) * NH],
                  c_acum[0:NH, cc * CS:(cc + 1) * CS], c_idf[0:NH, 0:NH])
        t_at = seal("te", c_t)
        t_acT = one("ve", c_v, "tensor_copy", c_acumT.rearrange("p a b -> p (a b)"),
                    PB[3][:, 0:KC * NH], waits=[t_at])
        read_bank(3, t_acT)

        tbank = (PBbf[2], PBbf[3])
        tp_r = {0: None, 1: None}
        nt_box = [0]

        def do_transpose_bf(src_ap, dst_ap, p_src, f_src, ident, extra=()):
            half = nt_box[0] % 2
            nt_box[0] += 1
            w = list(extra)
            if tp_r[half] is not None:
                w.append(tp_r[half])
            if (2 + half) in bank_r:
                w.append(bank_r.pop(2 + half))
            for wt in w:
                R.w("te", wt)
            c_t.nxt()
            R.i("te", "transpose", tbank[half][0:f_src, 0:p_src],
                src_ap, ident, inc=(c_t.sem, 1))
            t_tp = c_t.tok()
            tk = one("sc", c_s, "activation", dst_ap,
                     tbank[half][0:f_src, 0:p_src],
                     AF.Identity, waits=[t_tp])
            tp_r[half] = tk
            read_bank(2 + half, tk)
            return tk

        t_prev_g = None
        xdb_tok = {0: None, 1: None}
        for g in range(G):
            bdt = 4 + (g % 2)
            t_rp = mm_group(bdt, [(PB[bdt][:, 0:TtC],
                                   c_repl[0:NH, g * 128:(g + 1) * 128],
                                   c_dtbf[0:NH, :], True, True, None)],
                            waits=([t_dtbf, t_xs_all, t_acT] if g == 0 else ()))
            xw = [t_rp]
            if t_prev_g is not None:
                xw.append(t_prev_g)
            t_xg = one("ve", c_v, "tensor_tensor", out=c_xdtg, in0=c_xssm[:, g, :],
                       in1=PB[bdt][:, 0:TtC], op=ALU.mult, waits=xw)
            read_bank(bdt, t_xg)
            bdc = 6 + (g % 2)
            t_rp2 = mm_group(bdc, [(PB[bdc][:, 0:TtC],
                                    c_repl[0:NH, g * 128:(g + 1) * 128],
                                    c_decbf[0:NH, :], True, True, None)],
                             waits=([t_decs] if g == 0 else ()))
            t_xd = None
            for cc in range(KC):
                t_tr1 = do_transpose_bf(c_xdtg[:, cc * CS:(cc + 1) * CS],
                                        c_xdtT[:, cc, g * 128:(g + 1) * 128],
                                        128, 128, c_idbf, extra=[t_xg])
                xdw = [t_rp2]
                if xdb_tok[cc % 2] is not None:
                    xdw.append(xdb_tok[cc % 2])
                t_xd = one("ve", c_v, "tensor_tensor", out=c_xddb[:, cc % 2, :],
                           in0=c_xdtg[:, cc * CS:(cc + 1) * CS],
                           in1=PB[bdc][:, cc * CS:(cc + 1) * CS], op=ALU.mult,
                           waits=xdw)
                xdb_tok[cc % 2] = do_transpose_bf(
                    c_xddb[:, cc % 2, :], c_xddT[:, cc, g * 128:(g + 1) * 128],
                    128, 128, c_idbf, extra=[t_xd])
            read_bank(bdc, t_xd)
            t_prev_g = xdb_tok[(KC - 1) % 2]
        t_trans_last = None
        for cc in range(KC):
            t_trans_last = do_transpose_bf(
                c_Bc[0:64, cc * CS:(cc + 1) * CS],
                c_BT[:, cc, :], 64, 128, c_idbf[0:64, 0:64], extra=[t_dt_done])
        t_trans_done = t_trans_last

        t_prev_up = None
        t_cb_last = None
        for cc in range(KC):
            t_cb = mm_group(2, [(PB[2][:, 0:CS], c_Bc[0:64, cc * CS:(cc + 1) * CS],
                                 c_Cc[0:64, cc * CS:(cc + 1) * CS], True, True,
                                 None)], waits=([t_trans_done] if cc == 0 else []))
            tk = one("ve", c_v, "tensor_tensor", out=c_cbt[:, cc, :],
                     in0=PB[2][:, 0:CS], in1=c_tri, op=ALU.mult, waits=[t_cb])
            read_bank(2, tk)
            t_cb_last = tk
            pw = [tk] + ([t_prev_up] if t_prev_up else [])
            for wt in pw:
                R.w("te", wt)
            for g in range(G):
                c_t.nxt()
                R.i("te", "transpose", PB[2][0:64, g * 128:(g + 1) * 128]
                    if g < 4 else PB[3][0:64, (g - 4) * 128:(g - 4 + 1) * 128],
                    c_prev[:, g, :], c_idf, inc=(c_t.sem, 1))
            t_pt = c_t.tok()
            tk1 = one("sc", c_s, "activation", c_prevT[0:64, cc, 0:512],
                      PB[2][0:64, 0:512], AF.Identity, waits=[t_pt, t_acT])
            tk2 = one("sc", c_s, "activation", c_prevT[0:64, cc, 512:1024],
                      PB[3][0:64, 0:512], AF.Identity)
            read_bank(2, tk2)
            read_bank(3, tk2)
            t_st_mm = mm_group(6, [(PB[6][:, g * 64:(g + 1) * 64],
                                    c_xddT[:, cc, g * 128:(g + 1) * 128],
                                    c_BT[:, cc, :], True, True, None)
                                   for g in range(G)], waits=[tk2])
            t_cd = mm_group(7, [(PB[7][:, g:g + 1],
                                 c_repl[0:NH, g * 128:(g + 1) * 128],
                                 c_expAbf[0:NH, (cc + 1) * CS - 1:(cc + 1) * CS],
                                 True, True, None) for g in range(G)], waits=[])
            tk = one("ve", c_v, "tensor_tensor", out=c_prev,
                     in0=c_prev,
                     in1=PB[7][:, 0:G][:, :, None].broadcast_to([128, G, DS]),
                     op=ALU.mult, waits=[t_cd, tk2])
            read_bank(7, tk)
            t_prev_up = one("ve", c_v, "tensor_tensor", out=c_prev, in0=c_prev,
                            in1=PB[6][:, 0:G * DS].rearrange("p (a b) -> p a b", a=G),
                            op=ALU.add, waits=[t_st_mm])
            read_bank(6, t_prev_up)

        t_y_done = None
        grp_read = None
        t_exp_prev = None
        for grp in range(G):
            bw = [q_bc.tok()]
            if t_exp_prev is not None:
                bw.append(t_exp_prev)
            else:
                bw.append(t_br2)
            t_bdma = ld("sy", q_bc, [
                (segflat[:, 0:4 * TtC],
                 acum_d[4 * grp:4 * grp + 4, 0:TtC].rearrange(
                     "a b -> (a b)")[None, :].broadcast_to([128, 4 * TtC]))],
                waits=bw)
            t_sub = one("ve", c_v, "tensor_tensor",
                        out=c_seg.rearrange("p a (b c) -> p a b c", b=KC),
                        in0=c_seg.rearrange("p a (b c) -> p a b c", b=KC),
                        in1=c_acumT[:, :, 4 * grp:4 * grp + 4].rearrange(
                            "p a b -> p b a")[:, :, :, None].broadcast_to(
                            [128, 4, KC, CS]),
                        op=ALU.subtract, waits=[t_bdma, t_acT, t_cb_last])
            t_min = one("ve", c_v, "tensor_scalar", out=segflat[:, 0:4 * KC * CS],
                        in0=segflat[:, 0:4 * KC * CS], scalar1=0.0, scalar2=None,
                        op0=ALU.min, waits=[t_sub])
            t_exp = one("sc", c_s, "activation",
                        c_G4.rearrange("p a b -> p (a b)"),
                        segflat[:, 0:4 * KC * CS], AF.Exp, waits=[t_min])
            t_exp_prev = t_exp
            t_g4 = one("ve", c_v, "tensor_tensor",
                       out=c_G4, in0=c_G4,
                       in1=c_cbt[:, None, :, :].broadcast_to([128, 4, KC, CS]),
                       op=ALU.mult, waits=[t_exp])
            ydl = []
            for cc in range(KC):
                for hh in range(4):
                    h = 4 * grp + hh
                    ydl.append((PB[2][hh * 32:(hh + 1) * 32, cc * CS:(cc + 1) * CS],
                                c_xdtT[:, cc, h * 32:(h + 1) * 32],
                                c_G4[:, hh, cc * CS:(cc + 1) * CS], True, True,
                                (0, 32 * hh)))
            t_yd = mm_group(2, ydl, waits=[t_g4, t_prev_up])
            yol = []
            for cc in range(KC):
                yol.append((PB[3][:, cc * CS:(cc + 1) * CS],
                            c_prevT[0:64, cc, grp * 128:(grp + 1) * 128],
                            c_Cc[0:64, cc * CS:(cc + 1) * CS], True, True, None))
            t_yo = mm_group(3, yol, waits=[])
            t_ea = mm_group(4, [(PB[4][:, cc * CS:(cc + 1) * CS],
                                 c_repl[0:NH, grp * 128:(grp + 1) * 128],
                                 c_expAbf[0:NH, cc * CS:(cc + 1) * CS], True, True,
                                 None) for cc in range(KC)], waits=[t_min])
            tk0 = one("ve", c_v, "tensor_copy", c_ys, PB[3][:, 0:TtC],
                      waits=[t_yo])
            read_bank(3, tk0)
            tk = one("ve", c_v, "tensor_tensor", out=c_ys, in0=c_ys,
                     in1=PB[4][:, 0:TtC], op=ALU.mult, waits=[t_ea])
            read_bank(4, tk)
            tk2 = one("ve", c_v, "tensor_tensor", out=c_ys, in0=c_ys,
                      in1=PB[2][:, 0:TtC], op=ALU.add, waits=[t_yd])
            read_bank(2, tk2)
            t_y_done = one("ve", c_v, "scalar_tensor_tensor", out=c_ybf[:, grp, :],
                           in0=c_xssm[:, grp, :], scalar=c_dexp[:, grp:grp + 1],
                           in1=c_ys, op0=ALU.mult, op1=ALU.add)

        for jm in range(G // 2):
            bb = 2 * (jm % 2)
            for h in range(2):
                m = 2 * jm + h
                mw = [t_y_done] if m == 0 else []
                if m == 0 and i - 1 in tokC_st:
                    mw.append(tokC_st[i - 1])
                mml = [(PB[bb + h][:, 0:TtC], c_wout[:, g, m * 128:(m + 1) * 128],
                        c_ybf[:, g, :], g == 0, False, None) for g in range(G)]
                mml += [(PB[bb + h][:, 0:TtC], c_wq[:, g, m * 128:(m + 1) * 128],
                         c_xn1[:, g, :], False, g == G - 1, None)
                        for g in range(G)]
                t_mm = mm_group(bb + h, mml, waits=mw)
            tk = one("ve", c_v, "tensor_tensor",
                     out=c_out[:, 2 * jm:2 * jm + 2, :],
                     in0=cx1[par][:, 2 * jm:2 * jm + 2, :],
                     in1=PBPR2[bb // 2][:, :, 0:TtC],
                     op=ALU.add, waits=[t_mm])
            read_bank(bb, tk)
            read_bank(bb + 1, tk)
        tokC_cons[i] = (c_v.sem, c_v.v)
        stw = [tokC_cons[i]]
        if i - 1 in tokC_st:
            stw.append(tokC_st[i - 1])
        tokC_st[i] = ld("gp", q_so, [(fmaj(outT, t0, TtC), c_out)], waits=stw)

    R.w("gp", q_so.tok())
    R.w("gp", q_g.tok())
    return _finish(nc, R, es)


def _finish(nc, R, es):
    with nc.Block() as block:
        @block.sync
        def _(sy):
            R.replay("sy", sy)

        @block.tensor
        def _(te):
            R.replay("te", te)

        @block.vector
        def _(ve):
            R.replay("ve", ve)

        @block.scalar
        def _(sc):
            R.replay("sc", sc)

        @block.gpsimd
        def _(gp):
            R.replay("gp", gp)

    es.close()
    return nc


_NCORES = 8
_T = 8192


def _pack_pg(v):
    return np.ascontiguousarray(np.asarray(v, np.float32).reshape(G, 128).T)


def kernel(
    L0_attn_x, L0_attn_time_decay, L0_attn_time_first, L0_attn_time_mix_k,
    L0_attn_time_mix_v, L0_attn_time_mix_r, L0_attn_Wk, L0_attn_Wv, L0_attn_Wr,
    L0_attn_Wo, L0_ffn_Wr, L0_ffn_br, L0_ffn_W1, L0_ffn_b1, L0_ffn_W2, L0_ffn_b2,
    L1_attn_W_in, L1_attn_dt_bias, L1_attn_A_log, L1_attn_D, L1_attn_W_out,
    L1_ffn_weight,
):
    x = np.asarray(L0_attn_x, np.float32)
    Bsz, T, _ = x.shape
    shared = {
        "wkT": np.ascontiguousarray(np.asarray(L0_attn_Wk, np.float32).T),
        "wvT": np.ascontiguousarray(np.asarray(L0_attn_Wv, np.float32).T),
        "wrT": np.ascontiguousarray(np.asarray(L0_attn_Wr, np.float32).T),
        "woT": np.ascontiguousarray(np.asarray(L0_attn_Wo, np.float32).T),
        "tmk": _pack_pg(L0_attn_time_mix_k),
        "tmv": _pack_pg(L0_attn_time_mix_v),
        "tmr": _pack_pg(L0_attn_time_mix_r),
        "tdp": _pack_pg(L0_attn_time_decay),
        "tfp": _pack_pg(L0_attn_time_first),
        "wr_rtr": np.ascontiguousarray(np.asarray(L0_ffn_Wr, np.float32)),
        "br_rtr": np.asarray(L0_ffn_br, np.float32).reshape(1, NE),
        "w1": np.ascontiguousarray(np.asarray(L0_ffn_W1, np.float32)),
        "b1": np.ascontiguousarray(np.asarray(L0_ffn_b1, np.float32)),
        "w2": np.ascontiguousarray(np.asarray(L0_ffn_W2, np.float32)),
        "b2": np.ascontiguousarray(np.asarray(L0_ffn_b2, np.float32)),
        "winT": np.ascontiguousarray(np.asarray(L1_attn_W_in, np.float32).T),
        "dtb": np.asarray(L1_attn_dt_bias, np.float32).reshape(NH, 1),
        "alog": np.asarray(L1_attn_A_log, np.float32).reshape(NH, 1),
        "dexp": _pack_pg(np.repeat(np.asarray(L1_attn_D, np.float32), HD)),
        "woutT": np.ascontiguousarray(np.asarray(L1_attn_W_out, np.float32).T),
        "wfT": np.ascontiguousarray(np.asarray(L1_ffn_weight, np.float32).T),
    }
    in_maps = []
    for b in range(Bsz):
        m = dict(shared)
        m["xT"] = np.ascontiguousarray(x[b].T)
        in_maps.append(m)
    nc = build_program(T)
    res = run_bass_kernel_spmd(nc, in_maps, core_ids=list(range(_NCORES)))
    out = np.stack([np.asarray(r["outT"]).T for r in res.results], axis=0)
    return np.ascontiguousarray(out.astype(np.float32))


# version 10
# speedup vs baseline: 1.1473x; 1.0230x over previous
import numpy as np
import ml_dtypes
import concourse.bass as bass
import concourse.mybir as mybir
from concourse.bass_utils import run_bass_kernel_spmd
from concourse.bass_core_rust import IndirectOffsetOnAxis
from contextlib import ExitStack

F32 = mybir.dt.float32
BF16 = mybir.dt.bfloat16
I32 = mybir.dt.int32
U32 = mybir.dt.uint32
U8 = mybir.dt.uint8
AF = mybir.ActivationFunctionType
ALU = mybir.AluOpType
AX = mybir.AxisListType
BFNP = ml_dtypes.bfloat16

D = 1024
G = 8
NE = 8
NH, HD, DS = 32, 32, 64
CS = 128
EPS_LN = 1e-5
EPS_TERN = 1e-5

ENGS = ("sy", "te", "ve", "sc", "gp")


class Rec:
    def __init__(self):
        self.q = {e: [] for e in ENGS}

    def w(self, eng, tok):
        if tok is not None:
            sem, val = tok
            if val > 0:
                self.q[eng].append(("wait_ge", (sem, val), {}, None))

    def i(self, eng, method, *args, inc=None, **kw):
        self.q[eng].append((method, args, kw, inc))

    def replay(self, name, proxy):
        for method, args, kw, inc in self.q[name]:
            if method == "__act_table__":
                r = proxy.add_instruction(mybir.InstLoadActFuncSet(
                    name=proxy.bass.get_next_instruction_name(),
                    act_func_set_id=args[0], ins=[], outs=[]))
            else:
                r = getattr(proxy, method)(*args, **kw)
            if inc is not None:
                r.then_inc(inc[0], inc[1])


class Ctr:
    def __init__(self, sm, step=1):
        self.sem, self.v, self.step = sm, 0, step

    def nxt(self):
        self.v += self.step
        return (self.sem, self.v)

    def tok(self):
        return (self.sem, self.v)


def build_program(T, TtA=256, TtB=512, TtC=512, phases="ABC", a_stop=10**9, b_stop=10**9, r_stop=10**9, rcheck=False):
    nc = bass.Bass("TRN2", target_bir_lowering=False, detect_race_conditions=rcheck)
    es = ExitStack()
    nA, nB, nC = T // TtA, T // TtB, T // TtC
    KC = TtC // CS

    def din(name, shape, dtype=F32):
        return nc.dram_tensor(name, list(shape), dtype, kind="ExternalInput").ap()

    xT = din("xT", [D, T])
    wkT = din("wkT", [D, D]); wvT = din("wvT", [D, D])
    wrT = din("wrT", [D, D]); woT = din("woT", [D, D])
    tmk = din("tmk", [128, G]); tmv = din("tmv", [128, G]); tmr = din("tmr", [128, G])
    tdp = din("tdp", [128, G]); tfp = din("tfp", [128, G])
    wr_rtr = din("wr_rtr", [D, NE]); br_rtr = din("br_rtr", [1, NE])
    w1 = din("w1", [NE, D, D]); b1 = din("b1", [NE, D])
    w2 = din("w2", [NE, D, D]); b2 = din("b2", [NE, D])
    winT = din("winT", [D, 1184])
    dtb = din("dtb", [NH, 1]); alog = din("alog", [NH, 1])
    dexp = din("dexp", [128, G])
    woutT = din("woutT", [D, D])
    wfT = din("wfT", [D, D])
    outT = nc.dram_tensor("outT", [D, T], F32, kind="ExternalOutput").ap()

    xnbf_d = nc.dram_tensor("xnbf_d", [D, T], BF16, kind="Internal").ap()
    xa_d = nc.dram_tensor("xa_d", [D, T], F32, kind="Internal").ap()
    x1_d = nc.dram_tensor("x1_d", [D, T], F32, kind="Internal").ap()
    w1g_d = nc.dram_tensor("w1g_d", [2, D, D], BF16, kind="Internal").ap()
    w2g_d = nc.dram_tensor("w2g_d", [2, D, D], BF16, kind="Internal").ap()
    bg_d = nc.dram_tensor("bg_d", [4, D], F32, kind="Internal").ap()
    idx_d = nc.dram_tensor("idx_d", [1, 2], U32, kind="Internal").ap()
    acum_d = nc.dram_tensor("acum_d", [NH, 512], F32, kind="Internal").ap()

    ident_bf_c = nc.inline_tensor(np.eye(128, dtype=BFNP), "ident_bf_c")
    ident_f_c = nc.inline_tensor(np.eye(128, dtype=np.float32), "ident_f_c")
    ones_row_f_c = nc.inline_tensor(np.ones((1, 128), np.float32), "ones_row_f_c")
    ones_col_bf_c = nc.inline_tensor(np.ones((128, 1), BFNP), "ones_col_bf_c")
    repl_np = np.zeros((NH, G * 128), np.float32)
    for g in range(G):
        for m in range(128):
            repl_np[4 * g + m // 32, g * 128 + m] = 1.0
    repl_bf_c = nc.inline_tensor(repl_np.astype(BFNP), "repl_bf_c")
    tri_c = nc.inline_tensor(
        (np.arange(CS)[:, None] <= np.arange(CS)[None, :]).astype(np.float32), "tri_c")
    iota2d_c = nc.inline_tensor(
        (np.arange(128)[:, None] + 128 * np.arange(G)[None, :]).astype(np.int32),
        "iota2d_c")
    iota8_c = nc.inline_tensor(
        np.tile(np.arange(8, dtype=np.float32), (128, 1)), "iota8_c")
    selh_np = np.zeros((NH, NH * 128), np.float32)
    for hh_ in range(NH):
        selh_np[hh_, hh_ * 128:(hh_ + 1) * 128] = 1.0
    selh_c = nc.inline_tensor(selh_np, "selh_c")

    nbytes = nc.sbuf_bytes_remaining - 2048
    nbytes -= nbytes % 256
    arena = nc.alloc_sbuf_tensor("arena", [128, nbytes], U8)
    AFV = {F32: arena.bitcast(F32), BF16: arena.bitcast(BF16),
           I32: arena.bitcast(I32), U32: arena.bitcast(U32)}
    DSZ = {F32: 4, BF16: 2, I32: 4, U32: 4}

    class Alloc:
        def __init__(self, base):
            self.cur = base

        def mk(self, dt_, *shape):
            n = int(np.prod(shape))
            sz = DSZ[dt_]
            off = (self.cur + 63) // 64 * 64
            self.cur = off + n * sz
            assert self.cur <= nbytes, f"sbuf arena overflow: {self.cur} > {nbytes}"
            v = AFV[dt_][:, off // sz: off // sz + n]
            if len(shape) == 3:
                return v.rearrange("p (a b c) -> p a b c", a=shape[0], b=shape[1])
            if len(shape) == 2:
                return v.rearrange("p (a b) -> p a b", a=shape[0])
            return v

        def f32(self, *s):
            return self.mk(F32, *s)

        def bf(self, *s):
            return self.mk(BF16, *s)

        def i32(self, *s):
            return self.mk(I32, *s)

        def view(self, dt_, off, *shape):
            n = int(np.prod(shape))
            sz = DSZ[dt_]
            v = AFV[dt_][:, off // sz: off // sz + n]
            if len(shape) == 3:
                return v.rearrange("p (a b c) -> p a b c", a=shape[0], b=shape[1])
            if len(shape) == 2:
                return v.rearrange("p (a b) -> p a b", a=shape[0])
            return v

    P = Alloc(0)
    c_tmx = {"k": P.f32(G), "v": P.f32(G), "r": P.f32(G)}
    c_ew = P.f32(G); c_eu = P.f32(G); c_dexp = P.f32(G)
    c_A = P.f32(1); c_dtb = P.f32(1)
    c_idbf = P.bf(128); c_idf = P.f32(128)
    c_onesr = P.f32(128)
    c_onescbf = P.bf(1)
    c_onescf = P.f32(1)
    c_repl = P.bf(G * 128)
    c_tri = P.f32(CS)
    c_iota2 = P.i32(G)
    c_wr = P.bf(G, NE)
    c_br = P.f32(NE)
    c_sco = P.f32(64)
    c_idxu = P.mk(U32, 8)
    c_eb32 = P.i32(2)
    c_gidx = P.i32(2, G)
    c_b1g = P.f32(2, G); c_b2g = P.f32(2, G); c_bb = P.f32(G)
    c_scl = P.f32(2)
    c_prev = P.f32(G, DS)
    c_tmpg = P.f32(G)
    c_fc = P.f32(2)
    c_msel = P.f32(2)
    c_ebf = P.f32(2)
    c_iotf = P.f32(1)
    persist_end = P.cur

    ps = nc.alloc_psum_tensor("ps", [128, 4096], F32)
    psa = ps.ap()
    PB = [psa[:, 512 * b: 512 * (b + 1)] for b in range(8)]
    ps_bf = ps.bitcast(BF16)
    PBbf = [ps_bf[:, 1024 * b: 1024 * (b + 1)] for b in range(8)]
    PBPR2 = [psa[:, 1024 * p: 1024 * (p + 1)].rearrange(
        "p (a c) -> p a c", a=2) for p in range(2)]

    def mksem(name):
        return es.enter_context(nc.semaphore(name))

    q_in = Ctr(mksem("q_in"), 16)
    q_w = Ctr(mksem("q_w"), 16)
    qx = [Ctr(mksem("qx0"), 16), Ctr(mksem("qx1"), 16)]
    q_sxn = Ctr(mksem("q_sxn"), 16)
    q_sxa = Ctr(mksem("q_sxa"), 16)
    q_sx1 = Ctr(mksem("q_sx1"), 16)
    q_so = Ctr(mksem("q_so"), 16)
    q_g = Ctr(mksem("q_g"), 16)
    q_bc = Ctr(mksem("q_bc"), 16)
    c_t = Ctr(mksem("c_t"))
    c_v = Ctr(mksem("c_v"))
    c_s = Ctr(mksem("c_s"))
    q_st = q_sxn

    R = Rec()

    def ld(qeng, ctr, pairs, waits=(), **kw):
        for wt in waits:
            R.w(qeng, wt)
        for out, in_ in pairs:
            ctr.nxt()
            R.i(qeng, "dma_start", out=out, in_=in_, inc=(ctr.sem, 16), **kw)
        return ctr.tok()

    CTRS = {"ve": c_v, "sc": c_s, "te": c_t}

    def one(eng, ctr, method, *args, waits=(), **kw):
        if eng in ("ve", "sc") and ctr.v > 0:
            R.w(eng, ctr.tok())
        for wt in waits:
            R.w(eng, wt)
        ctr.nxt()
        R.i(eng, method, *args, inc=(ctr.sem, 1), **kw)
        return ctr.tok()

    def quiet(eng, method, *args, waits=(), **kw):
        if eng in ("ve", "sc"):
            return one(eng, CTRS[eng], method, *args, waits=waits, **kw)
        for wt in waits:
            R.w(eng, wt)
        R.i(eng, method, *args, **kw)

    def seal(eng, ctr):
        if eng in ("ve", "sc"):
            return ctr.tok()
        m, a, k, inc = R.q[eng][-1]
        assert inc is None and m != "wait_ge", (m, inc)
        ctr.nxt()
        R.q[eng][-1] = (m, a, k, (ctr.sem, 1))
        return ctr.tok()

    def fmaj(ap_, t0, tt):
        return ap_.rearrange("(g p) t -> p g t", p=128)[:, :, t0:t0 + tt]

    from concourse.hw_specs import get_activation_tables
    _act_sets = list(get_activation_tables(nc.m.arch).values())

    def load_act_set(funcs):
        for idx, fs in enumerate(_act_sets):
            if all(f in fs for f in funcs):
                if c_s.v > 0:
                    R.w("sc", c_s.tok())
                c_s.nxt()
                R.i("sc", "__act_table__", idx, inc=(c_s.sem, 1))
                return c_s.tok()
        raise ValueError(f"no act table set contains {funcs}")

    bank_r = {}

    def mm_group(b, mm_list, waits=()):
        w = list(waits)
        if b in bank_r:
            w.append(bank_r.pop(b))
        first = True
        for (out, lhsT, rhs, st, sp, tp) in mm_list:
            kw = {"start": st, "stop": sp}
            if tp is not None:
                kw["tile_position"] = tp
            quiet("te", "matmul", out, lhsT, rhs, waits=(w if first else ()), **kw)
            first = False
        return seal("te", c_t)

    def read_bank(b, tok):
        bank_r[b] = tok

    t_pro = ld("sy", q_in, [
        (c_tmx["k"], tmk), (c_tmx["v"], tmv), (c_tmx["r"], tmr),
        (c_dexp, dexp),
        (c_idbf, ident_bf_c.ap()), (c_idf, ident_f_c.ap()),
        (c_onesr[0:1, :], ones_row_f_c.ap()),
        (c_onescbf, ones_col_bf_c.ap()),
        (c_repl[0:NH, :], repl_bf_c.ap()),
        (c_tri, tri_c.ap()),
        (c_iota2, iota2d_c.ap()),
        (c_br[0:1, :], br_rtr),
        (c_sco[:, 36:44], iota8_c.ap()),
        (c_A[0:NH, :], alog),
        (c_dtb[0:NH, :], dtb),
        (c_tmpg, tdp),
    ])
    load_act_set([AF.Exp, AF.Ln, AF.Square, AF.Identity])
    one("sc", c_s, "activation", c_ew, c_tmpg, AF.Exp, waits=[t_pro])
    one("sc", c_s, "activation", c_ew, c_ew, AF.Exp, scale=-1.0)
    t_sA = one("sc", c_s, "activation", c_A[0:NH, :], c_A[0:NH, :], AF.Exp)
    t_tf = ld("sy", q_in, [(c_tmpg, tfp)], waits=[t_sA])
    t_pro_sc = one("sc", c_s, "activation", c_eu, c_tmpg, AF.Exp, waits=[t_tf])
    one("ve", c_v, "tensor_scalar_mul", c_A[0:NH, :], c_A[0:NH, :], -1.0,
        waits=[t_sA])
    one("ve", c_v, "memset", c_onescf, 1.0, waits=[t_pro])
    t_pro_ve = one("ve", c_v, "memset", c_fc[:, 0:1], EPS_LN)

    if phases == "pro":
        R.w("sy", q_in.tok())
        R.w("sc", (c_s.sem, c_s.v))
        R.w("ve", (c_v.sem, c_v.v))
        return _finish(nc, R, es)

    A = Alloc(persist_end)
    a_x = [A.f32(G, TtA), A.f32(G, TtA)]
    a_xnb = A.f32(G, TtA + 1)
    a_al = A.f32(G, TtA + 1)
    a_be = A.f32(G, TtA + 1)
    off_dl = (A.cur + 63) // 64 * 64
    a_dl = A.f32(G, TtA)
    a_tmp = A.f32(G, TtA)
    a_ek = A.f32(G, TtA)
    a_ekv = A.f32(G, TtA)
    a_er = A.f32(G, TtA)
    a_xsq = A.view(BF16, off_dl, G, TtA)
    a_xnbf = A.bf(G, TtA)
    a_xbf = a_xnbf
    a_xm = [{q: A.bf(G, TtA) for q in ("k", "v", "r")} for _ in range(2)]
    a_rwkv = A.bf(G, TtA)
    a_xa = A.f32(G, TtA)
    a_rows = A.f32(4 * TtA)
    a_wk = A.bf(G, D); a_wv = A.bf(G, D); a_wrr = A.bf(G, D); a_wo = A.bf(G, D)
    S = Alloc(nbytes - 4 * D - 192)
    a_stage = S.f32(D)
    assert A.cur <= nbytes - 4 * D - 192, (A.cur, nbytes - 4 * D - 192)

    t_wA = ld("gp", q_w, [
        (c_wr, wr_rtr.rearrange("(g p) e -> p g e", p=128)),
        (a_wk, wkT.rearrange("(g p) o -> p g o", p=128)),
        (a_wv, wvT.rearrange("(g p) o -> p g o", p=128)),
        (a_wrr, wrT.rearrange("(g p) o -> p g o", p=128)),
        (a_wo, woT.rearrange("(g p) o -> p g o", p=128)),
    ])

    one("ve", c_v, "memset", a_xnb[:, :, 0:1], 0.0)
    one("ve", c_v, "memset", a_al[:, :, 0:1], 0.0)
    one("ve", c_v, "memset", a_be[:, :, 0:1], 0.0)
    one("ve", c_v, "memset", c_prev, 0.0)

    tokA_cons = {}
    tokA_st = {}
    tokA_xst = {}
    tokA_stat = {}
    tokA_kvr = {}
    tokA_mixv = {}
    tokA_rwkv = {}
    evq = {}
    s1tok = {}
    t_gather_done = None
    mixes = (("k", a_wk), ("v", a_wv), ("r", a_wrr))

    def emit_A1(i):
        par = i % 2
        t0 = i * TtA
        xin = a_x[par]

        lw = [t_pro]
        if i - 2 in tokA_cons:
            lw.append(tokA_cons[i - 2])
        t_ld = ld("sy", qx[par], [(xin, fmaj(xT, t0, TtA))], waits=lw)

        cw = [t_ld, t_pro_sc]
        if i - 1 in tokA_stat:
            cw.append(tokA_stat[i - 1])
        if i - 1 in tokA_mixv:
            cw.append(tokA_mixv[i - 1])
        if i - 1 in tokA_xst:
            cw.append(tokA_xst[i - 1])
        one("sc", c_s, "activation", a_xbf, xin, AF.Identity, waits=cw)
        t_c2 = one("sc", c_s, "activation", a_xsq, xin, AF.Square)

        t_stat = mm_group(6,
                          [(PB[6][0:1, 0:TtA], c_onescbf, a_xbf[:, g, :],
                            g == 0, g == G - 1, None) for g in range(G)] +
                          [(PB[6][0:1, TtA:2 * TtA], c_onescbf, a_xsq[:, g, :],
                            g == 0, g == G - 1, None) for g in range(G)],
                          waits=[t_c2])
        tokA_stat[i] = t_stat

        t_rows = one("ve", c_v, "tensor_scalar", out=a_rows[0:1, 0:2 * TtA],
                     in0=PB[6][0:1, 0:2 * TtA], scalar1=1.0 / D, scalar2=None,
                     op0=ALU.mult, waits=[t_stat, t_pro_ve])
        read_bank(6, t_rows)
        t_m2 = one("sc", c_s, "activation", a_rows[0:1, 2 * TtA:3 * TtA],
                   a_rows[0:1, 0:TtA], AF.Square, waits=[t_rows])
        t_var = one("ve", c_v, "tensor_tensor", out=a_rows[0:1, 3 * TtA:4 * TtA],
                    in0=a_rows[0:1, TtA:2 * TtA], in1=a_rows[0:1, 2 * TtA:3 * TtA],
                    op=ALU.subtract, waits=[t_m2])
        one("sc", c_s, "activation", a_rows[0:1, 3 * TtA:4 * TtA],
            a_rows[0:1, 3 * TtA:4 * TtA], AF.Ln, bias=c_fc[0:1, 0:1], waits=[t_var])
        t_rstd = one("sc", c_s, "activation", a_rows[0:1, 3 * TtA:4 * TtA],
                     a_rows[0:1, 3 * TtA:4 * TtA], AF.Exp, scale=-0.5)

        t_bc = mm_group(5, [
            (PB[5][:, 0:TtA], c_onesr[0:1, :], a_rows[0:1, 0:TtA], True, True, None),
            (PB[5][:, TtA:2 * TtA], c_onesr[0:1, :], a_rows[0:1, 3 * TtA:4 * TtA],
             True, True, None),
        ], waits=[t_rstd])

        if i > 0:
            quiet("ve", "tensor_copy", a_xnb[:, :, 0:1], a_xnb[:, :, TtA:TtA + 1])
        one("ve", c_v, "tensor_tensor", out=a_xnb[:, :, 1:TtA + 1], in0=xin,
            in1=PB[5][:, None, 0:TtA].broadcast_to([128, G, TtA]),
            op=ALU.subtract, waits=[t_bc])
        t_xn = one("ve", c_v, "tensor_tensor", out=a_xnb[:, :, 1:TtA + 1],
                   in0=a_xnb[:, :, 1:TtA + 1],
                   in1=PB[5][:, None, TtA:2 * TtA].broadcast_to([128, G, TtA]),
                   op=ALU.mult)
        read_bank(5, t_xn)
        xnbw = [t_xn]
        if i - 1 in tokA_xst:
            xnbw.append(tokA_xst[i - 1])
        t_xnbf = one("sc", c_s, "activation", a_xnbf, a_xnb[:, :, 1:TtA + 1],
                     AF.Identity, waits=xnbw)
        tokA_xst[i] = ld("sy", q_sxn, [(fmaj(xnbf_d, t0, TtA), a_xnbf)],
                         waits=[t_xnbf])

        t_dl = one("ve", c_v, "tensor_tensor", out=a_dl, in0=a_xnb[:, :, 1:TtA + 1],
                   in1=a_xnb[:, :, 0:TtA], op=ALU.subtract)
        tok_mix = {}
        for qi, (q, _) in enumerate(mixes):
            quiet("ve", "tensor_tensor", out=a_tmp, in0=a_dl,
                  in1=c_tmx[q][:, :, None].broadcast_to([128, G, TtA]),
                  op=ALU.mult)
            vw = []
            if qi == 0 and i - 1 in tokA_kvr:
                vw.append(tokA_kvr[i - 1])
            tok_mix[q] = one("ve", c_v, "tensor_tensor", out=a_xm[par][q],
                             in0=a_tmp, in1=a_xnb[:, :, 0:TtA], op=ALU.add,
                             waits=vw)
        tokA_mixv[i] = tok_mix["r"]
        s1tok[i] = (xin, tok_mix, t_xnbf)

    def emit_A2(i):
        par = i % 2
        t0 = i * TtA
        xin, tok_mix, _ = s1tok.pop(i)
        xm = a_xm[par]

        first_evac = True
        for qi, (q, wsb) in enumerate(mixes):
            for j in range(G // 2):
                bb = 2 * (j % 2)
                for h in range(2):
                    m = 2 * j + h
                    mw = [t_wA, tok_mix[q]] if (j == 0 and h == 0) else []
                    t_mm = mm_group(bb + h,
                                    [(PB[bb + h][:, 0:TtA],
                                      wsb[:, g, m * 128:(m + 1) * 128],
                                      xm[q][:, g, :], g == 0, g == G - 1, None)
                                     for g in range(G)], waits=mw)
                pview = PBPR2[bb // 2][:, :, 0:TtA]
                ew_ = [t_mm]
                if first_evac and i - 1 in tokA_rwkv:
                    ew_.append(tokA_rwkv[i - 1])
                first_evac = False
                if q == "k":
                    tk = one("sc", c_s, "activation", a_ek[:, 2 * j:2 * j + 2, :],
                             pview, AF.Exp, waits=ew_)
                elif q == "v":
                    tk = one("ve", c_v, "tensor_tensor",
                             out=a_ekv[:, 2 * j:2 * j + 2, :],
                             in0=a_ek[:, 2 * j:2 * j + 2, :], in1=pview,
                             op=ALU.mult, waits=ew_ + [evq[("k", j)]])
                else:
                    tk = one("sc", c_s, "activation", a_er[:, 2 * j:2 * j + 2, :],
                             pview, AF.Exp, scale=-1.0, waits=ew_)
                evq[(q, j)] = tk
                read_bank(bb, tk)
                read_bank(bb + 1, tk)
        tokA_kvr[i] = (c_t.sem, c_t.v)
        t_ek_all = (c_s.sem, c_s.v)

        if i > 0:
            quiet("ve", "tensor_copy", a_al[:, :, 0:1], a_al[:, :, TtA:TtA + 1])
            quiet("ve", "tensor_copy", a_be[:, :, 0:1], a_be[:, :, TtA:TtA + 1])
        for g in range(G):
            quiet("ve", "tensor_tensor_scan", a_al[:, g, 1:TtA + 1],
                  c_ew[:, g:g + 1].broadcast_to([128, TtA]), a_ekv[:, g, :],
                  a_al[:, g, 0:1], ALU.mult, ALU.add,
                  waits=([t_ek_all] if g == 0 else ()))
        for g in range(G):
            quiet("ve", "tensor_tensor_scan", a_be[:, g, 1:TtA + 1],
                  c_ew[:, g:g + 1].broadcast_to([128, TtA]), a_ek[:, g, :],
                  a_be[:, g, 0:1], ALU.mult, ALU.add)
        t_scan = seal("ve", c_v)

        one("ve", c_v, "tensor_tensor", out=a_ekv, in0=a_ekv,
            in1=c_eu[:, :, None].broadcast_to([128, G, TtA]), op=ALU.mult)
        one("ve", c_v, "tensor_tensor", out=a_ekv, in0=a_ekv,
            in1=a_al[:, :, 0:TtA], op=ALU.add)
        one("ve", c_v, "tensor_tensor", out=a_ek, in0=a_ek,
            in1=c_eu[:, :, None].broadcast_to([128, G, TtA]), op=ALU.mult)
        one("ve", c_v, "tensor_tensor", out=a_ek, in0=a_ek,
            in1=a_be[:, :, 0:TtA], op=ALU.add)
        one("ve", c_v, "tensor_scalar", out=a_er, in0=a_er, scalar1=1.0,
            scalar2=None, op0=ALU.add)
        one("ve", c_v, "tensor_tensor", out=a_ek, in0=a_ek, in1=a_er, op=ALU.mult)
        one("ve", c_v, "reciprocal", a_ek, a_ek)
        t_rwkv = one("ve", c_v, "tensor_tensor", out=a_rwkv, in0=a_ekv, in1=a_ek,
                     op=ALU.mult)
        tokA_rwkv[i] = t_rwkv

        for j in range(G // 2):
            bb = 2 * (j % 2)
            for h in range(2):
                m = 2 * j + h
                mw = [t_rwkv] if (j == 0 and h == 0) else []
                t_mm = mm_group(bb + h,
                                [(PB[bb + h][:, 0:TtA],
                                  a_wo[:, g, m * 128:(m + 1) * 128],
                                  a_rwkv[:, g, :], g == 0, g == G - 1, None)
                                 for g in range(G)], waits=mw)
            xw2 = [t_mm]
            if j == 0 and i - 1 in tokA_st:
                xw2.append(tokA_st[i - 1])
            tk = one("ve", c_v, "tensor_tensor", out=a_xa[:, 2 * j:2 * j + 2, :],
                     in0=xin[:, 2 * j:2 * j + 2, :], in1=PBPR2[bb // 2][:, :, 0:TtA],
                     op=ALU.add, waits=xw2)
            read_bank(bb, tk)
            read_bank(bb + 1, tk)
        tokA_cons[i] = (c_v.sem, c_v.v)
        tokA_st[i] = ld("sy", q_sxa, [(fmaj(xa_d, t0, TtA), a_xa)],
                        waits=[tokA_cons[i]])

    def emit_router():
        nonlocal t_gather_done
        _, _, t_xnbf = s1tok[0]
        if True:
            t_rt = mm_group(7, [(PB[7][0:1, 0:NE], a_xnbf[:, g, 0:1], c_wr[:, g, :],
                                 g == 0, g == G - 1, None) for g in range(G)],
                            waits=[t_xnbf, t_wA])
            tokA_stat[0] = t_rt
            t_rt = one("ve", c_v, "tensor_tensor", out=c_sco[0:1, 0:NE],
                       in0=PB[7][0:1, 0:NE], in1=c_br[0:1, :], op=ALU.add,
                       waits=[t_rt])
            read_bank(7, t_rt)
            t_sb = mm_group(6, [(PB[6][:, 0:8], c_onesr[0:1, :],
                                 c_sco[0:1, 0:8], True, True, None)],
                            waits=[t_rt])
            one("ve", c_v, "tensor_copy", c_sco[:, 0:8], PB[6][:, 0:8],
                waits=[t_sb])
            read_bank(6, (c_v.sem, c_v.v))
            def sel_level(a_sl, b_sl, out_sl, w):
                quiet("ve", "tensor_tensor", out=c_sco[:, 44:44 + w], in0=a_sl,
                      in1=b_sl, op=ALU.subtract)
                quiet("ve", "tensor_scalar", out=c_sco[:, 48:48 + w],
                      in0=c_sco[:, 44:44 + w], scalar1=0.0, scalar2=None,
                      op0=ALU.is_ge)
                quiet("ve", "tensor_tensor", out=c_sco[:, 52:52 + w], in0=a_sl,
                      in1=c_sco[:, 48:48 + w], op=ALU.mult)
                quiet("ve", "tensor_scalar", out=c_sco[:, 48:48 + w],
                      in0=c_sco[:, 48:48 + w], scalar1=-1.0, scalar2=1.0,
                      op0=ALU.mult, op1=ALU.add)
                quiet("ve", "tensor_tensor", out=c_sco[:, 56:56 + w], in0=b_sl,
                      in1=c_sco[:, 48:48 + w], op=ALU.mult)
                quiet("ve", "tensor_tensor", out=out_sl,
                      in0=c_sco[:, 52:52 + w], in1=c_sco[:, 56:56 + w],
                      op=ALU.add)

            def exact_max8(src0, dst_col):
                sel_level(c_sco[:, src0:src0 + 4], c_sco[:, src0 + 4:src0 + 8],
                          c_sco[:, 60:64], 4)
                sel_level(c_sco[:, 60:62], c_sco[:, 62:64], c_sco[:, 60:62], 2)
                sel_level(c_sco[:, 60:61], c_sco[:, 61:62],
                          c_sco[:, dst_col:dst_col + 1], 1)

            def tree_sum8(src0, dst_col):
                quiet("ve", "tensor_tensor", out=c_sco[:, 60:64],
                      in0=c_sco[:, src0:src0 + 4],
                      in1=c_sco[:, src0 + 4:src0 + 8], op=ALU.add)
                quiet("ve", "tensor_tensor", out=c_sco[:, 44:46],
                      in0=c_sco[:, 60:62], in1=c_sco[:, 62:64], op=ALU.add)
                quiet("ve", "tensor_tensor",
                      out=c_sco[:, dst_col:dst_col + 1],
                      in0=c_sco[:, 44:45], in1=c_sco[:, 45:46], op=ALU.add)

            exact_max8(0, 32)
            one("ve", c_v, "tensor_scalar", out=c_sco[:, 8:16],
                in0=c_sco[:, 0:8], scalar1=c_sco[:, 32:33], scalar2=None,
                op0=ALU.is_ge)
            one("ve", c_v, "tensor_tensor", out=c_sco[:, 16:24],
                in0=c_sco[:, 8:16], in1=c_sco[:, 36:44], op=ALU.mult)
            tree_sum8(16, 34)
            one("ve", c_v, "scalar_tensor_tensor", out=c_sco[:, 16:24],
                in0=c_sco[:, 8:16], scalar=-1e30, in1=c_sco[:, 0:8],
                op0=ALU.mult, op1=ALU.add)
            exact_max8(16, 33)
            one("ve", c_v, "tensor_scalar", out=c_sco[:, 24:32],
                in0=c_sco[:, 16:24], scalar1=c_sco[:, 33:34], scalar2=None,
                op0=ALU.is_ge)
            one("ve", c_v, "tensor_tensor", out=c_sco[:, 16:24],
                in0=c_sco[:, 24:32], in1=c_sco[:, 36:44], op=ALU.mult)
            tree_sum8(16, 35)
            t_e1 = one("ve", c_v, "tensor_copy", c_ebf[:, 0:1], c_sco[:, 34:35])
            if r_stop <= 1:
                t_gather_done = q_g.tok()
                return
            t_mc = mm_group(7, [
                (PB[7][0:NE, 0:1], c_sco[0:1, 8:16], c_onescf[0:1, 0:1],
                 True, True, None),
                (PB[7][0:NE, 1:2], c_sco[0:1, 24:32], c_onescf[0:1, 0:1],
                 True, True, None)], waits=[t_e1])
            t_msl = one("ve", c_v, "tensor_copy", c_msel[0:NE, 0:2],
                        PB[7][0:NE, 0:2], waits=[t_mc])
            read_bank(7, t_msl)
            one("ve", c_v, "tensor_copy", c_ebf[:, 0:1], c_sco[:, 34:35],
                waits=[t_e1])
            one("ve", c_v, "tensor_copy", c_ebf[:, 1:2], c_sco[:, 35:36])
            t_rb = one("ve", c_v, "tensor_copy", c_eb32[:, 0:2], c_ebf[:, 0:2])
            t_gi = t_rb
            for s in range(2):
                t_gi = one("ve", c_v, "tensor_scalar", out=c_gidx[:, s, :],
                           in0=c_eb32[:, s:s + 1].broadcast_to([128, G]),
                           scalar1=1024, scalar2=None, op0=ALU.mult,
                           waits=([t_rb] if s == 0 else ()))
                t_gi = one("ve", c_v, "tensor_tensor", out=c_gidx[:, s, :],
                           in0=c_gidx[:, s, :], in1=c_iota2, op=ALU.add)
            if r_stop <= 2:
                t_gather_done = q_g.tok()
                return
            t_bsel_last = None
            for bsrc, dstb in ((b1, c_b1g), (b2, c_b2g)):
                blw = [t_rt] if t_bsel_last is None else [t_bsel_last]
                t_bld = ld("sy", q_in, [(a_stage[0:NE, :], bsrc)], waits=blw)
                for s in range(2):
                    t_bm2 = mm_group(7, [
                        (PB[7][:, g:g + 1], a_stage[0:NE, g * 128:(g + 1) * 128],
                         c_msel[0:NE, s:s + 1], True, True, None)
                        for g in range(G)], waits=[t_msl, t_bld])
                    t_bv = one("ve", c_v, "tensor_copy", dstb[:, s, :],
                               PB[7][:, 0:G], waits=[t_bm2])
                    read_bank(7, t_bv)
                    t_bsel_last = t_bm2
            if r_stop <= 3:
                ld("gp", q_g, [(outT.rearrange("a b -> a b")[0:128, 0:8], c_gidx[:, 0, :])],
                   waits=[t_gi])
                ld("gp", q_g, [(outT[0:128, 8:16], c_gidx[:, 1, :])])
                ld("gp", q_g, [(outT[0:128, 16:18], c_ebf[:, 0:2])])
                ld("gp", q_g, [(outT[0:8, 20:22], c_msel[0:NE, 0:2])], waits=[t_msl])
                t_gather_done = q_g.tok()
                return
            for s in range(2):
                for (wsrc, wdst) in ((w1, w1g_d), (w2, w2g_d)):
                    for cch in range(G):
                        sb = a_stage
                        R.w("gp", t_gi)
                        R.w("gp", t_bsel_last)
                        R.w("gp", q_g.tok())
                        q_g.nxt()
                        R.i("gp", "indirect_dma_start", out=sb, out_offset=None,
                            in_=wsrc.rearrange("e d f -> (e d) f"),
                            in_offset=IndirectOffsetOnAxis(c_gidx[:, s, cch:cch + 1], 0),
                            inc=(q_g.sem, 16))
                        R.w("gp", q_g.tok())
                        q_g.nxt()
                        R.i("gp", "dma_start",
                            out=wdst[s, cch * 128:(cch + 1) * 128, :], in_=sb,
                            inc=(q_g.sem, 16))
            t_gather_done = q_g.tok()

    emit_A1(0)
    if phases != "An":
        emit_router()
    for i in range(nA):
        if i + 1 < nA:
            emit_A1(i + 1)
        emit_A2(i)

    tA = [(c_v.sem, c_v.v), (c_s.sem, c_s.v), (c_t.sem, c_t.v), q_sxn.tok(),
          q_sxa.tok(), t_gather_done]
    if phases in ("A", "An"):
        if t_gather_done is None:
            tA[5] = tA[3]
        for wt in tA:
            R.w("gp", wt)
        import os as _os
        if _os.environ.get('DUMPXN'):
            ld("gp", q_g, [(outT, xnbf_d)])
            for jj in range(4):
                ld("gp", q_g, [(outT[1020 + jj:1021 + jj, 0:TtA],
                                a_rows[0:1, jj * TtA:(jj + 1) * TtA])])
        else:
            ld("gp", q_g, [(outT, xa_d)])
        R.w("gp", q_g.tok())
        return _finish(nc, R, es)

    B = Alloc(persist_end)
    b_xn = [B.bf(G, TtB), B.bf(G, TtB)]
    b_xa = [B.f32(G, TtB), B.f32(G, TtB)]
    b_h = B.bf(G, TtB)
    b_acc = B.f32(G, TtB)
    b_x1 = [B.f32(G, TtB), B.f32(G, TtB)]
    b_w1g = B.bf(2, G, D)
    b_w2g = B.bf(2, G, D)
    assert B.cur <= nbytes

    t_wB = ld("gp", q_w, [
        (b_w1g[:, 0], w1g_d[0].rearrange("(g p) o -> p g o", p=128)),
        (b_w1g[:, 1], w1g_d[1].rearrange("(g p) o -> p g o", p=128)),
        (b_w2g[:, 0], w2g_d[0].rearrange("(g p) o -> p g o", p=128)),
        (b_w2g[:, 1], w2g_d[1].rearrange("(g p) o -> p g o", p=128)),
    ], waits=tA)
    one("ve", c_v, "tensor_tensor", out=c_bb, in0=c_b2g[:, 0, :],
        in1=c_b2g[:, 1, :], op=ALU.add, waits=tA[:1])
    t_bb = one("ve", c_v, "tensor_scalar", out=c_bb, in0=c_bb, scalar1=0.5,
               scalar2=None, op0=ALU.mult)
    load_act_set([AF.Gelu, AF.Identity])

    tokB_cons = {}
    tokB_st = {}
    t_o_done = None
    for i in range(nB):
        par = i % 2
        t0 = i * TtB
        lw = [t_wB]
        if i - 2 in tokB_cons:
            lw.append(tokB_cons[i - 2])
        t_ld = ld("sy", qx[par], [
            (b_xn[par], fmaj(xnbf_d, t0, TtB)),
            (b_xa[par], fmaj(xa_d, t0, TtB)),
        ], waits=lw)

        if b_stop <= 1:
            tokB_cons[i] = (c_v.sem, c_v.v) if c_v.v else (c_t.sem, c_t.v)
            continue
        for s in range(2):
            hw = [t_ld, t_bb]
            if s == 1:
                hw.append(t_o_done)
            for m in range(G):
                b = m % 2
                t_mm = mm_group(b, [(PB[b][:, 0:TtB],
                                     b_w1g[:, s, g, m * 128:(m + 1) * 128],
                                     b_xn[par][:, g, :], g == 0, g == G - 1, None)
                                    for g in range(G)], waits=(hw if m == 0 else ()))
                tk = one("sc", c_s, "activation", b_h[:, m, :], PB[b][:, 0:TtB],
                         AF.Gelu, bias=c_b1g[:, s, m:m + 1], waits=[t_mm])
                read_bank(b, tk)
            t_h_all = (c_s.sem, c_s.v)
            if b_stop <= 2:
                continue
            for m in range(G):
                b = 2 + m % 2
                t_mm = mm_group(b, [(PB[b][:, 0:TtB],
                                     b_w2g[:, s, g, m * 128:(m + 1) * 128],
                                     b_h[:, g, :], g == 0, g == G - 1, None)
                                    for g in range(G)],
                                waits=([t_h_all] if m == 0 else ()))
                if s == 0:
                    tk = one("ve", c_v, "tensor_copy", b_acc[:, m, :], PB[b][:, 0:TtB],
                             waits=[t_mm])
                else:
                    tv = one("ve", c_v, "tensor_tensor", out=b_acc[:, m, :],
                             in0=b_acc[:, m, :], in1=PB[b][:, 0:TtB], op=ALU.add,
                             waits=[t_mm])
                    ts_ = one("sc", c_s, "activation", b_acc[:, m, :], b_acc[:, m, :],
                              AF.Identity, scale=0.5, bias=c_bb[:, m:m + 1],
                              waits=[tv])
                    tk = one("ve", c_v, "tensor_tensor", out=b_x1[par][:, m, :],
                             in0=b_acc[:, m, :], in1=b_xa[par][:, m, :], op=ALU.add,
                             waits=[ts_])
                read_bank(b, tk)
            if s == 0:
                t_o_done = (c_v.sem, c_v.v)
        tokB_cons[i] = (c_v.sem, c_v.v)
        if b_stop <= 3:
            continue
        stw = [tokB_cons[i]]
        if i - 2 in tokB_st:
            stw.append(tokB_st[i - 2])
        tokB_st[i] = ld("gp", q_sx1, [(fmaj(x1_d, t0, TtB), b_x1[par])], waits=stw)

    tB = [(c_v.sem, c_v.v), (c_s.sem, c_s.v), (c_t.sem, c_t.v), q_sx1.tok()]
    if phases == "AB":
        for wt in tB:
            R.w("gp", wt)
        ld("gp", q_g, [(outT, x1_d)])
        R.w("gp", q_g.tok())
        return _finish(nc, R, es)

    C = Alloc(persist_end)
    cx1 = [C.f32(G, TtC), C.f32(G, TtC)]
    c_xn1 = C.bf(G, TtC)
    c_win = C.bf(G, 1184)
    c_wout = C.bf(G, D)
    c_wq = C.bf(G, D)
    c_out = C.f32(G, TtC)
    c_xssm = C.f32(G, TtC)
    c_xsq2 = C.bf(G, TtC)
    c_Bc = C.bf(TtC)
    c_Cc = C.bf(TtC)
    c_dt = C.f32(TtC)
    c_acum = C.f32(TtC)
    c_dec = C.f32(TtC)
    c_expA = C.f32(TtC)
    c_dtbf = C.bf(TtC)
    c_decbf = C.bf(TtC)
    c_expAbf = C.bf(TtC)
    c_acumT = C.f32(KC, NH)
    c_xdtg = C.bf(TtC)
    c_xddb = C.bf(2, CS)
    c_xdtT = C.bf(KC, D)
    c_xddT = C.bf(KC, D)
    c_BT = C.bf(KC, DS)
    c_prevT = C.bf(KC, D)
    c_seg = C.f32(4, KC * CS)
    c_G4 = C.bf(4, KC * CS)
    c_cbt = C.bf(KC, CS)
    c_ys = C.f32(KC * CS)
    c_ybf = C.bf(G, TtC)
    c_rows2 = c_seg.rearrange("p a b -> p (a b)")
    assert C.cur <= nbytes, f"phase C arena {C.cur} > {nbytes}"

    t_wC = ld("gp", q_w, [
        (c_win, winT.rearrange("(g p) o -> p g o", p=128)),
        (c_wout, woutT.rearrange("(g p) o -> p g o", p=128)),
    ], waits=tB)

    load_act_set([AF.Exp, AF.Ln, AF.Square, AF.Identity])

    tstg0, tstg1 = c_xssm, c_out
    wf_r = wfT.rearrange("(h g p) o -> p (h g) o", p=128, h=2)
    t_tl = ld("sy", q_in, [(tstg0, wf_r[:, 0:4, :]), (tstg1, wf_r[:, 4:8, :])],
              waits=tB)
    flat0 = tstg0.rearrange("p a b -> p (a b)")
    flat1 = tstg1.rearrange("p a b -> p (a b)")
    one("ve", c_v, "reduce_sum", c_ys[:, 0:1], flat0, AX.X,
        apply_absolute_value=True, waits=[t_tl] + tB[:1])
    one("ve", c_v, "reduce_sum", c_ys[:, 1:2], flat1, AX.X,
        apply_absolute_value=True)
    t_r2 = one("ve", c_v, "tensor_tensor", out=c_ys[:, 0:1], in0=c_ys[:, 0:1],
               in1=c_ys[:, 1:2], op=ALU.add)
    t_sc0 = mm_group(6, [(PB[6][0:1, 0:1], c_ys[:, 0:1], c_onescf, True, True,
                          None)], waits=[t_r2, tB[2]])
    t_sc1 = one("ve", c_v, "tensor_scalar", out=c_ys[0:1, 2:3], in0=PB[6][0:1, 0:1],
                scalar1=1.0 / (D * D), scalar2=EPS_TERN, op0=ALU.mult, op1=ALU.max,
                waits=[t_sc0])
    read_bank(6, t_sc1)
    t_sc2 = mm_group(6, [(PB[6][:, 4:5], c_onesr[0:1, :], c_ys[0:1, 2:3],
                          True, True, None)], waits=[t_sc1])
    t_sc3 = one("ve", c_v, "tensor_copy", c_scl[:, 0:1], PB[6][:, 4:5], waits=[t_sc2])
    read_bank(6, t_sc3)
    t_sc4 = one("ve", c_v, "reciprocal", c_scl[:, 1:2], c_scl[:, 0:1])
    segflat = c_seg.rearrange("p a b -> p (a b)")
    ybfflat = c_ybf.rearrange("p a b -> p (a b)").bitcast(F32)
    xn1flat = c_xn1.rearrange("p a b -> p (a b)").bitcast(F32)
    tmpA = ybfflat
    tmpB = xn1flat
    wqf = c_wq.rearrange("p a b -> p (a b)")
    t_q = t_sc4
    for h, flat in ((0, flat0), (1, flat1)):
        one("ve", c_v, "tensor_scalar", out=tmpA[:, 0:2048], in0=flat[:, 0:2048],
            scalar1=c_scl[:, 1:2], scalar2=0.5, op0=ALU.mult, op1=ALU.is_ge,
            waits=[t_q])
        one("ve", c_v, "tensor_scalar", out=tmpB[:, 0:2048], in0=flat[:, 0:2048],
            scalar1=c_scl[:, 1:2], scalar2=-0.5, op0=ALU.mult, op1=ALU.is_le)
        one("ve", c_v, "tensor_tensor", out=tmpA[:, 0:2048], in0=tmpA[:, 0:2048],
            in1=tmpB[:, 0:2048], op=ALU.subtract)
        one("ve", c_v, "tensor_scalar", out=tmpB[:, 0:2048], in0=flat[:, 2048:4096],
            scalar1=c_scl[:, 1:2], scalar2=0.5, op0=ALU.mult, op1=ALU.is_ge)
        one("ve", c_v, "tensor_scalar", out=segflat[:, 0:2048], in0=flat[:, 2048:4096],
            scalar1=c_scl[:, 1:2], scalar2=-0.5, op0=ALU.mult, op1=ALU.is_le)
        one("ve", c_v, "tensor_tensor", out=tmpB[:, 0:2048], in0=tmpB[:, 0:2048],
            in1=segflat[:, 0:2048], op=ALU.subtract)
        one("ve", c_v, "tensor_scalar", out=wqf[:, h * 4096:h * 4096 + 2048],
            in0=tmpA[:, 0:2048], scalar1=c_scl[:, 0:1], scalar2=None, op0=ALU.mult)
        t_q = one("ve", c_v, "tensor_scalar",
                  out=wqf[:, h * 4096 + 2048:(h + 1) * 4096], in0=tmpB[:, 0:2048],
                  scalar1=c_scl[:, 0:1], scalar2=None, op0=ALU.mult)
    t_tern = t_q

    tokC_cons = {}
    tokC_st = {}
    for i in range(nC):
        par = i % 2
        t0 = i * TtC
        lw = [t_tern]
        if i - 2 in tokC_cons:
            lw.append(tokC_cons[i - 2])
        t_ld = ld("sy", qx[par], [(cx1[par], fmaj(x1_d, t0, TtC))], waits=lw)

        scw = [t_ld]
        if i - 1 in tokC_cons:
            scw.append(tokC_cons[i - 1])
        t_cbf = one("sc", c_s, "activation", c_ybf, cx1[par], AF.Identity,
                    waits=scw)
        t_sq = one("sc", c_s, "activation", c_xsq2, cx1[par], AF.Square)
        t_s1 = mm_group(6, [(PB[6][0:1, 0:TtC], c_onescbf, c_ybf[:, g, :],
                             g == 0, g == G - 1, None) for g in range(G)],
                        waits=[t_cbf, t_wC, t_tern])
        t_s2 = mm_group(7, [(PB[7][0:1, 0:TtC], c_onescbf, c_xsq2[:, g, :],
                             g == 0, g == G - 1, None) for g in range(G)],
                        waits=[t_sq])
        t_rw = one("ve", c_v, "tensor_scalar", out=c_rows2[0:1, 0:TtC],
                   in0=PB[6][0:1, 0:TtC], scalar1=1.0 / D, scalar2=None,
                   op0=ALU.mult, waits=[t_s1, t_tern])
        t_rw2 = one("ve", c_v, "tensor_scalar", out=c_rows2[0:1, TtC:2 * TtC],
                    in0=PB[7][0:1, 0:TtC], scalar1=1.0 / D, scalar2=None,
                    op0=ALU.mult, waits=[t_s2])
        t_m2 = one("sc", c_s, "activation", c_rows2[0:1, 2 * TtC:3 * TtC],
                   c_rows2[0:1, 0:TtC], AF.Square, waits=[t_rw])
        t_vr = one("ve", c_v, "tensor_tensor", out=c_rows2[0:1, 3 * TtC:4 * TtC],
                   in0=c_rows2[0:1, TtC:2 * TtC], in1=c_rows2[0:1, 2 * TtC:3 * TtC],
                   op=ALU.subtract, waits=[t_m2, t_rw2])
        one("sc", c_s, "activation", c_rows2[0:1, 3 * TtC:4 * TtC],
            c_rows2[0:1, 3 * TtC:4 * TtC], AF.Ln, bias=c_fc[0:1, 0:1], waits=[t_vr])
        t_rs = one("sc", c_s, "activation", c_rows2[0:1, 3 * TtC:4 * TtC],
                   c_rows2[0:1, 3 * TtC:4 * TtC], AF.Exp, scale=-0.5)
        t_bm = mm_group(6, [(PB[6][:, 0:TtC], c_onesr[0:1, :], c_rows2[0:1, 0:TtC],
                             True, True, None)], waits=[t_rw, t_rs])
        t_br2 = mm_group(7, [(PB[7][:, 0:TtC], c_onesr[0:1, :],
                              c_rows2[0:1, 3 * TtC:4 * TtC], True, True, None)],
                         waits=[t_rw2, t_rs])
        t_xnc = one("ve", c_v, "tensor_tensor", out=c_xssm, in0=cx1[par],
                    in1=PB[6][:, None, 0:TtC].broadcast_to([128, G, TtC]),
                    op=ALU.subtract, waits=[t_bm])
        read_bank(6, t_xnc)
        t_xn1 = one("ve", c_v, "tensor_tensor", out=c_xn1, in0=c_xssm,
                    in1=PB[7][:, None, 0:TtC].broadcast_to([128, G, TtC]),
                    op=ALU.mult, waits=[t_br2])
        read_bank(7, t_xn1)

        for jm in range(G // 2):
            bb = 2 * (jm % 2)
            for h in range(2):
                m = 2 * jm + h
                mw = [t_xn1] if m == 0 else []
                t_mm = mm_group(bb + h, [(PB[bb + h][:, 0:TtC],
                                          c_win[:, g, m * 128:(m + 1) * 128],
                                          c_xn1[:, g, :], g == 0, g == G - 1,
                                          None) for g in range(G)], waits=mw)
            tk = one("ve", c_v, "tensor_copy", c_xssm[:, 2 * jm:2 * jm + 2, :],
                     PBPR2[bb // 2][:, :, 0:TtC], waits=[t_mm, t_xn1])
            read_bank(bb, tk)
            read_bank(bb + 1, tk)
        mm_specs = [("Ba", c_win[:, :, 1024:1088], 64),
                    ("Cb", c_win[:, :, 1088:1152], 64),
                    ("dt", c_win[:, :, 1152:1184], 32)]
        for si, (mk, wsl, mwid) in enumerate(mm_specs):
            b = si % 2
            mw = []
            t_mm = mm_group(b, [(PB[b][0:mwid, 0:TtC], wsl[:, g, :], c_xn1[:, g, :],
                                 g == 0, g == G - 1, None) for g in range(G)],
                            waits=mw)
            if mk == "Ba":
                tk = one("sc", c_s, "activation", c_Bc[0:64, :], PB[b][0:64, 0:TtC],
                         AF.Identity, waits=[t_mm])
            elif mk == "Cb":
                tk = one("sc", c_s, "activation", c_Cc[0:64, :], PB[b][0:64, 0:TtC],
                         AF.Identity, waits=[t_mm])
            else:
                one("sc", c_s, "activation", c_dt[0:NH, :], PB[b][0:NH, 0:TtC],
                    AF.Exp, bias=c_dtb[0:NH, 0:1], waits=[t_mm])
                tk = one("sc", c_s, "activation", c_dt[0:NH, :], c_dt[0:NH, :],
                         AF.Ln, bias=c_onescf[0:NH, 0:1])
            read_bank(b, tk)
        t_dt_done = tk
        t_xs_all = (c_v.sem, c_v.v)

        t_dtbf = one("sc", c_s, "activation", c_dtbf[0:NH, :], c_dt[0:NH, :],
                     AF.Identity, waits=[t_dt_done])
        t_a = one("ve", c_v, "tensor_scalar", out=c_dt[0:NH, :], in0=c_dt[0:NH, :],
                  scalar1=c_A[0:NH, 0:1], scalar2=None, op0=ALU.mult,
                  waits=[t_dtbf])
        for cc in range(KC):
            quiet("ve", "tensor_tensor_scan", c_acum[0:NH, cc * CS:(cc + 1) * CS],
                  c_onescf[0:NH, 0:1].broadcast_to([NH, CS]),
                  c_dt[0:NH, cc * CS:(cc + 1) * CS], 0.0, ALU.mult, ALU.add)
        t_acum = seal("ve", c_v)
        for cc in range(KC):
            quiet("sc", "activation", c_dec[0:NH, cc * CS:(cc + 1) * CS],
                  c_acum[0:NH, cc * CS:(cc + 1) * CS], AF.Exp, scale=-1.0,
                  bias=c_acum[0:NH, (cc + 1) * CS - 1:(cc + 1) * CS],
                  waits=([t_acum] if cc == 0 else ()))
        one("sc", c_s, "activation", c_expA[0:NH, :], c_acum[0:NH, :], AF.Exp)
        one("sc", c_s, "activation", c_expAbf[0:NH, :], c_expA[0:NH, :], AF.Identity)
        t_decs = one("sc", c_s, "activation", c_decbf[0:NH, :], c_dec[0:NH, :],
                     AF.Identity)

        ld("sy", q_bc, [(acum_d[:, 0:TtC], c_acum[0:NH, 0:TtC])],
           waits=[t_acum, q_bc.tok()])

        atw = [t_acum]
        if 3 in bank_r:
            atw.append(bank_r.pop(3))
        for wt in atw:
            R.w("te", wt)
        for cc in range(KC):
            quiet("te", "transpose", PB[3][:, cc * NH:(cc + 1) * NH],
                  c_acum[0:NH, cc * CS:(cc + 1) * CS], c_idf[0:NH, 0:NH])
        t_at = seal("te", c_t)
        t_acT = one("ve", c_v, "tensor_copy", c_acumT.rearrange("p a b -> p (a b)"),
                    PB[3][:, 0:KC * NH], waits=[t_at])
        read_bank(3, t_acT)

        tbank = (PBbf[2], PBbf[3])
        tp_r = {0: None, 1: None}
        nt_box = [0]

        def do_transpose_bf(src_ap, dst_ap, p_src, f_src, ident, extra=()):
            half = nt_box[0] % 2
            nt_box[0] += 1
            w = list(extra)
            if tp_r[half] is not None:
                w.append(tp_r[half])
            if (2 + half) in bank_r:
                w.append(bank_r.pop(2 + half))
            for wt in w:
                R.w("te", wt)
            c_t.nxt()
            R.i("te", "transpose", tbank[half][0:f_src, 0:p_src],
                src_ap, ident, inc=(c_t.sem, 1))
            t_tp = c_t.tok()
            tk = one("sc", c_s, "activation", dst_ap,
                     tbank[half][0:f_src, 0:p_src],
                     AF.Identity, waits=[t_tp])
            tp_r[half] = tk
            read_bank(2 + half, tk)
            return tk

        t_prev_g = None
        for g in range(G):
            bdt = 4 + (g % 2)
            t_rp = mm_group(bdt, [(PB[bdt][:, 0:TtC],
                                   c_repl[0:NH, g * 128:(g + 1) * 128],
                                   c_dtbf[0:NH, :], True, True, None)],
                            waits=([t_dtbf, t_xs_all, t_acT] if g == 0 else ()))
            xw = [t_rp]
            if t_prev_g is not None:
                xw.append(t_prev_g)
            t_xg = one("ve", c_v, "tensor_tensor", out=c_xdtg, in0=c_xssm[:, g, :],
                       in1=PB[bdt][:, 0:TtC], op=ALU.mult, waits=xw)
            read_bank(bdt, t_xg)
            bdc = 6 + (g % 2)
            t_rp2 = mm_group(bdc, [(PB[bdc][:, 0:TtC],
                                    c_repl[0:NH, g * 128:(g + 1) * 128],
                                    c_decbf[0:NH, :], True, True, None)],
                             waits=([t_decs] if g == 0 else ()))
            w2 = [t_xg]
            if 2 in bank_r:
                w2.append(bank_r.pop(2))
            for wt in w2:
                R.w("te", wt)
            for cc in range(KC):
                quiet("te", "transpose", PBbf[2][0:128, cc * CS:(cc + 1) * CS],
                      c_xdtg[:, cc * CS:(cc + 1) * CS], c_idbf)
            t_tp1 = seal("te", c_t)
            t_ev1 = one("sc", c_s, "activation",
                        c_xdtT[:, :, g * 128:(g + 1) * 128],
                        PBbf[2][:, 0:KC * CS].rearrange("p (a b) -> p a b", a=KC),
                        AF.Identity, waits=[t_tp1])
            read_bank(2, t_ev1)
            t_xd = one("ve", c_v, "tensor_tensor", out=c_xdtg, in0=c_xdtg,
                       in1=PB[bdc][:, 0:TtC], op=ALU.mult, waits=[t_tp1, t_rp2])
            read_bank(bdc, t_xd)
            w3 = [t_xd]
            if 3 in bank_r:
                w3.append(bank_r.pop(3))
            for wt in w3:
                R.w("te", wt)
            for cc in range(KC):
                quiet("te", "transpose", PBbf[3][0:128, cc * CS:(cc + 1) * CS],
                      c_xdtg[:, cc * CS:(cc + 1) * CS], c_idbf)
            t_prev_g = seal("te", c_t)
            t_ev2 = one("sc", c_s, "activation",
                        c_xddT[:, :, g * 128:(g + 1) * 128],
                        PBbf[3][:, 0:KC * CS].rearrange("p (a b) -> p a b", a=KC),
                        AF.Identity, waits=[t_prev_g])
            read_bank(3, t_ev2)
        t_trans_last = None
        for cc in range(KC):
            t_trans_last = do_transpose_bf(
                c_Bc[0:64, cc * CS:(cc + 1) * CS],
                c_BT[:, cc, :], 64, 128, c_idbf[0:64, 0:64], extra=[t_dt_done])
        t_trans_done = t_trans_last

        t_prev_up = None
        t_cb_last = None
        for cc in range(KC):
            t_cb = mm_group(2, [(PB[2][:, 0:CS], c_Bc[0:64, cc * CS:(cc + 1) * CS],
                                 c_Cc[0:64, cc * CS:(cc + 1) * CS], True, True,
                                 None)], waits=([t_trans_done] if cc == 0 else []))
            tk = one("ve", c_v, "tensor_tensor", out=c_cbt[:, cc, :],
                     in0=PB[2][:, 0:CS], in1=c_tri, op=ALU.mult, waits=[t_cb])
            read_bank(2, tk)
            t_cb_last = tk
            pw = [tk] + ([t_prev_up] if t_prev_up else [])
            for wt in pw:
                R.w("te", wt)
            for g in range(G):
                c_t.nxt()
                R.i("te", "transpose", PB[2][0:64, g * 128:(g + 1) * 128]
                    if g < 4 else PB[3][0:64, (g - 4) * 128:(g - 4 + 1) * 128],
                    c_prev[:, g, :], c_idf, inc=(c_t.sem, 1))
            t_pt = c_t.tok()
            tk1 = one("sc", c_s, "activation", c_prevT[0:64, cc, 0:512],
                      PB[2][0:64, 0:512], AF.Identity, waits=[t_pt, t_acT])
            tk2 = one("sc", c_s, "activation", c_prevT[0:64, cc, 512:1024],
                      PB[3][0:64, 0:512], AF.Identity)
            read_bank(2, tk2)
            read_bank(3, tk2)
            t_st_mm = mm_group(6, [(PB[6][:, g * 64:(g + 1) * 64],
                                    c_xddT[:, cc, g * 128:(g + 1) * 128],
                                    c_BT[:, cc, :], True, True, None)
                                   for g in range(G)], waits=[tk2])
            t_cd = mm_group(7, [(PB[7][:, g:g + 1],
                                 c_repl[0:NH, g * 128:(g + 1) * 128],
                                 c_expAbf[0:NH, (cc + 1) * CS - 1:(cc + 1) * CS],
                                 True, True, None) for g in range(G)], waits=[])
            tk = one("ve", c_v, "tensor_tensor", out=c_prev,
                     in0=c_prev,
                     in1=PB[7][:, 0:G][:, :, None].broadcast_to([128, G, DS]),
                     op=ALU.mult, waits=[t_cd, tk2])
            read_bank(7, tk)
            t_prev_up = one("ve", c_v, "tensor_tensor", out=c_prev, in0=c_prev,
                            in1=PB[6][:, 0:G * DS].rearrange("p (a b) -> p a b", a=G),
                            op=ALU.add, waits=[t_st_mm])
            read_bank(6, t_prev_up)

        t_y_done = None
        grp_read = None
        t_exp_prev = None
        for grp in range(G):
            bw = [q_bc.tok()]
            if t_exp_prev is not None:
                bw.append(t_exp_prev)
            else:
                bw.append(t_br2)
            t_bdma = ld("sy", q_bc, [
                (segflat[:, 0:4 * TtC],
                 acum_d[4 * grp:4 * grp + 4, 0:TtC].rearrange(
                     "a b -> (a b)")[None, :].broadcast_to([128, 4 * TtC]))],
                waits=bw)
            t_sub = one("ve", c_v, "tensor_tensor",
                        out=c_seg.rearrange("p a (b c) -> p a b c", b=KC),
                        in0=c_seg.rearrange("p a (b c) -> p a b c", b=KC),
                        in1=c_acumT[:, :, 4 * grp:4 * grp + 4].rearrange(
                            "p a b -> p b a")[:, :, :, None].broadcast_to(
                            [128, 4, KC, CS]),
                        op=ALU.subtract, waits=[t_bdma, t_acT, t_cb_last])
            t_min = one("ve", c_v, "tensor_scalar", out=segflat[:, 0:4 * KC * CS],
                        in0=segflat[:, 0:4 * KC * CS], scalar1=0.0, scalar2=None,
                        op0=ALU.min, waits=[t_sub])
            t_exp = one("sc", c_s, "activation",
                        c_G4.rearrange("p a b -> p (a b)"),
                        segflat[:, 0:4 * KC * CS], AF.Exp, waits=[t_min])
            t_exp_prev = t_exp
            t_g4 = one("ve", c_v, "tensor_tensor",
                       out=c_G4, in0=c_G4,
                       in1=c_cbt[:, None, :, :].broadcast_to([128, 4, KC, CS]),
                       op=ALU.mult, waits=[t_exp])
            ydl = []
            for cc in range(KC):
                for hh in range(4):
                    h = 4 * grp + hh
                    ydl.append((PB[2][hh * 32:(hh + 1) * 32, cc * CS:(cc + 1) * CS],
                                c_xdtT[:, cc, h * 32:(h + 1) * 32],
                                c_G4[:, hh, cc * CS:(cc + 1) * CS], True, True,
                                (0, 32 * hh)))
            t_yd = mm_group(2, ydl, waits=[t_g4, t_prev_up])
            yol = []
            for cc in range(KC):
                yol.append((PB[3][:, cc * CS:(cc + 1) * CS],
                            c_prevT[0:64, cc, grp * 128:(grp + 1) * 128],
                            c_Cc[0:64, cc * CS:(cc + 1) * CS], True, True, None))
            t_yo = mm_group(3, yol, waits=[])
            t_ea = mm_group(4, [(PB[4][:, cc * CS:(cc + 1) * CS],
                                 c_repl[0:NH, grp * 128:(grp + 1) * 128],
                                 c_expAbf[0:NH, cc * CS:(cc + 1) * CS], True, True,
                                 None) for cc in range(KC)], waits=[t_min])
            tk0 = one("ve", c_v, "tensor_copy", c_ys, PB[3][:, 0:TtC],
                      waits=[t_yo])
            read_bank(3, tk0)
            tk = one("ve", c_v, "tensor_tensor", out=c_ys, in0=c_ys,
                     in1=PB[4][:, 0:TtC], op=ALU.mult, waits=[t_ea])
            read_bank(4, tk)
            tk2 = one("ve", c_v, "tensor_tensor", out=c_ys, in0=c_ys,
                      in1=PB[2][:, 0:TtC], op=ALU.add, waits=[t_yd])
            read_bank(2, tk2)
            t_y_done = one("ve", c_v, "scalar_tensor_tensor", out=c_ybf[:, grp, :],
                           in0=c_xssm[:, grp, :], scalar=c_dexp[:, grp:grp + 1],
                           in1=c_ys, op0=ALU.mult, op1=ALU.add)

        for jm in range(G // 2):
            bb = 2 * (jm % 2)
            for h in range(2):
                m = 2 * jm + h
                mw = [t_y_done] if m == 0 else []
                if m == 0 and i - 1 in tokC_st:
                    mw.append(tokC_st[i - 1])
                mml = [(PB[bb + h][:, 0:TtC], c_wout[:, g, m * 128:(m + 1) * 128],
                        c_ybf[:, g, :], g == 0, False, None) for g in range(G)]
                mml += [(PB[bb + h][:, 0:TtC], c_wq[:, g, m * 128:(m + 1) * 128],
                         c_xn1[:, g, :], False, g == G - 1, None)
                        for g in range(G)]
                t_mm = mm_group(bb + h, mml, waits=mw)
            tk = one("ve", c_v, "tensor_tensor",
                     out=c_out[:, 2 * jm:2 * jm + 2, :],
                     in0=cx1[par][:, 2 * jm:2 * jm + 2, :],
                     in1=PBPR2[bb // 2][:, :, 0:TtC],
                     op=ALU.add, waits=[t_mm])
            read_bank(bb, tk)
            read_bank(bb + 1, tk)
        tokC_cons[i] = (c_v.sem, c_v.v)
        stw = [tokC_cons[i]]
        if i - 1 in tokC_st:
            stw.append(tokC_st[i - 1])
        tokC_st[i] = ld("gp", q_so, [(fmaj(outT, t0, TtC), c_out)], waits=stw)

    R.w("gp", q_so.tok())
    R.w("gp", q_g.tok())
    return _finish(nc, R, es)


def _finish(nc, R, es):
    with nc.Block() as block:
        @block.sync
        def _(sy):
            R.replay("sy", sy)

        @block.tensor
        def _(te):
            R.replay("te", te)

        @block.vector
        def _(ve):
            R.replay("ve", ve)

        @block.scalar
        def _(sc):
            R.replay("sc", sc)

        @block.gpsimd
        def _(gp):
            R.replay("gp", gp)

    es.close()
    return nc


_NCORES = 8
_T = 8192


def _pack_pg(v):
    return np.ascontiguousarray(np.asarray(v, np.float32).reshape(G, 128).T)


def kernel(
    L0_attn_x, L0_attn_time_decay, L0_attn_time_first, L0_attn_time_mix_k,
    L0_attn_time_mix_v, L0_attn_time_mix_r, L0_attn_Wk, L0_attn_Wv, L0_attn_Wr,
    L0_attn_Wo, L0_ffn_Wr, L0_ffn_br, L0_ffn_W1, L0_ffn_b1, L0_ffn_W2, L0_ffn_b2,
    L1_attn_W_in, L1_attn_dt_bias, L1_attn_A_log, L1_attn_D, L1_attn_W_out,
    L1_ffn_weight,
):
    x = np.asarray(L0_attn_x, np.float32)
    Bsz, T, _ = x.shape
    shared = {
        "wkT": np.ascontiguousarray(np.asarray(L0_attn_Wk, np.float32).T),
        "wvT": np.ascontiguousarray(np.asarray(L0_attn_Wv, np.float32).T),
        "wrT": np.ascontiguousarray(np.asarray(L0_attn_Wr, np.float32).T),
        "woT": np.ascontiguousarray(np.asarray(L0_attn_Wo, np.float32).T),
        "tmk": _pack_pg(L0_attn_time_mix_k),
        "tmv": _pack_pg(L0_attn_time_mix_v),
        "tmr": _pack_pg(L0_attn_time_mix_r),
        "tdp": _pack_pg(L0_attn_time_decay),
        "tfp": _pack_pg(L0_attn_time_first),
        "wr_rtr": np.ascontiguousarray(np.asarray(L0_ffn_Wr, np.float32)),
        "br_rtr": np.asarray(L0_ffn_br, np.float32).reshape(1, NE),
        "w1": np.ascontiguousarray(np.asarray(L0_ffn_W1, np.float32)),
        "b1": np.ascontiguousarray(np.asarray(L0_ffn_b1, np.float32)),
        "w2": np.ascontiguousarray(np.asarray(L0_ffn_W2, np.float32)),
        "b2": np.ascontiguousarray(np.asarray(L0_ffn_b2, np.float32)),
        "winT": np.ascontiguousarray(np.asarray(L1_attn_W_in, np.float32).T),
        "dtb": np.asarray(L1_attn_dt_bias, np.float32).reshape(NH, 1),
        "alog": np.asarray(L1_attn_A_log, np.float32).reshape(NH, 1),
        "dexp": _pack_pg(np.repeat(np.asarray(L1_attn_D, np.float32), HD)),
        "woutT": np.ascontiguousarray(np.asarray(L1_attn_W_out, np.float32).T),
        "wfT": np.ascontiguousarray(np.asarray(L1_ffn_weight, np.float32).T),
    }
    in_maps = []
    for b in range(Bsz):
        m = dict(shared)
        m["xT"] = np.ascontiguousarray(x[b].T)
        in_maps.append(m)
    nc = build_program(T)
    res = run_bass_kernel_spmd(nc, in_maps, core_ids=list(range(_NCORES)))
    out = np.stack([np.asarray(r["outT"]).T for r in res.results], axis=0)
    return np.ascontiguousarray(out.astype(np.float32))
